# version 07
# baseline (speedup 1.0000x reference)
import sys
import numpy as np
from contextlib import ExitStack

try:
    import concourse.bass as bass
except ImportError:
    sys.path.insert(0, "/opt/trn_rl_repo")
    import concourse.bass as bass
import ml_dtypes
import concourse.mybir as mybir
from concourse.bass_utils import run_bass_kernel_spmd
from concourse.ap import AP as APcls


def bcast_last(ap3, count):
    return APcls(tensor=ap3.tensor, offset=ap3.offset,
                 ap=[list(q) for q in ap3.ap[:-1]] + [[0, count]])

F32 = mybir.dt.float32
BF16 = mybir.dt.bfloat16
I8 = mybir.dt.int8
I32 = mybir.dt.int32
F16 = mybir.dt.float16
AF = mybir.ActivationFunctionType
ALU = mybir.AluOpType
AX = mybir.AxisListType

T = 4096
D = 2048
DT = 16
NEG_INF = -1e30
ATT_SCALE = 1.0 / float(np.sqrt(2048.0))
I_FFN0 = 8192
I_FFN1 = 5632
NI1 = I_FFN1 // 128
NO1 = 2 * I_FFN1 // 128
PI = float(np.pi)



class E:

    NRING = 8

    def __init__(self, plan, key, semname, step):
        self.plan = plan
        self.key = key
        self.semname = semname
        self.step = step
        self.is_dma = step == 16
        if self.is_dma:
            self.ringnames = [f"{semname}{i}" for i in range(self.NRING)]
            self.ringvals = [0] * self.NRING
            self.ndma = 0
            self.slotmap = {}
        self.n = 0
        self.marks = {}
        self.pending = []

    def add(self, fn, waits=(), inc=True, slot=None):
        todo = []
        for ev in list(self.pending) + list(waits):
            if ev is None:
                continue
            s, v = ev
            if v <= 0 or self.marks.get(s, 0) >= v:
                continue
            self.marks[s] = v
            todo.append((s, v))
        self.pending = []
        plan = self.plan
        myev = None
        if inc and self.is_dma:
            if slot is not None:
                if slot not in self.slotmap:
                    self.slotmap[slot] = [f"{self.semname}s{len(self.slotmap)}", 0]
                ent = self.slotmap[slot]
                semname = ent[0]
                ent[1] += 16
                myev = (semname, ent[1])
            else:
                rslot = self.ndma % self.NRING
                self.ndma += 1
                semname = self.ringnames[rslot]
                prev = self.ringvals[rslot]
                if prev > 0 and self.marks.get(semname, 0) < prev:
                    todo.append((semname, prev))
                    self.marks[semname] = prev
                self.ringvals[rslot] = prev + 16
                myev = (semname, prev + 16)

            def cl(q):
                for s, v in todo:
                    q.wait_ge(plan.sems[s], v)
                fn(q).then_inc(plan.sems[semname], 16)
        elif inc:
            self.n += self.step
            myev = (self.semname, self.n)
            semname = self.semname
            step = self.step

            def cl(q):
                for s, v in todo:
                    q.wait_ge(plan.sems[s], v)
                fn(q).then_inc(plan.sems[semname], step)
        else:
            def cl(q):
                for s, v in todo:
                    q.wait_ge(plan.sems[s], v)
                fn(q)
        plan.streams[self.key].append(cl)
        return myev

    def now(self):
        if self.is_dma:
            return None
        return (self.semname, self.n)

    def all_events(self):
        if self.is_dma:
            evs = [(self.ringnames[i], self.ringvals[i]) for i in range(self.NRING)
                   if self.ringvals[i] > 0]
            evs += [(ent[0], ent[1]) for ent in self.slotmap.values() if ent[1] > 0]
            return evs
        return [(self.semname, self.n)] if self.n > 0 else []


class Plan:
    def __init__(self):
        self.streams = {k: [] for k in ("sync", "gpsimd", "tensor", "vector", "scalar")}
        self.sems = {}
        self.T = E(self, "tensor", "sT", 1)
        self.V = E(self, "vector", "sV", 1)
        self.S = E(self, "scalar", "sS", 1)
        self.Q = E(self, "sync", "sQ", 16)
        self.G = E(self, "gpsimd", "sG", 16)
        self.all = [self.T, self.V, self.S, self.Q, self.G]

    def barrier(self):
        evs = []
        for e in self.all:
            evs.extend(e.all_events())
        for e in self.all:
            e.pending.extend(evs)

    def emit(self, nc, es):
        for name in ("sT", "sV", "sS"):
            self.sems[name] = es.enter_context(nc.semaphore(name))
        for base in ("sQ", "sG"):
            for i in range(E.NRING):
                name = f"{base}{i}"
                self.sems[name] = es.enter_context(nc.semaphore(name))
        for e in (self.Q, self.G):
            for ent in e.slotmap.values():
                self.sems[ent[0]] = es.enter_context(nc.semaphore(ent[0]))
        block = es.enter_context(nc.Block())

        @block.sync
        def _(q):
            for cl in self.streams["sync"]:
                cl(q)

        @block.gpsimd
        def _(q):
            for cl in self.streams["gpsimd"]:
                cl(q)

        @block.tensor
        def _(q):
            for cl in self.streams["tensor"]:
                cl(q)

        @block.vector
        def _(q):
            for cl in self.streams["vector"]:
                cl(q)

        @block.scalar
        def _(q):
            for cl in self.streams["scalar"]:
                cl(q)


def ln_tile(p, xt, out_bf, tmp, waits=()):
    V, S = p.V, p.S
    ev = None
    for k in range(4):
        ev = V.add(lambda v, k=k: v.bn_stats(tmp["bn"][:, 6 * k:6 * k + 6],
                                             xt[:, 512 * k:512 * (k + 1)]),
                   waits=(waits if k == 0 else ()), inc=(k == 3))
    ev = V.add(lambda v: v.bn_aggr(tmp["mv"][:], tmp["bn"][:]), waits=[ev])
    e1 = S.add(lambda s: s.activation(tmp["r1"][:], tmp["mv"][:, 1:2], AF.Ln, bias=tmp["eps"]),
               waits=[ev])
    e2 = S.add(lambda s: s.activation(tmp["r2"][:], tmp["r1"][:], AF.Exp, scale=-0.5),
               waits=[e1])
    e3 = V.add(lambda v: v.scalar_tensor_tensor(tmp["nmr"][:], tmp["mv"][:, 0:1], -1.0,
                                                tmp["r2"][:], ALU.mult, ALU.mult),
               waits=[e2])
    e4 = V.add(lambda v: v.tensor_scalar(out_bf[:], xt[:], tmp["r2"][:, 0:1],
                                         tmp["nmr"][:, 0:1], ALU.mult, ALU.add),
               waits=[e3])
    return e4


def mk_ln_tmp(nc, es, tag, eps_ap=None):
    return {
        "bn": es.enter_context(nc.sbuf_tensor(f"{tag}_bn", [128, 24], F32)),
        "mv": es.enter_context(nc.sbuf_tensor(f"{tag}_mv", [128, 2], F32)),
        "r1": es.enter_context(nc.sbuf_tensor(f"{tag}_r1", [128, 1], F32)),
        "r2": es.enter_context(nc.sbuf_tensor(f"{tag}_r2", [128, 1], F32)),
        "nmr": es.enter_context(nc.sbuf_tensor(f"{tag}_nmr", [128, 1], F32)),
        "eps": eps_ap,
    }



def build_nc(upto="P4", start="P1", p3sub=None):
    nc = bass.Bass("TRN2", target_bir_lowering=False, debug=False)
    p = Plan()
    es = ExitStack()

    x_in = nc.dram_tensor("x_in", [T, D], F32, kind="ExternalInput").ap()
    wq = nc.dram_tensor("wq", [D, D], F32, kind="ExternalInput").ap()
    wk = nc.dram_tensor("wk", [D, D], F32, kind="ExternalInput").ap()
    wv = nc.dram_tensor("wv", [D, D], F32, kind="ExternalInput").ap()
    wg8 = nc.dram_tensor("wg8", [D, I_FFN0], I8, kind="ExternalInput").ap()
    wu8 = nc.dram_tensor("wu8", [D, I_FFN0], I8, kind="ExternalInput").ap()
    wd8 = nc.dram_tensor("wd8", [I_FFN0, D], I8, kind="ExternalInput").ap()
    sgate = nc.dram_tensor("sgate", [I_FFN0], F32, kind="ExternalInput").ap()
    sup = nc.dram_tensor("sup", [I_FFN0], F32, kind="ExternalInput").ap()
    sdown = nc.dram_tensor("sdown", [D], F32, kind="ExternalInput").ap()
    lre = nc.dram_tensor("lre", [1, 32], F32, kind="ExternalInput").ap()
    lim = nc.dram_tensor("lim", [1, 32], F32, kind="ExternalInput").ap()
    bre = nc.dram_tensor("bre", [32, D], F32, kind="ExternalInput").ap()
    bim = nc.dram_tensor("bim", [32, D], F32, kind="ExternalInput").ap()
    cre = nc.dram_tensor("cre", [D, 32], F32, kind="ExternalInput").ap()
    cim = nc.dram_tensor("cim", [D, 32], F32, kind="ExternalInput").ap()
    dvec = nc.dram_tensor("dvec", [1, D], F32, kind="ExternalInput").ap()
    logdelta = nc.dram_tensor("logdelta", [1, 64], F32, kind="ExternalInput").ap()
    gup_pk = nc.dram_tensor("gup_pk", [2 * I_FFN1, 256], I32, kind="ExternalInput").ap()
    gup_sc = nc.dram_tensor("gup_sc", [16, 2 * I_FFN1], F16, kind="ExternalInput").ap()
    dn_pk = nc.dram_tensor("dn_pk", [D, 704], I32, kind="ExternalInput").ap()
    dn_sc = nc.dram_tensor("dn_sc", [44, D], F16, kind="ExternalInput").ap()
    c_idbf = nc.dram_tensor("c_idbf", [128, 128], BF16, kind="ExternalInput").ap()
    c_idf = nc.dram_tensor("c_idf", [128, 128], F32, kind="ExternalInput").ap()
    c_mask = nc.dram_tensor("c_mask", [128, 128], F32, kind="ExternalInput").ap()
    c_u128 = nc.dram_tensor("c_u128", [128, 128], F32, kind="ExternalInput").ap()
    c_iota0 = nc.dram_tensor("c_iota0", [1, 128], F32, kind="ExternalInput").ap()
    c_iota1 = nc.dram_tensor("c_iota1", [1, 128], F32, kind="ExternalInput").ap()
    c_ones = nc.dram_tensor("c_ones", [1, 128], F32, kind="ExternalInput").ap()
    c_e127 = nc.dram_tensor("c_e127", [128, 1], F32, kind="ExternalInput").ap()
    c_eps = nc.dram_tensor("c_eps", [128, 4], F32, kind="ExternalInput").ap()

    out = nc.dram_tensor("out", [T, D], F32, kind="ExternalOutput").ap()

    qTs = nc.dram_tensor("qTs", [D, T], BF16, kind="Internal").ap()
    kTs = nc.dram_tensor("kTs", [D, T], BF16, kind="Internal").ap()
    vSs = nc.dram_tensor("vSs", [T, D], BF16, kind="Internal").ap()
    x1s = nc.dram_tensor("x1s", [T, D], F32, kind="Internal").ap()
    x2s = nc.dram_tensor("x2s", [T, D], F32, kind="ExternalInput" if start == "P3" else "Internal").ap()
    x3s = nc.dram_tensor("x3s", [T, D], F32, kind="Internal" if start != "P4" else "ExternalInput").ap()
    ulnSc = nc.dram_tensor("ulnSc", [T, D], BF16, kind="Internal").ap()
    xln2Ts = nc.dram_tensor("xln2Ts", [D, T], BF16, kind="Internal").ap()
    xln4Ts = nc.dram_tensor("xln4Ts", [D, T], BF16, kind="Internal" if start != "P4" else "ExternalInput").ap()
    wguTs = nc.dram_tensor("wguTs", [D, 2 * I_FFN1], BF16, kind="Internal").ap()
    wdnTs = nc.dram_tensor("wdnTs", [I_FFN1, D], BF16, kind="Internal").ap()

    Tn, V, S, Q, G = p.T, p.V, p.S, p.Q, p.G

    id_bf = es.enter_context(nc.sbuf_tensor("id_bf", [128, 128], BF16))
    id_f = es.enter_context(nc.sbuf_tensor("id_f", [128, 128], F32))
    msk = es.enter_context(nc.sbuf_tensor("msk", [128, 128], F32))
    u128 = es.enter_context(nc.sbuf_tensor("u128", [128, 128], F32))
    iota0 = es.enter_context(nc.sbuf_tensor("iota0", [1, 128], F32))
    iota1 = es.enter_context(nc.sbuf_tensor("iota1", [1, 128], F32))
    ones1 = es.enter_context(nc.sbuf_tensor("ones1", [1, 128], F32))
    e127 = es.enter_context(nc.sbuf_tensor("e127", [128, 1], F32))
    epsc = es.enter_context(nc.sbuf_tensor("epsc", [128, 4], F32))

    for dst, src in [(id_bf, c_idbf), (id_f, c_idf), (msk, c_mask), (u128, c_u128),
                     (iota0, c_iota0), (iota1, c_iota1), (ones1, c_ones), (e127, c_e127), (epsc, c_eps)]:
        Q.add(lambda q, d=dst, s=src: q.dma_start(out=d[:], in_=s[:]))
    p.barrier()

    env = dict(locals())

    if start == "P1":
        p1a_emit(nc, p, es, env)
        p1b_emit(nc, p, es, env)
        if upto == "P1":
            return finish_partial(nc, p, es, out, x1s)

        p2_emit(nc, p, es, env)
        if upto == "P2":
            return finish_partial(nc, p, es, out, x2s)

    if start != "P4":
        p3_emit(nc, p, es, env, p3sub)
        if p3sub is not None:
            return finish_partial(nc, p, es, out, x2s)
        if upto == "P3":
            return finish_partial(nc, p, es, out, x3s)
    with ExitStack() as dqes:
        dq = DequantPump(nc, p, dqes, env)
        dq.drain()
        p.barrier()

    p4_emit(nc, p, es, env)
    p.barrier()
    p.Q.add(lambda q: q.nop(), inc=False)
    p.emit(nc, es)
    es.close()
    return nc


def finish_partial(nc, p, es, out, src):
    fin = p.Q.add(lambda q: q.dma_start(out=out[:], in_=src[:]))
    p.barrier()
    p.Q.add(lambda q: q.nop(), inc=False)
    p.emit(nc, es)
    es.close()
    return nc



def p1a_emit(nc, p, es, env):
    Tn, V, S, Q, G = p.T, p.V, p.S, p.Q, p.G
    x_in, wq, wk, wv = env["x_in"], env["wq"], env["wk"], env["wv"]
    qTs, kTs, vSs = env["qTs"], env["kTs"], env["vSs"]
    id_bf = env["id_bf"]
    CH = 1024
    NCH = T // CH
    with ExitStack() as ph:
        xbuf = ph.enter_context(nc.sbuf_tensor("p1_x", [128, 8 * D], F32))
        xlnT = [ph.enter_context(nc.sbuf_tensor(f"p1_xlnT{i}", [128, 16 * CH], BF16)) for i in range(2)]
        xln_bf = ph.enter_context(nc.sbuf_tensor("p1_xln", [128, D], BF16))
        wqk_sb = [ph.enter_context(nc.sbuf_tensor(f"p1_wqk{i}", [128, 16 * 128], BF16)) for i in range(3)]
        wvb_sb = [ph.enter_context(nc.sbuf_tensor(f"p1_wv{i}", [128, 16 * 512], BF16)) for i in range(2)]
        stg = [ph.enter_context(nc.sbuf_tensor(f"p1_stg{i}", [128, 512], BF16)) for i in range(4)]
        lt = mk_ln_tmp(nc, ph, "p1", env["epsc"][:, 0:1])
        ptr = [ph.enter_context(nc.psum_tensor(f"p1_ptr{i}", [128, 1024], BF16)) for i in range(2)]
        pqk = [ph.enter_context(nc.psum_tensor(f"p1_pqk{i}", [128, 512], F32)) for i in range(4)]

        xbuf_free = None
        xlnT_free = [None, None]
        w_free = [None, None, None]
        wv_free = [None, None]
        stg_free = [None] * 4
        ptr_free = [None, None]
        pqk_free = [None] * 4
        x_ld = [None] * NCH
        nstg = 0
        nptr = 0
        npqk = 0
        nw = 0

        x_ld[0] = Q.add(lambda q: q.dma_start(
            out=xbuf.ap().rearrange("p (a d) -> p a d", a=8),
            in_=x_in[0:CH, :].rearrange("(a p) d -> p a d", p=128)), slot="p1x")
        for c in range(NCH):
            sl = c % 2
            ev_tr_last = None
            ev_ln_last = None
            prev_tr = None
            for a in range(8):
                xt = xbuf[:, D * a:D * (a + 1)]
                V.pending.append(x_ld[c])
                if prev_tr is not None:
                    V.pending.append(prev_tr)
                ev_ln = ln_tile(p, xt, xln_bf, lt)
                ev_ln_last = ev_ln
                for half in range(2):
                    pb = nptr % 2
                    nptr += 1
                    evt = None
                    for k in range(8):
                        dt = 8 * half + k
                        evt = Tn.add(lambda t, pb=pb, k=k, dt=dt: t.transpose(
                            ptr[pb][:, 128 * k:128 * (k + 1)],
                            xln_bf[:, 128 * dt:128 * (dt + 1)], id_bf[:]),
                            waits=([ev_ln, ptr_free[pb]] if k == 0 else []),
                            inc=(k == 7))
                        prev_tr = evt
                    ev_tr_last = S.add(lambda s, pb=pb, a=a, half=half, sl=sl: s.copy(
                        xlnT[sl].ap().rearrange("p (t c) -> p t c", t=16)[:, 8 * half:8 * half + 8,
                                                                         128 * a:128 * (a + 1)],
                        ptr[pb].ap().rearrange("p (t c) -> p t c", t=8)),
                        waits=[evt, xlnT_free[sl]])
                    ptr_free[pb] = ev_tr_last
            xbuf_free = ev_ln_last
            if c + 1 < NCH:
                x_ld[c + 1] = Q.add(lambda q, c=c: q.dma_start(
                    out=xbuf.ap().rearrange("p (a d) -> p a d", a=8),
                    in_=x_in[CH * (c + 1):CH * (c + 2), :].rearrange("(a p) d -> p a d", p=128)),
                    waits=[xbuf_free], slot="p1x")
            ev_xlnT = ev_tr_last

            ev_mm = None
            for wmat, dstT in [(wq, qTs), (wk, kTs)]:
                for et in range(DT):
                    wsl = nw % 3
                    nw += 1
                    ev_w = G.add(lambda q, wmat=wmat, et=et, wsl=wsl: q.dma_start(
                        out=wqk_sb[wsl].ap().rearrange("p (a n) -> p a n", a=16),
                        in_=wmat[:, 128 * et:128 * (et + 1)].rearrange("(a p) n -> p a n", p=128)),
                        waits=[w_free[wsl]], slot=f"wqk{wsl}")
                    for th in range(2):
                        pb = npqk % 4
                        npqk += 1
                        for dt in range(DT):
                            ev_mm = Tn.add(lambda t, pb=pb, wsl=wsl, dt=dt, sl=sl, th=th: t.matmul(
                                pqk[pb][:], wqk_sb[wsl][:, 128 * dt:128 * (dt + 1)],
                                xlnT[sl][:, CH * dt + 512 * th: CH * dt + 512 * (th + 1)],
                                start=(dt == 0), stop=(dt == DT - 1)),
                                waits=([ev_w, ev_xlnT, pqk_free[pb]] if dt == 0 else []),
                                inc=(dt == DT - 1))
                        sg = nstg % 4
                        nstg += 1
                        ev_ev = V.add(lambda v, sg=sg, pb=pb: v.tensor_copy(stg[sg][:], pqk[pb][:]),
                                      waits=[ev_mm, stg_free[sg]])
                        pqk_free[pb] = ev_ev
                        ev_st = Q.add(lambda q, sg=sg, et=et, c=c, th=th, dstT=dstT: q.dma_start(
                            out=dstT[128 * et:128 * (et + 1), CH * c + 512 * th: CH * c + 512 * (th + 1)],
                            in_=stg[sg][:]), waits=[ev_ev], slot=f"stg{sg}")
                        stg_free[sg] = ev_st
                    w_free[wsl] = ev_mm
            for nch in range(4):
                wsl = nch % 2
                ev_w = G.add(lambda q, nch=nch, wsl=wsl: q.dma_start(
                    out=wvb_sb[wsl].ap().rearrange("p (a n) -> p a n", a=16),
                    in_=wv[:, 512 * nch:512 * (nch + 1)].rearrange("(a p) n -> p a n", p=128)),
                    waits=[wv_free[wsl]], slot=f"wvb{wsl}")
                for a in range(8):
                    pb = npqk % 4
                    npqk += 1
                    for dt in range(DT):
                        ev_mm = Tn.add(lambda t, pb=pb, wsl=wsl, dt=dt, sl=sl, a=a: t.matmul(
                            pqk[pb][:], xlnT[sl][:, CH * dt + 128 * a: CH * dt + 128 * (a + 1)],
                            wvb_sb[wsl][:, 512 * dt:512 * (dt + 1)],
                            start=(dt == 0), stop=(dt == DT - 1)),
                            waits=([ev_w, ev_xlnT, pqk_free[pb]] if dt == 0 else []),
                            inc=(dt == DT - 1))
                    sg = nstg % 4
                    nstg += 1
                    ev_ev = V.add(lambda v, sg=sg, pb=pb: v.tensor_copy(stg[sg][:], pqk[pb][:]),
                                  waits=[ev_mm, stg_free[sg]])
                    pqk_free[pb] = ev_ev
                    ev_st = Q.add(lambda q, sg=sg, nch=nch, c=c, a=a: q.dma_start(
                        out=vSs[CH * c + 128 * a: CH * c + 128 * (a + 1), 512 * nch:512 * (nch + 1)],
                        in_=stg[sg][:]), waits=[ev_ev], slot=f"stg{sg}")
                    stg_free[sg] = ev_st
                wv_free[wsl] = ev_mm
            xlnT_free[sl] = ev_mm
        p.barrier()



def p1b_emit(nc, p, es, env):
    Tn, V, S, Q, G = p.T, p.V, p.S, p.Q, p.G
    x_in, qTs, kTs, vSs = env["x_in"], env["qTs"], env["kTs"], env["vSs"]
    x1s, xln2Ts = env["x1s"], env["xln2Ts"]
    id_bf, msk = env["id_bf"], env["msk"]
    with ExitStack() as ph:
        qTg = ph.enter_context(nc.sbuf_tensor("p1b_qTg", [128, 16 * 512], BF16))
        kv = [ph.enter_context(nc.sbuf_tensor(f"p1b_kv{i}", [128, 16 * 512], BF16)) for i in range(2)]
        Ssl = [ph.enter_context(nc.sbuf_tensor(f"p1b_S{j}", [128, T], F32)) for j in range(4)]
        Psl = [ph.enter_context(nc.sbuf_tensor(f"p1b_P{j}", [128, T], BF16)) for j in range(4)]
        Osl = [ph.enter_context(nc.sbuf_tensor(f"p1b_O{j}", [128, D], F32)) for j in range(4)]
        nmax = ph.enter_context(nc.sbuf_tensor("p1b_nmax", [128, 4], F32))
        rsum = ph.enter_context(nc.sbuf_tensor("p1b_rsum", [128, 4], F32))
        rinv = ph.enter_context(nc.sbuf_tensor("p1b_rinv", [128, 4], F32))
        pTt = [ph.enter_context(nc.sbuf_tensor(f"p1b_pT{i}", [128, 512], BF16)) for i in range(2)]
        xres = ph.enter_context(nc.sbuf_tensor("p1b_xres", [128, D], F32))
        x1t = ph.enter_context(nc.sbuf_tensor("p1b_x1t", [128, D], F32))
        xln2b = ph.enter_context(nc.sbuf_tensor("p1b_xln2", [128, D], BF16))
        stg2 = ph.enter_context(nc.sbuf_tensor("p1b_stg2", [128, 16 * 128], BF16))
        lt2 = mk_ln_tmp(nc, ph, "p1b", env["epsc"][:, 0:1])
        psS = [ph.enter_context(nc.psum_tensor(f"p1b_psS{i}", [128, 512], F32)) for i in range(2)]
        psO = ph.enter_context(nc.psum_tensor("p1b_psO", [128, 2048], F32))
        psT = [ph.enter_context(nc.psum_tensor(f"p1b_psT{i}", [128, 512], BF16)) for i in range(2)]

        kv_free = [None, None]
        psS_free = [None, None]
        psT_free = [None, None]
        pTt_free = [None, None]
        psO_free = None
        qTg_free = None
        stg2_free = None
        xres_free = None
        x1t_free_dma = None
        x1t_free_ln = None
        pT_last = [None] * 4
        nS = 0
        nT2 = 0
        nV = 0
        ev_mm = None

        for g in range(8):
            ev_q = Q.add(lambda q, g=g: q.dma_start(
                out=qTg.ap().rearrange("p (a n) -> p a n", a=16),
                in_=qTs[:, 512 * g:512 * (g + 1)].rearrange("(a p) n -> p a n", p=128)),
                waits=[qTg_free], slot="qTg")
            ev_S_done = [None] * 4
            for kb in range(g + 1):
                ksl = kb % 2
                ev_k = Q.add(lambda q, kb=kb, ksl=ksl: q.dma_start(
                    out=kv[ksl].ap().rearrange("p (a n) -> p a n", a=16),
                    in_=kTs[:, 512 * kb:512 * (kb + 1)].rearrange("(a p) n -> p a n", p=128)),
                    waits=[kv_free[ksl]], slot=f"kv{ksl}")
                for j in range(4):
                    width = 512 if kb < g else 128 * (j + 1)
                    pb = nS % 2
                    nS += 1
                    for dt in range(DT):
                        ev_mm = Tn.add(lambda t, pb=pb, dt=dt, j=j, ksl=ksl, width=width: t.matmul(
                            psS[pb][:, 0:width], qTg[:, 512 * dt + 128 * j: 512 * dt + 128 * (j + 1)],
                            kv[ksl][:, 512 * dt:512 * dt + width],
                            start=(dt == 0), stop=(dt == DT - 1)),
                            waits=([ev_q, ev_k, psS_free[pb]] if dt == 0 else []),
                            inc=(dt == DT - 1))
                    ev_ev = S.add(lambda s, j=j, kb=kb, pb=pb, width=width: s.activation(
                        Ssl[j][:, 512 * kb:512 * kb + width], psS[pb][:, 0:width],
                        AF.Copy, scale=ATT_SCALE), waits=[ev_mm])
                    psS_free[pb] = ev_ev
                    if kb == g:
                        ev_ev = V.add(lambda v, j=j, kb=kb: v.tensor_tensor(
                            Ssl[j][:, 512 * kb + 128 * j: 512 * kb + 128 * (j + 1)],
                            Ssl[j][:, 512 * kb + 128 * j: 512 * kb + 128 * (j + 1)],
                            msk[:], ALU.add), waits=[ev_ev])
                    ev_S_done[j] = ev_ev
                kv_free[ksl] = ev_mm
            ev_P = [None] * 4
            ev_rinv = [None] * 4
            for j in range(4):
                w = 512 * g + 128 * (j + 1)
                ev1 = V.add(lambda v, j=j, w=w: v.reduce_max(
                    nmax[:, j:j + 1], Ssl[j][:, 0:w], axis=AX.X, negate=True),
                    waits=[ev_S_done[j]])
                ev2 = S.add(lambda s, j=j, w=w: s.activation(
                    Psl[j][:, 0:w], Ssl[j][:, 0:w], AF.Exp, bias=nmax[:, j:j + 1],
                    accum_out=rsum[:, j:j + 1]), waits=[ev1, pT_last[j]])
                ev_P[j] = ev2
                ev_rinv[j] = V.add(lambda v, j=j: v.reciprocal(
                    rinv[:, j:j + 1], rsum[:, j:j + 1]), waits=[ev2])
            ev_O = [None] * 4
            for kb in range(g + 1):
                vsl = nV % 2
                nV += 1
                ev_v = Q.add(lambda q, kb=kb, vsl=vsl: q.dma_start(
                    out=kv[vsl].ap().rearrange("p (a d) -> p a d", a=4),
                    in_=vSs[512 * kb:512 * (kb + 1), :].rearrange("(a p) d -> p a d", p=128)),
                    waits=[kv_free[vsl]], slot=f"kv{vsl}")
                for j in range(4):
                    nkt = 4 if kb < g else j + 1
                    pb = nT2 % 2
                    nT2 += 1
                    evt = None
                    for kt in range(nkt):
                        evt = Tn.add(lambda t, pb=pb, j=j, kb=kb, kt=kt: t.transpose(
                            psT[pb][:, 128 * kt:128 * (kt + 1)],
                            Psl[j][:, 512 * kb + 128 * kt: 512 * kb + 128 * (kt + 1)],
                            id_bf[:]), waits=([ev_P[j], psT_free[pb]] if kt == 0 else []),
                            inc=(kt == nkt - 1))
                    pT_last[j] = evt
                    ev_pc = S.add(lambda s, pb=pb, nkt=nkt: s.copy(
                        pTt[pb][:, 0:128 * nkt], psT[pb][:, 0:128 * nkt]),
                        waits=[evt, pTt_free[pb]])
                    psT_free[pb] = ev_pc
                    for kt in range(nkt):
                        for nchi in range(4):
                            w = []
                            if kt == 0 and nchi == 0:
                                w = [ev_pc, ev_v, psO_free]
                            ev_mm = Tn.add(lambda t, pb=pb, vsl=vsl, kt=kt, nchi=nchi, nkt=nkt: t.matmul(
                                psO[:, 512 * nchi:512 * (nchi + 1)], pTt[pb][:, 128 * kt:128 * (kt + 1)],
                                kv[vsl][:, D * kt + 512 * nchi: D * kt + 512 * (nchi + 1)],
                                start=(kt == 0), stop=(kt == nkt - 1)),
                                waits=w, inc=(kt == nkt - 1 and nchi == 3))
                    pTt_free[pb] = ev_mm
                    if kb == 0:
                        ev_O[j] = V.add(lambda v, j=j: v.tensor_copy(Osl[j][:], psO[:]),
                                        waits=[ev_mm])
                    else:
                        ev_O[j] = V.add(lambda v, j=j: v.tensor_add(Osl[j][:], Osl[j][:], psO[:]),
                                        waits=[ev_mm, ev_O[j]])
                    psO_free = ev_O[j]
                kv_free[vsl] = ev_mm
            prev_tr2 = None
            for j in range(4):
                qi = 4 * g + j
                ev_x = G.add(lambda q, qi=qi: q.dma_start(
                    out=xres[:], in_=x_in[128 * qi:128 * (qi + 1), :]), waits=[xres_free],
                    slot="xres")
                ev_add = V.add(lambda v, j=j: v.scalar_tensor_tensor(
                    x1t[:], Osl[j][:], rinv[:, j:j + 1], xres[:], ALU.mult, ALU.add),
                    waits=[ev_O[j], ev_rinv[j], ev_x, x1t_free_dma, x1t_free_ln])
                xres_free = ev_add
                ev_x1st = G.add(lambda q, qi=qi: q.dma_start(
                    out=x1s[128 * qi:128 * (qi + 1), :], in_=x1t[:]), waits=[ev_add], slot="x1st")
                x1t_free_dma = ev_x1st
                if prev_tr2 is not None:
                    V.pending.append(prev_tr2)
                ev_ln = ln_tile(p, x1t, xln2b, lt2, waits=[ev_add])
                x1t_free_ln = ev_ln
                ev_stg2 = None
                for grp in range(4):
                    pb = nT2 % 2
                    nT2 += 1
                    evt = None
                    for k in range(4):
                        dt = 4 * grp + k
                        evt = Tn.add(lambda t, pb=pb, k=k, dt=dt: t.transpose(
                            psT[pb][:, 128 * k:128 * (k + 1)],
                            xln2b[:, 128 * dt:128 * (dt + 1)], id_bf[:]),
                            waits=([ev_ln, psT_free[pb]] if k == 0 else []), inc=(k == 3))
                        prev_tr2 = evt
                    ev_stg2 = S.add(lambda s, pb=pb, grp=grp: s.copy(
                        stg2[:, 512 * grp:512 * (grp + 1)], psT[pb][:, 0:512]),
                        waits=[evt, stg2_free])
                    psT_free[pb] = ev_stg2
                ev_st2 = G.add(lambda q, qi=qi: q.dma_start(
                    out=xln2Ts[:, 128 * qi:128 * (qi + 1)].rearrange("(a p) n -> p a n", p=128),
                    in_=stg2.ap().rearrange("p (a n) -> p a n", a=16)), waits=[ev_stg2],
                    slot="stg2")
                stg2_free = ev_st2
            qTg_free = ev_mm
        p.barrier()



def p2_emit(nc, p, es, env):
    Tn, V, S, Q, G = p.T, p.V, p.S, p.Q, p.G
    wg8, wu8, wd8 = env["wg8"], env["wu8"], env["wd8"]
    sgate, sup, sdown = env["sgate"], env["sup"], env["sdown"]
    x1s, x2s, xln2Ts = env["x1s"], env["x2s"], env["xln2Ts"]
    id_f = env["id_f"]
    with ExitStack() as ph:
        sg_sb = ph.enter_context(nc.sbuf_tensor("p2_sg", [128, 64], F32))
        su_sb = ph.enter_context(nc.sbuf_tensor("p2_su", [128, 64], F32))
        sd_sb = ph.enter_context(nc.sbuf_tensor("p2_sd", [128, 16], F32))
        ev_sg = Q.add(lambda q: q.dma_start(out=sg_sb[:], in_=sgate.rearrange("(c p) -> p c", p=128),
                                            allow_slow_non_contiguous=True))
        ev_su = Q.add(lambda q: q.dma_start(out=su_sb[:], in_=sup.rearrange("(c p) -> p c", p=128),
                                            allow_slow_non_contiguous=True))
        ev_sd = Q.add(lambda q: q.dma_start(out=sd_sb[:], in_=sdown.rearrange("(c p) -> p c", p=128),
                                            allow_slow_non_contiguous=True))
        xlnTb = ph.enter_context(nc.sbuf_tensor("p2_xlnT", [128, 16 * 512], BF16))
        wg_sb = [ph.enter_context(nc.sbuf_tensor(f"p2_wg{i}", [128, 16 * 256], BF16)) for i in range(2)]
        wu_sb = [ph.enter_context(nc.sbuf_tensor(f"p2_wu{i}", [128, 16 * 256], BF16)) for i in range(2)]
        wd_sb = [ph.enter_context(nc.sbuf_tensor(f"p2_wd{i}", [128, 64 * 128], BF16)) for i in range(2)]
        gate_sb = ph.enter_context(nc.sbuf_tensor("p2_gate", [128, 512], BF16))
        fuse = ph.enter_context(nc.sbuf_tensor("p2_fuse", [128, 64 * 512], BF16))
        yTt = [ph.enter_context(nc.sbuf_tensor(f"p2_yT{i}", [128, 512], F32)) for i in range(2)]
        ynat = ph.enter_context(nc.sbuf_tensor("p2_yn", [128, 4 * 2048], F32))
        x1b = [ph.enter_context(nc.sbuf_tensor(f"p2_x1b{i}", [128, D], F32)) for i in range(2)]
        psG = [ph.enter_context(nc.psum_tensor(f"p2_psG{i}", [128, 512], F32)) for i in range(2)]
        psU = [ph.enter_context(nc.psum_tensor(f"p2_psU{i}", [128, 512], F32)) for i in range(2)]
        psY = [ph.enter_context(nc.psum_tensor(f"p2_psY{i}", [128, 512], F32)) for i in range(2)]
        psN = [ph.enter_context(nc.psum_tensor(f"p2_psN{i}", [128, 512], F32)) for i in range(2)]

        wg_free = [None, None]
        wu_free = [None, None]
        wd_free = [None, None]
        psG_free = [None, None]
        psU_free = [None, None]
        psY_free = [None, None]
        psN_free = [None, None]
        yT_free = [None, None]
        ynat_free = [None]
        gate_free = None
        fuse_free = None
        x2_free = [None, None]
        x1_free = [None, None]
        ev_mm = None

        def load_xln(c, wait_ev):
            return Q.add(lambda q, c=c: q.dma_start(
                out=xlnTb.ap().rearrange("p (a n) -> p a n", a=16),
                in_=xln2Ts[:, 512 * c:512 * (c + 1)].rearrange("(a p) n -> p a n", p=128)),
                waits=[wait_ev], slot="p2x")

        def load_w(kind, it2, sl):
            if kind == "g":
                return G.add(lambda q, it2=it2, sl=sl: q.dma_start(
                    out=wg_sb[sl].ap().rearrange("p (a n) -> p a n", a=16),
                    in_=wg8[:, 256 * it2:256 * (it2 + 1)].rearrange("(a p) n -> p a n", p=128)),
                    waits=[wg_free[sl]], slot=f"wg{sl}")
            if kind == "u":
                return G.add(lambda q, it2=it2, sl=sl: q.dma_start(
                    out=wu_sb[sl].ap().rearrange("p (a n) -> p a n", a=16),
                    in_=wu8[:, 256 * it2:256 * (it2 + 1)].rearrange("(a p) n -> p a n", p=128)),
                    waits=[wu_free[sl]], slot=f"wu{sl}")
            return G.add(lambda q, it2=it2, sl=sl: q.dma_start(
                out=wd_sb[sl].ap().rearrange("p (a n) -> p a n", a=64),
                in_=wd8[:, 128 * it2:128 * (it2 + 1)].rearrange("(a p) n -> p a n", p=128)),
                waits=[wd_free[sl]], slot=f"wd{sl}")

        ev_xln = [None] * 8
        ev_xln[0] = load_xln(0, None)
        for c in range(8):
            sl = c % 2
            evw_g = [None] * 32
            evw_u = [None] * 32
            evw_d = [None] * 16
            evw_g[0] = load_w("g", 0, 0)
            evw_u[0] = load_w("u", 0, 0)
            evw_d[0] = load_w("d", 0, 0)
            evw_d[1] = load_w("d", 1, 1)
            ev_fuse_last = None
            for it in range(64):
                it2 = it // 2
                half = it % 2
                if half == 0 and it2 + 1 < 32:
                    evw_g[it2 + 1] = load_w("g", it2 + 1, (it2 + 1) % 2)
                    evw_u[it2 + 1] = load_w("u", it2 + 1, (it2 + 1) % 2)
                wsl = it2 % 2
                pb = it % 2
                ev_mmg = None
                for dt in range(DT):
                    ev_mmg = Tn.add(lambda t, pb=pb, wsl=wsl, dt=dt, sl=sl, half=half: t.matmul(
                        psG[pb][:], wg_sb[wsl][:, 256 * dt + 128 * half:256 * dt + 128 * (half + 1)],
                        xlnTb[:, 512 * dt:512 * (dt + 1)],
                        start=(dt == 0), stop=(dt == DT - 1)),
                        waits=([evw_g[it2], ev_xln[c], psG_free[pb]] if dt == 0 else []),
                        inc=(dt == DT - 1))
                ev_mmu = None
                for dt in range(DT):
                    ev_mmu = Tn.add(lambda t, pb=pb, wsl=wsl, dt=dt, sl=sl, half=half: t.matmul(
                        psU[pb][:], wu_sb[wsl][:, 256 * dt + 128 * half:256 * dt + 128 * (half + 1)],
                        xlnTb[:, 512 * dt:512 * (dt + 1)],
                        start=(dt == 0), stop=(dt == DT - 1)),
                        waits=([evw_u[it2], psU_free[pb]] if dt == 0 else []),
                        inc=(dt == DT - 1))
                if half == 1:
                    wg_free[wsl] = ev_mmg
                    wu_free[wsl] = ev_mmu
                ev_si = S.add(lambda s, pb=pb, it=it: s.activation(
                    gate_sb[:], psG[pb][:], AF.Silu, scale=sg_sb[:, it:it + 1]),
                    waits=[ev_mmg, ev_sg, gate_free])
                psG_free[pb] = ev_si
                ev_fu = V.add(lambda v, pb=pb, it=it: v.scalar_tensor_tensor(
                    fuse[:, 512 * it:512 * (it + 1)], psU[pb][:], su_sb[:, it:it + 1],
                    gate_sb[:], ALU.mult, ALU.mult),
                    waits=[ev_si, ev_mmu, ev_su, fuse_free])
                psU_free[pb] = ev_fu
                gate_free = ev_fu
                ev_fuse_last = ev_fu
            if c + 1 < 8:
                ev_xln[c + 1] = load_xln(c + 1, ev_mmu)
            ev_yn_done = None
            for ht in range(16):
                wsl = ht % 2
                pb = ht % 2
                for it in range(64):
                    ev_mm = Tn.add(lambda t, pb=pb, wsl=wsl, it=it: t.matmul(
                        psY[pb][:], wd_sb[wsl][:, 128 * it:128 * (it + 1)],
                        fuse[:, 512 * it:512 * (it + 1)],
                        start=(it == 0), stop=(it == 63)),
                        waits=([evw_d[ht], ev_fuse_last, psY_free[pb]] if it == 0 else []),
                        inc=(it == 63))
                wd_free[wsl] = ev_mm
                if ht + 2 < 16:
                    evw_d[ht + 2] = load_w("d", ht + 2, ht % 2)
                ysl = ht % 2
                ev_yT = S.add(lambda s, pb=pb, ht=ht, ysl=ysl: s.activation(
                    yTt[ysl][:], psY[pb][:], AF.Copy, scale=sd_sb[:, ht:ht + 1]),
                    waits=[ev_mm, ev_sd, yT_free[ysl]])
                psY_free[pb] = ev_yT
                pb2 = ht % 2
                evt = None
                for ts in range(4):
                    evt = Tn.add(lambda t, pb2=pb2, ysl=ysl, ts=ts: t.transpose(
                        psN[pb2][:, 128 * ts:128 * (ts + 1)],
                        yTt[ysl][:, 128 * ts:128 * (ts + 1)], id_f[:]),
                        waits=([ev_yT, psN_free[pb2]] if ts == 0 else []), inc=(ts == 3))
                ev_yn = V.add(lambda v, pb2=pb2, ht=ht: v.tensor_copy(
                    ynat.ap().rearrange("p (s h) -> p s h", s=4)[:, :, 128 * ht:128 * (ht + 1)],
                    psN[pb2].ap().rearrange("p (s h) -> p s h", s=4)),
                    waits=[evt, x2_free[0], x2_free[1]])
                psN_free[pb2] = ev_yn
                ev_yn_done = ev_yn
                yT_free[ysl] = evt
            fuse_free = ev_mm
            for ts in range(4):
                ti = 4 * c + ts
                bs = ti % 2
                ev_x1 = Q.add(lambda q, ti=ti, bs=bs: q.dma_start(
                    out=x1b[bs][:], in_=x1s[128 * ti:128 * (ti + 1), :]), waits=[x1_free[bs]],
                    slot=f"x1b{bs}")
                ev_x2 = V.add(lambda v, ts=ts, bs=bs: v.tensor_add(
                    ynat[:, 2048 * ts:2048 * (ts + 1)], ynat[:, 2048 * ts:2048 * (ts + 1)],
                    x1b[bs][:]),
                    waits=[ev_yn_done, ev_x1])
                x1_free[bs] = ev_x2
                ev_st = Q.add(lambda q, ti=ti, ts=ts: q.dma_start(
                    out=x2s[128 * ti:128 * (ti + 1), :], in_=ynat[:, 2048 * ts:2048 * (ts + 1)]),
                    waits=[ev_x2], slot=f"x2st{ti % 2}")
                x2_free[ti % 2] = ev_st
                ynat_free[0] = ev_st
        p.barrier()



def p3_emit(nc, p, es, env, p3sub=None):
    Tn, V, S, Q, G = p.T, p.V, p.S, p.Q, p.G
    x2s, x3s, xln4Ts, ulnSc = env["x2s"], env["x3s"], env["xln4Ts"], env["ulnSc"]
    lre, lim, bre, bim, cre, cim = env["lre"], env["lim"], env["bre"], env["bim"], env["cre"], env["cim"]
    dvec, logdelta = env["dvec"], env["logdelta"]
    id_bf, id_f = env["id_bf"], env["id_f"]
    u128, iota0, iota1, ones1, e127 = env["u128"], env["iota0"], env["iota1"], env["ones1"], env["e127"]

    with ExitStack() as ph:
        prm = ph.enter_context(nc.sbuf_tensor("p3_prm", [1, 32 * 26], F32))

        def pr(i):
            return prm[:, 32 * i:32 * (i + 1)]

        (LRE, LIM, LDT, LAMRE, LRED, LIMD, ABS_, EXRE, C_, S_, ARE, AIM,
         T1, T2, T3, T4, BBR, BBI, A128R, A128I, KRE, KIM, BSR, BSI, RT1, RT2) = range(26)
        prm_i = ph.enter_context(nc.sbuf_tensor("p3_prmi", [1, 32], I32))

        ev_lre = Q.add(lambda q: q.dma_start(out=pr(LRE), in_=lre[:]))
        ev_lim = Q.add(lambda q: q.dma_start(out=pr(LIM), in_=lim[:]))
        ev_ldt = Q.add(lambda q: q.dma_start(out=pr(LDT), in_=logdelta[:, 0:32]))
        e1 = S.add(lambda s: s.activation(pr(T1), pr(LRE), AF.Abs), waits=[ev_lre])
        e2 = S.add(lambda s: s.activation(pr(T2), pr(T1), AF.Exp, scale=-1.0), waits=[e1])
        e3 = S.add(lambda s: s.activation(pr(T3), pr(T2), AF.Ln, bias=env["epsc"][0:1, 1:2]), waits=[e2])
        e4 = V.add(lambda v: v.tensor_scalar_max(pr(T4), pr(LRE), 0.0), waits=[ev_lre])
        e5 = V.add(lambda v: v.tensor_add(pr(T4), pr(T4), pr(T3)), waits=[e3, e4])
        e6 = V.add(lambda v: v.tensor_scalar(pr(LAMRE), pr(T4), 0.0101, -1.0, ALU.add, ALU.mult),
                   waits=[e5, ev_lim])
        e7 = S.add(lambda s: s.activation(pr(T1), pr(LDT), AF.Exp), waits=[ev_ldt, e3])
        e8 = V.add(lambda v: v.tensor_mul(pr(LRED), pr(LAMRE), pr(T1)), waits=[e6, e7])
        e9 = V.add(lambda v: v.tensor_mul(pr(LIMD), pr(LIM), pr(T1)), waits=[e8])
        eA = S.add(lambda s: s.activation(pr(ABS_), pr(LRED), AF.Exp), waits=[e8])

        def rred(dst_i, src_i, off, wl):
            w1 = V.add(lambda v: v.tensor_scalar(pr(RT1), pr(src_i), off, 1.0 / (2 * PI),
                                                 ALU.add, ALU.mult), waits=wl)
            w2 = V.add(lambda v: v.tensor_copy(prm_i[:], pr(RT1)), waits=[w1])
            w3 = V.add(lambda v: v.tensor_copy(pr(RT1), prm_i[:]), waits=[w2])
            w4 = V.add(lambda v: v.scalar_tensor_tensor(pr(RT1), pr(RT1), -2 * PI, pr(src_i),
                                                        ALU.mult, ALU.add), waits=[w3])
            w5 = V.add(lambda v: v.tensor_scalar_add(pr(dst_i), pr(RT1), off), waits=[w4])
            f1 = V.add(lambda v: v.tensor_scalar(pr(RT1), pr(dst_i), PI, None, ALU.is_gt), waits=[w5])
            f2 = V.add(lambda v: v.scalar_tensor_tensor(pr(dst_i), pr(RT1), -2 * PI, pr(dst_i),
                                                        ALU.mult, ALU.add), waits=[f1])
            f3 = V.add(lambda v: v.tensor_scalar(pr(RT1), pr(dst_i), -PI, None, ALU.is_lt), waits=[f2])
            f4 = V.add(lambda v: v.scalar_tensor_tensor(pr(dst_i), pr(RT1), 2 * PI, pr(dst_i),
                                                        ALU.mult, ALU.add), waits=[f3])
            return f4

        eB = rred(T4, LIMD, 0.0, [e9, eA])
        eC = S.add(lambda s: s.activation(pr(S_), pr(T4), AF.Sin), waits=[eB])
        eD = rred(T4, LIMD, PI / 2, [eC])
        eE = S.add(lambda s: s.activation(pr(C_), pr(T4), AF.Sin), waits=[eD])
        eF = V.add(lambda v: v.tensor_mul(pr(ARE), pr(ABS_), pr(C_)), waits=[eE])
        eG = V.add(lambda v: v.tensor_mul(pr(AIM), pr(ABS_), pr(S_)), waits=[eF])
        eH = V.add(lambda v: v.tensor_scalar_add(pr(T1), pr(ARE), -1.0), waits=[eG])
        eI = V.add(lambda v: v.tensor_mul(pr(T2), pr(LAMRE), pr(LAMRE)), waits=[eH])
        eJ = V.add(lambda v: v.scalar_tensor_tensor(pr(T3), pr(LIM), 1.0, pr(LIM), ALU.mult, ALU.mult),
                   waits=[eI])
        eK = V.add(lambda v: v.tensor_add(pr(T2), pr(T2), pr(T3)), waits=[eJ])
        eL = V.add(lambda v: v.reciprocal(pr(T2), pr(T2)), waits=[eK])
        eM = V.add(lambda v: v.tensor_mul(pr(T3), pr(T1), pr(LAMRE)), waits=[eL])
        eN = V.add(lambda v: v.scalar_tensor_tensor(pr(T4), pr(AIM), 1.0, pr(LIM), ALU.mult, ALU.mult),
                   waits=[eM])
        eO = V.add(lambda v: v.tensor_add(pr(T3), pr(T3), pr(T4)), waits=[eN])
        eP = V.add(lambda v: v.tensor_mul(pr(BSR), pr(T3), pr(T2)), waits=[eO])
        eQ = V.add(lambda v: v.tensor_mul(pr(T3), pr(AIM), pr(LAMRE)), waits=[eP])
        eR = V.add(lambda v: v.scalar_tensor_tensor(pr(T4), pr(T1), 1.0, pr(LIM), ALU.mult, ALU.mult),
                   waits=[eQ])
        eS_ = V.add(lambda v: v.tensor_sub(pr(T3), pr(T3), pr(T4)), waits=[eR])
        eT = V.add(lambda v: v.tensor_mul(pr(BSI), pr(T3), pr(T2)), waits=[eS_])
        ev_bs = eT
        eU = S.add(lambda s: s.activation(pr(T1), pr(LRED), AF.Exp, scale=128.0), waits=[eT])
        eV = V.add(lambda v: v.tensor_scalar_mul(pr(T2), pr(LIMD), 128.0), waits=[eU])
        eW = rred(T4, T2, 0.0, [eV])
        eX = S.add(lambda s: s.activation(pr(A128I), pr(T4), AF.Sin), waits=[eW])
        eY = rred(T4, T2, PI / 2, [eX])
        eZ = S.add(lambda s: s.activation(pr(A128R), pr(T4), AF.Sin), waits=[eY])
        ea = V.add(lambda v: v.scalar_tensor_tensor(pr(A128R), pr(T1), 1.0, pr(A128R),
                                                    ALU.mult, ALU.mult), waits=[eZ])
        eb = V.add(lambda v: v.scalar_tensor_tensor(pr(A128I), pr(T1), 1.0, pr(A128I),
                                                    ALU.mult, ALU.mult), waits=[ea])
        ev_a128 = eb

        grids = ph.enter_context(nc.sbuf_tensor("p3_grids", [128, 32 * 10], F32))

        def gr(i):
            return grids[:, 32 * i:32 * (i + 1)]

        GLRE, GLIM, GPR, GPI, GIR, GII, G1R, G1I, GT1, GT2 = range(10)
        tmpm = ph.enter_context(nc.sbuf_tensor("p3_tmpm", [128, 32], F32))
        tmpp = ph.enter_context(nc.sbuf_tensor("p3_tmpp", [128, 32], F32))
        tmpq = ph.enter_context(nc.sbuf_tensor("p3_tmpq", [128, 32], F32))
        tki = ph.enter_context(nc.sbuf_tensor("p3_tki", [128, 32], I32))

        bbT = ph.enter_context(nc.sbuf_tensor("p3_bbT", [128, 16 * 64], BF16))
        ctT = ph.enter_context(nc.sbuf_tensor("p3_ctT", [32, 2 * D], BF16))
        cbuf = ph.enter_context(nc.sbuf_tensor("p3_cbuf", [128, 32], F32))
        dbc = ph.enter_context(nc.sbuf_tensor("p3_dbc", [128, D], F32))
        dline = ph.enter_context(nc.sbuf_tensor("p3_dline", [1, D], F32))
        with ExitStack() as pp:
            psg = pp.enter_context(nc.psum_tensor("p3_psg", [128, 512], F32))
            em1 = Tn.add(lambda t: t.matmul(psg[:, 0:32], iota0[:], pr(LRED), start=True, stop=True),
                         waits=[ev_bs, ev_a128])
            em1b = Tn.add(lambda t: t.matmul(psg[:, 32:64], iota0[:], pr(LIMD), start=True, stop=True),
                          waits=[em1])
            em2 = Tn.add(lambda t: t.matmul(psg[:, 64:96], iota1[:], pr(LRED), start=True, stop=True),
                         waits=[em1b])
            em2b = Tn.add(lambda t: t.matmul(psg[:, 96:128], iota1[:], pr(LIMD), start=True, stop=True),
                          waits=[em2])
            ec1 = V.add(lambda v: v.tensor_copy(gr(GLRE), psg[:, 0:32]), waits=[em2b])
            ec2 = V.add(lambda v: v.tensor_copy(gr(GLIM), psg[:, 32:64]), waits=[ec1])
            ec3 = V.add(lambda v: v.tensor_copy(gr(GT1), psg[:, 64:96]), waits=[ec2])
            ec4 = V.add(lambda v: v.tensor_copy(gr(GT2), psg[:, 96:128]), waits=[ec3])

            def grred(dst, src_ap, off, wl):
                w1 = V.add(lambda v: v.tensor_scalar(tmpq[:], src_ap, off, 1.0 / (2 * PI),
                                                     ALU.add, ALU.mult), waits=wl)
                w2 = V.add(lambda v: v.tensor_copy(tki[:], tmpq[:]), waits=[w1])
                w3 = V.add(lambda v: v.tensor_copy(tmpq[:], tki[:]), waits=[w2])
                w4 = V.add(lambda v: v.scalar_tensor_tensor(tmpq[:], tmpq[:], -2 * PI, src_ap,
                                                            ALU.mult, ALU.add), waits=[w3])
                w5 = V.add(lambda v: v.tensor_scalar_add(dst, tmpq[:], off), waits=[w4])
                f1 = V.add(lambda v: v.tensor_scalar(tmpq[:], dst, PI, None, ALU.is_gt), waits=[w5])
                f2 = V.add(lambda v: v.scalar_tensor_tensor(dst, tmpq[:], -2 * PI, dst,
                                                            ALU.mult, ALU.add), waits=[f1])
                f3 = V.add(lambda v: v.tensor_scalar(tmpq[:], dst, -PI, None, ALU.is_lt), waits=[f2])
                f4 = V.add(lambda v: v.scalar_tensor_tensor(dst, tmpq[:], 2 * PI, dst,
                                                            ALU.mult, ALU.add), waits=[f3])
                return f4

            def build_grid(mag_slot, ph_slot, dr, di, wl):
                wm = S.add(lambda s: s.activation(tmpm[:], gr(mag_slot), AF.Exp), waits=wl)
                wp = grred(tmpp[:], gr(ph_slot), 0.0, [wm])
                ws = S.add(lambda s: s.activation(gr(di), tmpp[:], AF.Sin), waits=[wp])
                wp2 = grred(tmpp[:], gr(ph_slot), PI / 2, [ws])
                wc = S.add(lambda s: s.activation(gr(dr), tmpp[:], AF.Sin), waits=[wp2])
                w1 = V.add(lambda v: v.tensor_mul(gr(dr), tmpm[:], gr(dr)), waits=[wc])
                w2 = V.add(lambda v: v.tensor_mul(gr(di), tmpm[:], gr(di)), waits=[w1])
                return w2

            ev_gp = build_grid(GLRE, GLIM, GPR, GPI, [ec4])
            ev_g1 = build_grid(GT1, GT2, G1R, G1I, [ev_gp])
            wm = S.add(lambda s: s.activation(tmpm[:], gr(GLRE), AF.Exp, scale=-1.0), waits=[ev_g1])
            wp = grred(tmpp[:], gr(GLIM), 0.0, [wm])
            wsn = S.add(lambda s: s.activation(tmpq[:], tmpp[:], AF.Sin), waits=[wp])
            wii = V.add(lambda v: v.scalar_tensor_tensor(gr(GII), tmpq[:], -1.0, tmpm[:],
                                                         ALU.mult, ALU.mult), waits=[wsn])
            wp2 = grred(tmpp[:], gr(GLIM), PI / 2, [wii])
            wcs = S.add(lambda s: s.activation(tmpq[:], tmpp[:], AF.Sin), waits=[wp2])
            wir = V.add(lambda v: v.tensor_mul(gr(GIR), tmpq[:], tmpm[:]), waits=[wcs])
            ev_grids = wir

            brebuf = pp.enter_context(nc.sbuf_tensor("p3_breb", [32, D], F32))
            bimbuf = pp.enter_context(nc.sbuf_tensor("p3_bimb", [32, D], F32))
            bsr_b = pp.enter_context(nc.sbuf_tensor("p3_bsrb", [128, 32], F32))
            bsi_b = pp.enter_context(nc.sbuf_tensor("p3_bsib", [128, 32], F32))
            psb0 = pp.enter_context(nc.psum_tensor("p3_psb0", [128, 128], F32))
            evb1 = Q.add(lambda q: q.dma_start(out=brebuf[:], in_=bre[:]))
            evb2 = Q.add(lambda q: q.dma_start(out=bimbuf[:], in_=bim[:]))
            emb = Tn.add(lambda t: t.matmul(psg[:, 128:160], ones1[:], pr(BSR), start=True, stop=True),
                         waits=[ev_bs, ev_grids])
            emb2 = Tn.add(lambda t: t.matmul(psg[:, 160:192], ones1[:], pr(BSI), start=True, stop=True),
                          waits=[emb])
            evbb1 = V.add(lambda v: v.tensor_copy(bsr_b[:], psg[:, 128:160]), waits=[emb2])
            evbb2 = V.add(lambda v: v.tensor_copy(bsi_b[:], psg[:, 160:192]), waits=[evbb1])
            ev_bbT = None
            for dt in range(DT):
                et1 = Tn.add(lambda t, dt=dt: t.transpose(
                    psb0[:, 0:32], brebuf[:, 128 * dt:128 * (dt + 1)], id_f[0:32, 0:32]),
                    waits=[evb1, evbb2] if dt == 0 else [ev_bbT])
                ev1 = V.add(lambda v: v.tensor_copy(tmpm[:], psb0[:, 0:32]), waits=[et1])
                et2 = Tn.add(lambda t, dt=dt: t.transpose(
                    psb0[:, 32:64], bimbuf[:, 128 * dt:128 * (dt + 1)], id_f[0:32, 0:32]),
                    waits=[ev1, evb2])
                ev2 = V.add(lambda v: v.tensor_copy(tmpp[:], psb0[:, 32:64]), waits=[et2])
                ev3 = V.add(lambda v: v.tensor_mul(tmpq[:], bsr_b[:], tmpm[:]), waits=[ev2])
                ev4 = V.add(lambda v: v.scalar_tensor_tensor(
                    gr(GT1), bsi_b[:], 1.0, tmpp[:], ALU.mult, ALU.mult), waits=[ev3])
                ev5 = V.add(lambda v: v.tensor_sub(tmpq[:], tmpq[:], gr(GT1)), waits=[ev4])
                ev6 = V.add(lambda v, dt=dt: v.tensor_copy(bbT[:, 64 * dt:64 * dt + 32], tmpq[:]),
                            waits=[ev5])
                ev7 = V.add(lambda v: v.tensor_mul(tmpq[:], bsr_b[:], tmpp[:]), waits=[ev6])
                ev8 = V.add(lambda v: v.scalar_tensor_tensor(
                    gr(GT1), bsi_b[:], 1.0, tmpm[:], ALU.mult, ALU.mult), waits=[ev7])
                ev9 = V.add(lambda v: v.tensor_add(tmpq[:], tmpq[:], gr(GT1)), waits=[ev8])
                ev_bbT = V.add(lambda v, dt=dt: v.tensor_copy(bbT[:, 64 * dt + 32:64 * dt + 64], tmpq[:]),
                               waits=[ev9])

            ev_ctT = None
            for half, (csrc, sgn) in enumerate([(cre, 2.0), (cim, -2.0)]):
                for dt in range(DT):
                    evc = Q.add(lambda q, csrc=csrc, dt=dt: q.dma_start(
                        out=cbuf[:], in_=csrc[128 * dt:128 * (dt + 1), :]),
                        waits=[ev_ctT])
                    etc = Tn.add(lambda t: t.transpose(psb0[0:32, :], cbuf[:], id_f[:]),
                                 waits=[evc, ev_ctT, ev_bbT])
                    ev_ctT = V.add(lambda v, half=half, dt=dt, sgn=sgn: v.tensor_scalar_mul(
                        ctT[:, D * half + 128 * dt: D * half + 128 * (dt + 1)],
                        psb0[0:32, :], sgn), waits=[etc])

            evd = Q.add(lambda q: q.dma_start(out=dline[:], in_=dvec[:]))
            ev_dbc = None
            for nch in range(4):
                emd = Tn.add(lambda t, nch=nch: t.matmul(
                    psg[:, 0:512], ones1[:], dline[:, 512 * nch:512 * (nch + 1)],
                    start=True, stop=True), waits=[evd, ev_dbc, ec4, evbb2, ev_bbT])
                ev_dbc = V.add(lambda v, nch=nch: v.tensor_copy(dbc[:, 512 * nch:512 * (nch + 1)],
                                                                psg[:, 0:512]), waits=[emd])
            p.barrier()
        if p3sub == "prep":
            return

        x2b = [ph.enter_context(nc.sbuf_tensor(f"p3_x2b{i}", [128, D], F32)) for i in range(2)]
        ulnb = [ph.enter_context(nc.sbuf_tensor(f"p3_ulnb{i}", [128, D], BF16)) for i in range(2)]
        ulnT = ph.enter_context(nc.sbuf_tensor("p3_ulnT", [128, 16 * 128], BF16))
        buR = ph.enter_context(nc.sbuf_tensor("p3_buR", [128, 32 * 32], F32))
        buI = ph.enter_context(nc.sbuf_tensor("p3_buI", [128, 32 * 32], F32))
        lt3 = mk_ln_tmp(nc, ph, "p3", env["epsc"][:, 0:1])
        with ExitStack() as pp:
            psu = [pp.enter_context(nc.psum_tensor(f"p3_psu{i}", [128, 1024], BF16)) for i in range(2)]
            psbu = pp.enter_context(nc.psum_tensor("p3_psbu", [128, 64], F32))
            x2_ld = [None] * 32
            x2_free = [None, None]
            psu_free = [None, None]
            ulnT_free = None
            ulnb_free = [None, None]
            ev_bu_all = None
            prev_tr = None

            def load_x2(i):
                sl = i % 2
                x2_ld[i] = Q.add(lambda q, i=i, sl=sl: q.dma_start(
                    out=x2b[sl][:], in_=x2s[128 * i:128 * (i + 1), :]), waits=[x2_free[sl]],
                    slot=f"x2b{sl}")

            load_x2(0)
            for i in range(32):
                if i + 1 < 32:
                    load_x2(i + 1)
                sl = i % 2
                V.pending.append(x2_ld[i])
                if ulnb_free[sl] is not None:
                    V.pending.append(ulnb_free[sl])
                ev_ln = ln_tile(p, x2b[sl][:], ulnb[sl], lt3)
                x2_free[sl] = ev_ln
                uln_st = G.add(lambda q, i=i, sl=sl: q.dma_start(
                    out=ulnSc[128 * i:128 * (i + 1), :], in_=ulnb[sl][:]), waits=[ev_ln],
                    slot="ulnst")
                ev_trl = None
                for half in range(2):
                    pb = half
                    evt = None
                    for k in range(8):
                        dt = 8 * half + k
                        evt = Tn.add(lambda t, pb=pb, k=k, dt=dt, sl=sl: t.transpose(
                            psu[pb][:, 128 * k:128 * (k + 1)],
                            ulnb[sl][:, 128 * dt:128 * (dt + 1)], id_bf[:]),
                            waits=([ev_ln, psu_free[pb]] if k == 0 else []), inc=(k == 7))
                        prev_tr = evt
                    ev_trl = S.add(lambda s, pb=pb, half=half: s.copy(
                        ulnT[:, 1024 * half:1024 * (half + 1)], psu[pb][:]),
                        waits=[evt, ulnT_free])
                    psu_free[pb] = ev_trl
                ulnb_free[sl] = prev_tr
                V.pending.append(uln_st)
                ev_mm = None
                for dt in range(DT):
                    ev_mm = Tn.add(lambda t, dt=dt: t.matmul(
                        psbu[:], ulnT[:, 128 * dt:128 * (dt + 1)],
                        bbT[:, 64 * dt:64 * (dt + 1)],
                        start=(dt == 0), stop=(dt == DT - 1)),
                        waits=([ev_trl, ev_bbT, ev_bu_all] if dt == 0 else []),
                        inc=(dt == DT - 1))
                ulnT_free = ev_mm
                if "dq" in env:
                    env["dq"].pump(2)
                evr = V.add(lambda v, i=i: v.tensor_copy(buR[:, 32 * i:32 * (i + 1)], psbu[:, 0:32]),
                            waits=[ev_mm])
                ev_bu_all = V.add(lambda v, i=i: v.tensor_copy(buI[:, 32 * i:32 * (i + 1)],
                                                               psbu[:, 32:64]), waits=[evr])
            p.barrier()
        if p3sub == "a":
            return

        xlR = ph.enter_context(nc.sbuf_tensor("p3_xlR", [128, 32 * 32], F32))
        xlI = ph.enter_context(nc.sbuf_tensor("p3_xlI", [128, 32 * 32], F32))
        cbR = ph.enter_context(nc.sbuf_tensor("p3_cbR", [128, 32], F32))
        cbI = ph.enter_context(nc.sbuf_tensor("p3_cbI", [128, 32], F32))
        srowR = ph.enter_context(nc.sbuf_tensor("p3_srowR", [1, 32 * 32], F32))
        srowI = ph.enter_context(nc.sbuf_tensor("p3_srowI", [1, 32 * 32], F32))
        krowR = ph.enter_context(nc.sbuf_tensor("p3_krowR", [1, 32 * 32], F32))
        krowI = ph.enter_context(nc.sbuf_tensor("p3_krowI", [1, 32 * 32], F32))
        kcR = ph.enter_context(nc.sbuf_tensor("p3_kcR", [1, 32], F32))
        kcI = ph.enter_context(nc.sbuf_tensor("p3_kcI", [1, 32], F32))
        xsTR = ph.enter_context(nc.sbuf_tensor("p3_xsTR", [32, T], BF16))
        xsTI = ph.enter_context(nc.sbuf_tensor("p3_xsTI", [32, T], BF16))
        with ExitStack() as pp:
            pscan = pp.enter_context(nc.psum_tensor("p3_pscan", [128, 64], F32))
            psrow = pp.enter_context(nc.psum_tensor("p3_psrow", [1, 64], F32))
            psb = pp.enter_context(nc.psum_tensor("p3_psb", [128, 128], F32))
            ev_srow = [None] * 32
            ev_xl = [None] * 32
            pscan_free = None
            psrow_free = None
            ev_cum_prev = None
            for ci in range(32):
                bR = buR[:, 32 * ci:32 * (ci + 1)]
                bI = buI[:, 32 * ci:32 * (ci + 1)]
                w1 = V.add(lambda v, bR=bR: v.tensor_mul(cbR[:], gr(GIR), bR), waits=[ev_cum_prev])
                w2 = V.add(lambda v, bI=bI: v.scalar_tensor_tensor(tmpq[:], gr(GII), 1.0, bI,
                                                                   ALU.mult, ALU.mult), waits=[w1])
                w3 = V.add(lambda v: v.tensor_sub(cbR[:], cbR[:], tmpq[:]), waits=[w2])
                w4 = V.add(lambda v, bI=bI: v.tensor_mul(cbI[:], gr(GIR), bI), waits=[w3])
                w5 = V.add(lambda v, bR=bR: v.scalar_tensor_tensor(tmpq[:], gr(GII), 1.0, bR,
                                                                   ALU.mult, ALU.mult), waits=[w4])
                w6 = V.add(lambda v: v.tensor_add(cbI[:], cbI[:], tmpq[:]), waits=[w5])
                m1 = Tn.add(lambda t: t.matmul(pscan[:, 0:32], u128[:], cbR[:], start=True, stop=True),
                            waits=[w6, pscan_free])
                m2 = Tn.add(lambda t: t.matmul(pscan[:, 32:64], u128[:], cbI[:], start=True, stop=True),
                            waits=[m1])
                ev_cum_prev = m2
                w7 = V.add(lambda v, ci=ci: v.tensor_mul(xlR[:, 32 * ci:32 * (ci + 1)],
                                                         gr(GPR), pscan[:, 0:32]), waits=[m2])
                w8 = V.add(lambda v: v.scalar_tensor_tensor(
                    tmpq[:], gr(GPI), 1.0, pscan[:, 32:64], ALU.mult, ALU.mult), waits=[w7])
                w9 = V.add(lambda v, ci=ci: v.tensor_sub(xlR[:, 32 * ci:32 * (ci + 1)],
                                                         xlR[:, 32 * ci:32 * (ci + 1)], tmpq[:]),
                           waits=[w8])
                wA = V.add(lambda v, ci=ci: v.tensor_mul(xlI[:, 32 * ci:32 * (ci + 1)],
                                                         gr(GPI), pscan[:, 0:32]), waits=[w9])
                wB = V.add(lambda v: v.scalar_tensor_tensor(
                    tmpq[:], gr(GPR), 1.0, pscan[:, 32:64], ALU.mult, ALU.mult), waits=[wA])
                wC = V.add(lambda v, ci=ci: v.tensor_add(xlI[:, 32 * ci:32 * (ci + 1)],
                                                         xlI[:, 32 * ci:32 * (ci + 1)], tmpq[:]),
                           waits=[wB])
                pscan_free = wC
                m3 = Tn.add(lambda t, ci=ci: t.matmul(psrow[:, 0:32], e127[:],
                                                      xlR[:, 32 * ci:32 * (ci + 1)],
                                                      start=True, stop=True), waits=[wC, psrow_free])
                m4 = Tn.add(lambda t, ci=ci: t.matmul(psrow[:, 32:64], e127[:],
                                                      xlI[:, 32 * ci:32 * (ci + 1)],
                                                      start=True, stop=True), waits=[m3])
                s1 = V.add(lambda v, ci=ci: v.tensor_copy(srowR[:, 32 * ci:32 * (ci + 1)],
                                                          psrow[:, 0:32]), waits=[m4])
                s2 = V.add(lambda v, ci=ci: v.tensor_copy(srowI[:, 32 * ci:32 * (ci + 1)],
                                                          psrow[:, 32:64]), waits=[s1])
                psrow_free = s2
                ev_srow[ci] = s2
                ev_xl[ci] = wC
                if "dq" in env:
                    env["dq"].pump(1)
            k1 = V.add(lambda v: v.tensor_copy(kcR[:], srowR[:, 0:32]), waits=[ev_srow[0]])
            k2 = V.add(lambda v: v.tensor_copy(kcI[:], srowI[:, 0:32]), waits=[k1])
            k3 = V.add(lambda v: v.tensor_copy(krowR[:, 0:32], kcR[:]), waits=[k2])
            kl = V.add(lambda v: v.tensor_copy(krowI[:, 0:32], kcI[:]), waits=[k3])
            for ci in range(1, 32):
                c1 = V.add(lambda v: v.tensor_mul(tmpq[0:1, :], pr(A128R), kcR[:]),
                           waits=[kl, ev_srow[ci]])
                c2 = V.add(lambda v: v.scalar_tensor_tensor(tmpm[0:1, :], pr(A128I), 1.0, kcI[:],
                                                            ALU.mult, ALU.mult), waits=[c1])
                c3 = V.add(lambda v: v.tensor_sub(tmpq[0:1, :], tmpq[0:1, :], tmpm[0:1, :]),
                           waits=[c2])
                c4 = V.add(lambda v, ci=ci: v.tensor_add(tmpq[0:1, :], tmpq[0:1, :],
                                                         srowR[:, 32 * ci:32 * (ci + 1)]), waits=[c3])
                c5 = V.add(lambda v: v.tensor_mul(tmpm[0:1, :], pr(A128R), kcI[:]), waits=[c4])
                c6 = V.add(lambda v: v.scalar_tensor_tensor(tmpp[0:1, :], pr(A128I), 1.0, kcR[:],
                                                            ALU.mult, ALU.mult), waits=[c5])
                c7 = V.add(lambda v: v.tensor_add(tmpm[0:1, :], tmpm[0:1, :], tmpp[0:1, :]),
                           waits=[c6])
                c8 = V.add(lambda v, ci=ci: v.tensor_add(kcI[:], tmpm[0:1, :],
                                                         srowI[:, 32 * ci:32 * (ci + 1)]), waits=[c7])
                c9 = V.add(lambda v: v.tensor_copy(kcR[:], tmpq[0:1, :]), waits=[c8])
                cA = V.add(lambda v, ci=ci: v.tensor_copy(krowR[:, 32 * ci:32 * (ci + 1)], kcR[:]),
                           waits=[c9])
                kl = V.add(lambda v, ci=ci: v.tensor_copy(krowI[:, 32 * ci:32 * (ci + 1)], kcI[:]),
                           waits=[cA])
            ev_xsT_last = None
            pscan_free = kl
            for ci in range(32):
                if ci > 0:
                    mb1 = Tn.add(lambda t, ci=ci: t.matmul(
                        pscan[:, 0:32], ones1[:], krowR[:, 32 * (ci - 1):32 * ci],
                        start=True, stop=True), waits=[kl, pscan_free])
                    mb2 = Tn.add(lambda t, ci=ci: t.matmul(
                        pscan[:, 32:64], ones1[:], krowI[:, 32 * (ci - 1):32 * ci],
                        start=True, stop=True), waits=[mb1])
                    a1 = V.add(lambda v: v.tensor_mul(tmpq[:], gr(G1R), pscan[:, 0:32]), waits=[mb2])
                    a2 = V.add(lambda v: v.scalar_tensor_tensor(tmpm[:], gr(G1I), 1.0, pscan[:, 32:64],
                                                                ALU.mult, ALU.mult), waits=[a1])
                    a3 = V.add(lambda v: v.tensor_sub(tmpq[:], tmpq[:], tmpm[:]), waits=[a2])
                    a4 = V.add(lambda v, ci=ci: v.tensor_add(xlR[:, 32 * ci:32 * (ci + 1)],
                                                             xlR[:, 32 * ci:32 * (ci + 1)], tmpq[:]),
                               waits=[a3])
                    a5 = V.add(lambda v: v.tensor_mul(tmpq[:], gr(G1I), pscan[:, 0:32]), waits=[a4])
                    a6 = V.add(lambda v: v.scalar_tensor_tensor(tmpm[:], gr(G1R), 1.0, pscan[:, 32:64],
                                                                ALU.mult, ALU.mult), waits=[a5])
                    a7 = V.add(lambda v: v.tensor_add(tmpq[:], tmpq[:], tmpm[:]), waits=[a6])
                    a8 = V.add(lambda v, ci=ci: v.tensor_add(xlI[:, 32 * ci:32 * (ci + 1)],
                                                             xlI[:, 32 * ci:32 * (ci + 1)], tmpq[:]),
                               waits=[a7])
                    pscan_free = a8
                    ev_apply = a8
                else:
                    ev_apply = kl
                t1 = Tn.add(lambda t, ci=ci: t.transpose(
                    psb[0:32, :], xlR[:, 32 * ci:32 * (ci + 1)], id_f[:]),
                    waits=[ev_apply, ev_xl[ci]])
                e1_ = V.add(lambda v, ci=ci: v.tensor_copy(xsTR[:, 128 * ci:128 * (ci + 1)],
                                                           psb[0:32, :]), waits=[t1])
                t2 = Tn.add(lambda t, ci=ci: t.transpose(
                    psb[0:32, :], xlI[:, 32 * ci:32 * (ci + 1)], id_f[:]), waits=[e1_])
                ev_xsT_last = V.add(lambda v, ci=ci: v.tensor_copy(xsTI[:, 128 * ci:128 * (ci + 1)],
                                                                   psb[0:32, :]), waits=[t2])
            p.barrier()
        if p3sub == "b":
            return

        x3t = [ph.enter_context(nc.sbuf_tensor(f"p3_x3t{i}", [128, D], F32)) for i in range(2)]
        ulnb2 = [ph.enter_context(nc.sbuf_tensor(f"p3_ulnb2{i}", [128, D], BF16)) for i in range(2)]
        xln4b = ph.enter_context(nc.sbuf_tensor("p3_xln4", [128, D], BF16))
        stg3 = ph.enter_context(nc.sbuf_tensor("p3_stg3", [128, 16 * 128], BF16))
        lt4 = mk_ln_tmp(nc, ph, "p3c", env["epsc"][:, 0:1])
        with ExitStack() as pp:
            psy = pp.enter_context(nc.psum_tensor("p3_psy", [128, 2048], F32))
            psu = [pp.enter_context(nc.psum_tensor(f"p3c_psu{i}", [128, 1024], BF16)) for i in range(2)]
            psy_free = None
            stg3_free = None
            x3t_free = [None, None]
            psu_free = [None, None]
            x2_free = [None, None]
            uln_free = [None, None]
            prev_tr = None
            for i in range(32):
                sl = i % 2
                ev_x2l = Q.add(lambda q, i=i, sl=sl: q.dma_start(
                    out=x2b[sl][:], in_=x2s[128 * i:128 * (i + 1), :]), waits=[x2_free[sl]],
                    slot=f"x2b{sl}")
                ev_ul = G.add(lambda q, i=i, sl=sl: q.dma_start(
                    out=ulnb2[sl][:], in_=ulnSc[128 * i:128 * (i + 1), :]), waits=[uln_free[sl]],
                    slot=f"uln2{sl}")
                ev_mm = None
                for nch in range(4):
                    Tn.add(lambda t, i=i, nch=nch: t.matmul(
                        psy[:, 512 * nch:512 * (nch + 1)], xsTR[:, 128 * i:128 * (i + 1)],
                        ctT[:, 512 * nch:512 * (nch + 1)], start=True, stop=False),
                        waits=([ev_xsT_last, ev_ctT, psy_free] if nch == 0 else []), inc=False)
                    ev_mm = Tn.add(lambda t, i=i, nch=nch: t.matmul(
                        psy[:, 512 * nch:512 * (nch + 1)], xsTI[:, 128 * i:128 * (i + 1)],
                        ctT[:, D + 512 * nch: D + 512 * (nch + 1)], start=False, stop=True),
                        inc=(nch == 3))
                ev_du = V.add(lambda v, i=i, sl=sl: v.tensor_mul(x3t[sl][:], dbc[:], ulnb2[sl][:]),
                              waits=[ev_mm, ev_dbc, ev_ul, x3t_free[sl]])
                uln_free[sl] = ev_du
                ev_y1 = V.add(lambda v, sl=sl: v.tensor_add(x3t[sl][:], x3t[sl][:], psy[:]),
                              waits=[ev_du])
                psy_free = ev_y1
                ev_y2 = V.add(lambda v, sl=sl: v.tensor_add(x3t[sl][:], x3t[sl][:], x2b[sl][:]),
                              waits=[ev_y1, ev_x2l])
                x2_free[sl] = ev_y2
                ev_st = G.add(lambda q, i=i, sl=sl: q.dma_start(
                    out=x3s[128 * i:128 * (i + 1), :], in_=x3t[sl][:]), waits=[ev_y2],
                    slot=f"x3st{sl}")
                if prev_tr is not None:
                    V.pending.append(prev_tr)
                ev_ln = ln_tile(p, x3t[sl], xln4b, lt4, waits=[ev_y2])
                x3t_free[sl] = ev_ln
                V.pending.append(ev_st)
                ev_stg3 = None
                for half in range(2):
                    pb = half
                    evt = None
                    for k in range(8):
                        dt = 8 * half + k
                        evt = Tn.add(lambda t, pb=pb, k=k, dt=dt: t.transpose(
                            psu[pb][:, 128 * k:128 * (k + 1)],
                            xln4b[:, 128 * dt:128 * (dt + 1)], id_bf[:]),
                            waits=([ev_ln, psu_free[pb]] if k == 0 else []), inc=(k == 7))
                        prev_tr = evt
                    ev_stg3 = S.add(lambda s, pb=pb, half=half: s.copy(
                        stg3[:, 1024 * half:1024 * (half + 1)], psu[pb][:]),
                        waits=[evt, stg3_free])
                    psu_free[pb] = ev_stg3
                ev_st3 = G.add(lambda q, i=i: q.dma_start(
                    out=xln4Ts[:, 128 * i:128 * (i + 1)].rearrange("(a p) n -> p a n", p=128),
                    in_=stg3.ap().rearrange("p (a n) -> p a n", a=16)), waits=[ev_stg3], slot="stg3")
                stg3_free = ev_st3
                if "dq" in env:
                    env["dq"].pump(1)
            p.barrier()




class DequantPump:

    def __init__(self, nc, p, es, env):
        self.nc, self.p, self.env = nc, p, env
        Tn, V, S, Q, G = p.T, p.V, p.S, p.Q, p.G
        id_f = env["id_f"]
        gup_sc, dn_sc = env["gup_sc"], env["dn_sc"]
        self.es = es
        self.sct = es.enter_context(nc.sbuf_tensor("dq_sct", [128, NO1 * 16 + 16 * 44], F32))
        self.psd = [es.enter_context(nc.psum_tensor(f"dq_psd{i}", [128, 1024], BF16)) for i in range(2)]
        self.pk_sb = [es.enter_context(nc.sbuf_tensor(f"dq_pk{i}", [128, 704], I32)) for i in range(2)]
        self.nib = es.enter_context(nc.sbuf_tensor("dq_nib", [128, 2048], I32))
        self.dq1 = es.enter_context(nc.sbuf_tensor("dq_dq1", [128, 2048], I32))
        self.dq2 = es.enter_context(nc.sbuf_tensor("dq_dq2", [128, 2048], I32))
        self.dq3 = es.enter_context(nc.sbuf_tensor("dq_dq3", [128, 2048], I32))
        self.wbf = es.enter_context(nc.sbuf_tensor("dq_wbf", [128, 2048], BF16))
        self.stg4 = [es.enter_context(nc.sbuf_tensor(f"dq_stg4{i}", [128, 1024], BF16)) for i in range(2)]
        self.state = {"pk_free": [None, None], "psd_free": [None, None],
                      "stg4_free": [None, None], "wbf_free": None, "nstg": 0, "npk": 0}
        with ExitStack() as pp:
            scbuf = pp.enter_context(nc.sbuf_tensor("dq_scbuf", [44, 2 * I_FFN1], F32))
            psf = pp.enter_context(nc.psum_tensor("dq_psf", [128, 128], F32))
            evs1 = G.add(lambda q: q.dma_start(out=scbuf[0:16, 0:2 * I_FFN1], in_=gup_sc[:]))
            ev_sct = None
            for ot in range(NO1):
                etc = Tn.add(lambda t, ot=ot: t.transpose(
                    psf[:, 0:16], scbuf[0:16, 128 * ot:128 * (ot + 1)], id_f[0:16, 0:16]),
                    waits=[evs1] if ot == 0 else [ev_sct])
                ev_sct = V.add(lambda v, ot=ot: v.tensor_copy(
                    self.sct[:, 16 * ot:16 * (ot + 1)], psf[:, 0:16]), waits=[etc])
            evs2 = G.add(lambda q: q.dma_start(out=scbuf[0:44, 0:D], in_=dn_sc[:]), waits=[ev_sct])
            DSOFF = NO1 * 16
            for ht in range(16):
                etc = Tn.add(lambda t, ht=ht: t.transpose(
                    psf[:, 0:44], scbuf[0:44, 128 * ht:128 * (ht + 1)], id_f[0:44, 0:44]),
                    waits=[evs2] if ht == 0 else [ev_sct])
                ev_sct = V.add(lambda v, ht=ht: v.tensor_copy(
                    self.sct[:, DSOFF + 44 * ht:DSOFF + 44 * (ht + 1)], psf[:, 0:44]),
                    waits=[etc])
            self.ev_sct = ev_sct
            self.p.barrier()
        self.work = []
        for ot in range(NO1):
            self.work.append((env["gup_pk"], ot, 0, 256, 16 * ot, env["wguTs"], 0))
        DSOFF = NO1 * 16
        for ht in range(16):
            for s3 in range(3):
                w0 = 256 * s3
                nw = 256 if s3 < 2 else 192
                self.work.append((env["dn_pk"], ht, w0, nw, DSOFF + 44 * ht + 16 * s3,
                                  env["wdnTs"], 8 * w0))
        self.cursor = 0
        self.pk_loaded = {}

    def pump(self, n=1):
        for _ in range(n):
            if self.cursor >= len(self.work):
                return
            self._emit(self.work[self.cursor])
            self.cursor += 1

    def drain(self):
        self.pump(len(self.work))

    def _emit(self, item):
        nc, p = self.nc, self.p
        Tn, V, S, Q, G = p.T, p.V, p.S, p.Q, p.G
        id_bf = self.env["id_bf"]
        st = self.state
        src_pk, ot, w0, nwords, scoff, dstT, rowbase = item
        nh = 8 * nwords
        nsc = nh // 128
        key = (id(src_pk), ot)
        if key in self.pk_loaded and self.pk_loaded[key][1] == w0 - 256 or key not in self.pk_loaded:
            pass
        if key not in self.pk_loaded or self.pk_loaded[key][0] is None:
            sl = st["npk"] % 2
            st["npk"] += 1
            tot_w = src_pk.shape[1]
            evp = G.add(lambda q, ot=ot, sl=sl, src_pk=src_pk, tot_w=tot_w: q.dma_start(
                out=self.pk_sb[sl][:, 0:tot_w], in_=src_pk[128 * ot:128 * (ot + 1), :]),
                waits=[st["pk_free"][sl]], slot=f"dqpk{sl}")
            self.pk_loaded = {key: (evp, sl)}
        evp, sl = self.pk_loaded[key]
        evn = None
        for j in range(8):
            evn = V.add(lambda v, j=j, sl=sl, w0=w0, nwords=nwords, nh=nh: v.tensor_scalar(
                self.nib[:, j:nh:8], self.pk_sb[sl][:, w0:w0 + nwords], 4 * j, 15,
                ALU.logical_shift_right, ALU.bitwise_and),
                waits=([evp, self.ev_sct] if j == 0 else []), inc=(j == 7))
        if w0 + nwords >= src_pk.shape[1]:
            st["pk_free"][sl] = evn
        w1 = V.add(lambda v, nh=nh: v.tensor_scalar(self.dq1[:, 0:nh], self.nib[:, 0:nh], 7, None,
                                                    ALU.bitwise_and), waits=[evn])
        w2 = V.add(lambda v, nh=nh: v.tensor_scalar(self.dq2[:, 0:nh], self.nib[:, 0:nh], 3, None,
                                                    ALU.logical_shift_right), waits=[w1])
        w3 = V.add(lambda v, nh=nh: v.tensor_scalar(self.dq3[:, 0:nh], self.dq1[:, 0:nh], 0x400000,
                                                    None, ALU.mult), waits=[w2])
        w4 = V.add(lambda v, nh=nh: v.tensor_scalar(self.nib[:, 0:nh], self.dq1[:, 0:nh], 1, None,
                                                    ALU.is_ge), waits=[w3])
        w5 = V.add(lambda v, nh=nh: v.scalar_tensor_tensor(self.dq3[:, 0:nh], self.nib[:, 0:nh],
                                                           float(0x3EC00000), self.dq3[:, 0:nh],
                                                           ALU.mult, ALU.add), waits=[w4])
        w6 = V.add(lambda v, nh=nh: v.tensor_scalar(self.nib[:, 0:nh], self.dq1[:, 0:nh], 2, None,
                                                    ALU.is_ge), waits=[w5])
        w7 = V.add(lambda v, nh=nh: v.scalar_tensor_tensor(self.dq3[:, 0:nh], self.nib[:, 0:nh],
                                                           float(0x400000), self.dq3[:, 0:nh],
                                                           ALU.mult, ALU.add), waits=[w6])
        w8 = V.add(lambda v, nh=nh: v.scalar_tensor_tensor(self.dq3[:, 0:nh], self.dq2[:, 0:nh],
                                                           float(-2147483648), self.dq3[:, 0:nh],
                                                           ALU.mult, ALU.add), waits=[w7])
        evw = V.add(lambda v, scoff=scoff, nsc=nsc, nh=nh: v.tensor_tensor(
            self.wbf[:, 0:nh].rearrange("p (g n) -> p g n", n=128),
            self.dq3[:, 0:nh].bitcast(F32).rearrange("p (g n) -> p g n", n=128),
            bcast_last(self.sct[:, scoff:scoff + nsc].rearrange("p (g o) -> p g o", o=1), 128),
            ALU.mult), waits=[w8, st["wbf_free"]])
        last_tr = None
        g0 = 0
        while g0 < nsc:
            cnt = min(8, nsc - g0)
            pb = (g0 // 8) % 2
            evt = None
            for k in range(cnt):
                gi = g0 + k
                evt = Tn.add(lambda t, pb=pb, k=k, gi=gi: t.transpose(
                    self.psd[pb][:, 128 * k:128 * (k + 1)], self.wbf[:, 128 * gi:128 * (gi + 1)],
                    id_bf[:]),
                    waits=([evw, st["psd_free"][pb]] if k == 0 else []),
                    inc=(k == cnt - 1))
            last_tr = evt
            sg = st["nstg"] % 2
            st["nstg"] += 1
            evc = S.add(lambda s, pb=pb, sg=sg, cnt=cnt: s.copy(
                self.stg4[sg][:, 0:128 * cnt], self.psd[pb][:, 0:128 * cnt]),
                waits=[evt, st["stg4_free"][sg]])
            st["psd_free"][pb] = evc
            ev_last = Q.add(lambda q, g0=g0, cnt=cnt, ot=ot, sg=sg, dstT=dstT,
                            rowbase=rowbase: q.dma_start(
                out=dstT[rowbase + 128 * g0:rowbase + 128 * (g0 + cnt),
                         128 * ot:128 * (ot + 1)].rearrange("(a p) n -> p a n", p=128),
                in_=self.stg4[sg][:, 0:128 * cnt].rearrange("p (a n) -> p a n", a=cnt)),
                waits=[evc], slot=f"dqst{sg}")
            st["stg4_free"][sg] = ev_last
            g0 += cnt
        st["wbf_free"] = last_tr



def p4_emit(nc, p, es, env):
    Tn, V, S, Q, G = p.T, p.V, p.S, p.Q, p.G
    x3s, xln4Ts, out = env["x3s"], env["xln4Ts"], env["out"]
    gup_pk, gup_sc, dn_pk, dn_sc = env["gup_pk"], env["gup_sc"], env["dn_pk"], env["dn_sc"]
    wguTs, wdnTs = env["wguTs"], env["wdnTs"]
    id_bf, id_f = env["id_bf"], env["id_f"]


    with ExitStack() as ph:
        xlnTb = [ph.enter_context(nc.sbuf_tensor(f"p4_xlnT{i}", [128, 16 * 512], BF16)) for i in range(2)]
        wgu_sb = [ph.enter_context(nc.sbuf_tensor(f"p4_wgu{i}", [128, 16 * 128], BF16)) for i in range(4)]
        wdn_sb = [ph.enter_context(nc.sbuf_tensor(f"p4_wdn{i}", [128, 44 * 128], BF16)) for i in range(2)]
        gate_sb = ph.enter_context(nc.sbuf_tensor("p4_gate", [128, 512], BF16))
        fuse = ph.enter_context(nc.sbuf_tensor("p4_fuse", [128, 44 * 512], BF16))
        yT = ph.enter_context(nc.sbuf_tensor("p4_yT", [128, 16 * 512], F32))
        ynat = ph.enter_context(nc.sbuf_tensor("p4_ynat", [128, 4 * 2048], F32))
        ysq = ph.enter_context(nc.sbuf_tensor("p4_ysq", [128, 2048], F32))
        x3b = [ph.enter_context(nc.sbuf_tensor(f"p4_x3b{i}", [128, D], F32)) for i in range(2)]
        rms = ph.enter_context(nc.sbuf_tensor("p4_rms", [128, 4], F32))
        psG = [ph.enter_context(nc.psum_tensor(f"p4_psG{i}", [128, 512], F32)) for i in range(2)]
        psU = [ph.enter_context(nc.psum_tensor(f"p4_psU{i}", [128, 512], F32)) for i in range(2)]
        psY = [ph.enter_context(nc.psum_tensor(f"p4_psY{i}", [128, 512], F32)) for i in range(2)]
        psN = [ph.enter_context(nc.psum_tensor(f"p4_psN{i}", [128, 512], F32)) for i in range(2)]

        xlnT_free = [None, None]
        wgu_free = [None] * 4
        wdn_free = [None, None]
        psG_free = [None, None]
        psU_free = [None, None]
        psY_free = [None, None]
        psN_free = [None, None]
        gate_free = None
        fuse_free = None
        yT_free = None
        ynat_free = None
        out_free = [None, None]
        x3_free = [None, None]
        ev_mm = None

        def load_xln(c):
            sl = c % 2
            return Q.add(lambda q, c=c, sl=sl: q.dma_start(
                out=xlnTb[sl].ap().rearrange("p (a n) -> p a n", a=16),
                in_=xln4Ts[:, 512 * c:512 * (c + 1)].rearrange("(a p) n -> p a n", p=128)),
                waits=[xlnT_free[sl]], slot=f"p4x{sl}")

        def load_wgu(j, slot, which):
            ot = j if which == 0 else j + NI1
            return Q.add(lambda q, ot=ot, slot=slot: q.dma_start(
                out=wgu_sb[slot].ap().rearrange("p (a n) -> p a n", a=16),
                in_=wguTs[:, 128 * ot:128 * (ot + 1)].rearrange("(a p) n -> p a n", p=128)),
                waits=[wgu_free[slot]], slot=f"wgu{slot}")

        def load_wdn(ht, slot):
            return G.add(lambda q, ht=ht, slot=slot: q.dma_start(
                out=wdn_sb[slot].ap().rearrange("p (a n) -> p a n", a=44),
                in_=wdnTs[:, 128 * ht:128 * (ht + 1)].rearrange("(a p) n -> p a n", p=128)),
                waits=[wdn_free[slot]], slot=f"wdn{slot}")

        ev_xln = [None] * 8
        ev_xln[0] = load_xln(0)
        for c in range(8):
            if c + 1 < 8:
                ev_xln[c + 1] = load_xln(c + 1)
            sl = c % 2
            evw_g = [None] * NI1
            evw_u = [None] * NI1
            evw_g[0] = load_wgu(0, 0, 0)
            evw_u[0] = load_wgu(0, 1, 1)
            ev_fuse_last = None
            for j in range(NI1):
                if j + 1 < NI1:
                    evw_g[j + 1] = load_wgu(j + 1, (2 * (j + 1)) % 4, 0)
                    evw_u[j + 1] = load_wgu(j + 1, (2 * (j + 1) + 1) % 4, 1)
                gslot = (2 * j) % 4
                uslot = (2 * j + 1) % 4
                pb = j % 2
                ev_mmg = None
                for dt in range(DT):
                    ev_mmg = Tn.add(lambda t, pb=pb, gslot=gslot, dt=dt, sl=sl: t.matmul(
                        psG[pb][:], wgu_sb[gslot][:, 128 * dt:128 * (dt + 1)],
                        xlnTb[sl][:, 512 * dt:512 * (dt + 1)],
                        start=(dt == 0), stop=(dt == DT - 1)),
                        waits=([evw_g[j], ev_xln[c], psG_free[pb]] if dt == 0 else []),
                        inc=(dt == DT - 1))
                wgu_free[gslot] = ev_mmg
                ev_mmu = None
                for dt in range(DT):
                    ev_mmu = Tn.add(lambda t, pb=pb, uslot=uslot, dt=dt, sl=sl: t.matmul(
                        psU[pb][:], wgu_sb[uslot][:, 128 * dt:128 * (dt + 1)],
                        xlnTb[sl][:, 512 * dt:512 * (dt + 1)],
                        start=(dt == 0), stop=(dt == DT - 1)),
                        waits=([evw_u[j], psU_free[pb]] if dt == 0 else []),
                        inc=(dt == DT - 1))
                wgu_free[uslot] = ev_mmu
                ev_si = S.add(lambda s, pb=pb: s.activation(gate_sb[:], psG[pb][:], AF.Silu),
                              waits=[ev_mmg, gate_free])
                psG_free[pb] = ev_si
                ev_fu = V.add(lambda v, pb=pb, j=j: v.tensor_tensor(
                    fuse[:, 512 * j:512 * (j + 1)], psU[pb][:], gate_sb[:], ALU.mult),
                    waits=[ev_si, ev_mmu, fuse_free])
                psU_free[pb] = ev_fu
                gate_free = ev_fu
                ev_fuse_last = ev_fu
            xlnT_free[sl] = ev_mmu
            evw_d = [None] * 16
            evw_d[0] = load_wdn(0, 0)
            ev_yT_all = None
            for ht in range(16):
                if ht + 1 < 16:
                    evw_d[ht + 1] = load_wdn(ht + 1, (ht + 1) % 2)
                wslot = ht % 2
                pb = ht % 2
                for it in range(NI1):
                    ev_mm = Tn.add(lambda t, pb=pb, wslot=wslot, it=it: t.matmul(
                        psY[pb][:], wdn_sb[wslot][:, 128 * it:128 * (it + 1)],
                        fuse[:, 512 * it:512 * (it + 1)],
                        start=(it == 0), stop=(it == NI1 - 1)),
                        waits=([evw_d[ht], ev_fuse_last, psY_free[pb]] if it == 0 else []),
                        inc=(it == NI1 - 1))
                wdn_free[wslot] = ev_mm
                ev_yT_all = V.add(lambda v, pb=pb, ht=ht: v.tensor_copy(
                    yT[:, 512 * ht:512 * (ht + 1)], psY[pb][:]), waits=[ev_mm, yT_free])
                psY_free[pb] = ev_yT_all
            fuse_free = ev_mm
            last_tr = None
            ev_yn = None
            for ht in range(16):
                pb = ht % 2
                evt = None
                for ts in range(4):
                    evt = Tn.add(lambda t, pb=pb, ht=ht, ts=ts: t.transpose(
                        psN[pb][:, 128 * ts:128 * (ts + 1)],
                        yT[:, 512 * ht + 128 * ts: 512 * ht + 128 * (ts + 1)],
                        id_f[:]), waits=([ev_yT_all, psN_free[pb]] if ts == 0 else []),
                        inc=(ts == 3))
                last_tr = evt
                ev_yn = V.add(lambda v, pb=pb, ht=ht: v.tensor_copy(
                    ynat.ap().rearrange("p (s h) -> p s h", s=4)[:, :, 128 * ht:128 * (ht + 1)],
                    psN[pb].ap().rearrange("p (s h) -> p s h", s=4)),
                    waits=[evt, out_free[0], out_free[1]])
                psN_free[pb] = ev_yn
            yT_free = last_tr
            for ts in range(4):
                ti = 4 * c + ts
                bs = ti % 2
                ev_x3 = G.add(lambda q, ti=ti, bs=bs: q.dma_start(
                    out=x3b[bs][:], in_=x3s[128 * ti:128 * (ti + 1), :]), waits=[x3_free[bs]],
                    slot=f"x3b{bs}")
                ev_sq = V.add(lambda v, ts=ts: v.tensor_mul(
                    ysq[:], ynat[:, 2048 * ts:2048 * (ts + 1)], ynat[:, 2048 * ts:2048 * (ts + 1)]),
                    waits=[ev_yn])
                ev_rs = V.add(lambda v, ts=ts: v.reduce_sum(rms[:, ts:ts + 1], ysq[:], axis=AX.X),
                              waits=[ev_sq])
                ev_l = S.add(lambda s, ts=ts: s.activation(rms[:, ts:ts + 1], rms[:, ts:ts + 1],
                                                           AF.Ln, scale=1.0 / D, bias=env["epsc"][:, 2:3]),
                             waits=[ev_rs])
                ev_e = S.add(lambda s, ts=ts: s.activation(rms[:, ts:ts + 1], rms[:, ts:ts + 1],
                                                           AF.Exp, scale=-0.5), waits=[ev_l])
                ev_o = V.add(lambda v, ts=ts, bs=bs: v.scalar_tensor_tensor(
                    ynat[:, 2048 * ts:2048 * (ts + 1)], ynat[:, 2048 * ts:2048 * (ts + 1)],
                    rms[:, ts:ts + 1], x3b[bs][:], ALU.mult, ALU.add),
                    waits=[ev_e, ev_x3])
                x3_free[bs] = ev_o
                ev_so = G.add(lambda q, ti=ti, ts=ts, bs=bs: q.dma_start(
                    out=out[128 * ti:128 * (ti + 1), :], in_=ynat[:, 2048 * ts:2048 * (ts + 1)]),
                    waits=[ev_o], slot=f"outst{bs}")
                out_free[bs] = ev_so
            ynat_free = ev_so



def make_consts():
    idbf = np.eye(128, dtype=ml_dtypes.bfloat16)
    idf = np.eye(128, dtype=np.float32)
    mask = np.where(np.arange(128)[None, :] > np.arange(128)[:, None],
                    np.float32(NEG_INF), np.float32(0.0)).astype(np.float32)
    u128 = np.triu(np.ones((128, 128), dtype=np.float32))
    iota0 = np.arange(128, dtype=np.float32).reshape(1, 128)
    iota1 = np.arange(1, 129, dtype=np.float32).reshape(1, 128)
    ones = np.ones((1, 128), dtype=np.float32)
    e127 = np.zeros((128, 1), dtype=np.float32)
    e127[127, 0] = 1.0
    eps = np.tile(np.array([[1e-5, 1.0, 1e-6, 0.0]], dtype=np.float32), (128, 1))
    return {"c_idbf": idbf, "c_idf": idf, "c_mask": mask, "c_u128": u128,
            "c_iota0": iota0, "c_iota1": iota1, "c_ones": ones, "c_e127": e127, "c_eps": eps}


def make_in_map(inputs, b):
    m = {
        "x_in": np.ascontiguousarray(inputs["L0_attn_x"][b]),
        "wq": inputs["L0_attn_Wq"], "wk": inputs["L0_attn_Wk"], "wv": inputs["L0_attn_Wv"],
        "wg8": inputs["L0_ffn_w_gate"], "wu8": inputs["L0_ffn_w_up"], "wd8": inputs["L0_ffn_w_down"],
        "sgate": inputs["L0_ffn_s_gate"], "sup": inputs["L0_ffn_s_up"], "sdown": inputs["L0_ffn_s_down"],
        "lre": np.asarray(inputs["L1_attn_Lambda_unconstrained_re"]).reshape(1, 32),
        "lim": np.asarray(inputs["L1_attn_Lambda_im"]).reshape(1, 32),
        "bre": inputs["L1_attn_B_re"], "bim": inputs["L1_attn_B_im"],
        "cre": inputs["L1_attn_C_re"], "cim": inputs["L1_attn_C_im"],
        "dvec": np.asarray(inputs["L1_attn_D"]).reshape(1, D),
        "logdelta": np.asarray(inputs["L1_attn_log_Delta"]).reshape(1, 64),
        "gup_pk": inputs["L1_ffn_gate_up_packed"], "gup_sc": inputs["L1_ffn_gate_up_scales"],
        "dn_pk": inputs["L1_ffn_down_packed"], "dn_sc": inputs["L1_ffn_down_scales"],
    }
    m = {k: np.asarray(v) for k, v in m.items()}
    m.update(make_consts())
    return m


def kernel(**inputs):
    nc = build_nc("P4")
    in_maps = [make_in_map(inputs, b) for b in range(8)]
    res = run_bass_kernel_spmd(nc, in_maps, core_ids=list(range(8)))
    return np.stack([res.results[b]["out"] for b in range(8)], axis=0)


# version 08
# speedup vs baseline: 1.0096x; 1.0096x over previous
import sys
import numpy as np
from contextlib import ExitStack

try:
    import concourse.bass as bass
except ImportError:
    sys.path.insert(0, "/opt/trn_rl_repo")
    import concourse.bass as bass
import ml_dtypes
import concourse.mybir as mybir
from concourse.bass_utils import run_bass_kernel_spmd
from concourse.ap import AP as APcls


def bcast_last(ap3, count):
    return APcls(tensor=ap3.tensor, offset=ap3.offset,
                 ap=[list(q) for q in ap3.ap[:-1]] + [[0, count]])

F32 = mybir.dt.float32
BF16 = mybir.dt.bfloat16
I8 = mybir.dt.int8
I32 = mybir.dt.int32
F16 = mybir.dt.float16
AF = mybir.ActivationFunctionType
ALU = mybir.AluOpType
AX = mybir.AxisListType

T = 4096
D = 2048
DT = 16
NEG_INF = -1e30
ATT_SCALE = 1.0 / float(np.sqrt(2048.0))
I_FFN0 = 8192
I_FFN1 = 5632
NI1 = I_FFN1 // 128
NO1 = 2 * I_FFN1 // 128
PI = float(np.pi)



class E:

    NRING = 8

    def __init__(self, plan, key, semname, step):
        self.plan = plan
        self.key = key
        self.semname = semname
        self.step = step
        self.is_dma = step == 16
        if self.is_dma:
            self.ringnames = [f"{semname}{i}" for i in range(self.NRING)]
            self.ringvals = [0] * self.NRING
            self.ndma = 0
            self.slotmap = {}
        self.n = 0
        self.marks = {}
        self.pending = []

    def add(self, fn, waits=(), inc=True, slot=None):
        todo = []
        for ev in list(self.pending) + list(waits):
            if ev is None:
                continue
            s, v = ev
            if v <= 0 or self.marks.get(s, 0) >= v:
                continue
            self.marks[s] = v
            todo.append((s, v))
        self.pending = []
        plan = self.plan
        myev = None
        if inc and self.is_dma:
            if slot is not None:
                if slot not in self.slotmap:
                    self.slotmap[slot] = [f"{self.semname}s{len(self.slotmap)}", 0]
                ent = self.slotmap[slot]
                semname = ent[0]
                ent[1] += 16
                myev = (semname, ent[1])
            else:
                rslot = self.ndma % self.NRING
                self.ndma += 1
                semname = self.ringnames[rslot]
                prev = self.ringvals[rslot]
                if prev > 0 and self.marks.get(semname, 0) < prev:
                    todo.append((semname, prev))
                    self.marks[semname] = prev
                self.ringvals[rslot] = prev + 16
                myev = (semname, prev + 16)

            def cl(q):
                for s, v in todo:
                    q.wait_ge(plan.sems[s], v)
                fn(q).then_inc(plan.sems[semname], 16)
        elif inc:
            self.n += self.step
            myev = (self.semname, self.n)
            semname = self.semname
            step = self.step

            def cl(q):
                for s, v in todo:
                    q.wait_ge(plan.sems[s], v)
                fn(q).then_inc(plan.sems[semname], step)
        else:
            def cl(q):
                for s, v in todo:
                    q.wait_ge(plan.sems[s], v)
                fn(q)
        plan.streams[self.key].append(cl)
        return myev

    def now(self):
        if self.is_dma:
            return None
        return (self.semname, self.n)

    def all_events(self):
        if self.is_dma:
            evs = [(self.ringnames[i], self.ringvals[i]) for i in range(self.NRING)
                   if self.ringvals[i] > 0]
            evs += [(ent[0], ent[1]) for ent in self.slotmap.values() if ent[1] > 0]
            return evs
        return [(self.semname, self.n)] if self.n > 0 else []


class Plan:
    def __init__(self):
        self.streams = {k: [] for k in ("sync", "gpsimd", "tensor", "vector", "scalar")}
        self.sems = {}
        self.T = E(self, "tensor", "sT", 1)
        self.V = E(self, "vector", "sV", 1)
        self.S = E(self, "scalar", "sS", 1)
        self.Q = E(self, "sync", "sQ", 16)
        self.G = E(self, "gpsimd", "sG", 16)
        self.A = E(self, "scalar", "sA", 16)
        self.all = [self.T, self.V, self.S, self.Q, self.G, self.A]

    def barrier(self):
        evs = []
        for e in self.all:
            evs.extend(e.all_events())
        for e in self.all:
            e.pending.extend(evs)

    def emit(self, nc, es):
        for name in ("sT", "sV", "sS"):
            self.sems[name] = es.enter_context(nc.semaphore(name))
        for base in ("sQ", "sG", "sA"):
            for i in range(E.NRING):
                name = f"{base}{i}"
                self.sems[name] = es.enter_context(nc.semaphore(name))
        for e in (self.Q, self.G, self.A):
            for ent in e.slotmap.values():
                self.sems[ent[0]] = es.enter_context(nc.semaphore(ent[0]))
        block = es.enter_context(nc.Block())

        @block.sync
        def _(q):
            for cl in self.streams["sync"]:
                cl(q)

        @block.gpsimd
        def _(q):
            for cl in self.streams["gpsimd"]:
                cl(q)

        @block.tensor
        def _(q):
            for cl in self.streams["tensor"]:
                cl(q)

        @block.vector
        def _(q):
            for cl in self.streams["vector"]:
                cl(q)

        @block.scalar
        def _(q):
            for cl in self.streams["scalar"]:
                cl(q)


def ln_tile(p, xt, out_bf, tmp, waits=()):
    V, S = p.V, p.S
    ev = None
    for k in range(4):
        ev = V.add(lambda v, k=k: v.bn_stats(tmp["bn"][:, 6 * k:6 * k + 6],
                                             xt[:, 512 * k:512 * (k + 1)]),
                   waits=(waits if k == 0 else ()), inc=(k == 3))
    ev = V.add(lambda v: v.bn_aggr(tmp["mv"][:], tmp["bn"][:]), waits=[ev])
    e1 = S.add(lambda s: s.activation(tmp["r1"][:], tmp["mv"][:, 1:2], AF.Ln, bias=tmp["eps"]),
               waits=[ev])
    e2 = S.add(lambda s: s.activation(tmp["r2"][:], tmp["r1"][:], AF.Exp, scale=-0.5),
               waits=[e1])
    e3 = V.add(lambda v: v.scalar_tensor_tensor(tmp["nmr"][:], tmp["mv"][:, 0:1], -1.0,
                                                tmp["r2"][:], ALU.mult, ALU.mult),
               waits=[e2])
    e4 = V.add(lambda v: v.tensor_scalar(out_bf[:], xt[:], tmp["r2"][:, 0:1],
                                         tmp["nmr"][:, 0:1], ALU.mult, ALU.add),
               waits=[e3])
    return e4


def mk_ln_tmp(nc, es, tag, eps_ap=None):
    return {
        "bn": es.enter_context(nc.sbuf_tensor(f"{tag}_bn", [128, 24], F32)),
        "mv": es.enter_context(nc.sbuf_tensor(f"{tag}_mv", [128, 2], F32)),
        "r1": es.enter_context(nc.sbuf_tensor(f"{tag}_r1", [128, 1], F32)),
        "r2": es.enter_context(nc.sbuf_tensor(f"{tag}_r2", [128, 1], F32)),
        "nmr": es.enter_context(nc.sbuf_tensor(f"{tag}_nmr", [128, 1], F32)),
        "eps": eps_ap,
    }



def build_nc(upto="P4", start="P1", p3sub=None):
    nc = bass.Bass("TRN2", target_bir_lowering=False, debug=False)
    p = Plan()
    es = ExitStack()

    x_in = nc.dram_tensor("x_in", [T, D], F32, kind="ExternalInput").ap()
    wq = nc.dram_tensor("wq", [D, D], F32, kind="ExternalInput").ap()
    wk = nc.dram_tensor("wk", [D, D], F32, kind="ExternalInput").ap()
    wv = nc.dram_tensor("wv", [D, D], F32, kind="ExternalInput").ap()
    wg8 = nc.dram_tensor("wg8", [D, I_FFN0], I8, kind="ExternalInput").ap()
    wu8 = nc.dram_tensor("wu8", [D, I_FFN0], I8, kind="ExternalInput").ap()
    wd8 = nc.dram_tensor("wd8", [I_FFN0, D], I8, kind="ExternalInput").ap()
    sgate = nc.dram_tensor("sgate", [I_FFN0], F32, kind="ExternalInput").ap()
    sup = nc.dram_tensor("sup", [I_FFN0], F32, kind="ExternalInput").ap()
    sdown = nc.dram_tensor("sdown", [D], F32, kind="ExternalInput").ap()
    lre = nc.dram_tensor("lre", [1, 32], F32, kind="ExternalInput").ap()
    lim = nc.dram_tensor("lim", [1, 32], F32, kind="ExternalInput").ap()
    bre = nc.dram_tensor("bre", [32, D], F32, kind="ExternalInput").ap()
    bim = nc.dram_tensor("bim", [32, D], F32, kind="ExternalInput").ap()
    cre = nc.dram_tensor("cre", [D, 32], F32, kind="ExternalInput").ap()
    cim = nc.dram_tensor("cim", [D, 32], F32, kind="ExternalInput").ap()
    dvec = nc.dram_tensor("dvec", [1, D], F32, kind="ExternalInput").ap()
    logdelta = nc.dram_tensor("logdelta", [1, 64], F32, kind="ExternalInput").ap()
    gup_pk = nc.dram_tensor("gup_pk", [2 * I_FFN1, 256], I32, kind="ExternalInput").ap()
    gup_sc = nc.dram_tensor("gup_sc", [16, 2 * I_FFN1], F16, kind="ExternalInput").ap()
    dn_pk = nc.dram_tensor("dn_pk", [D, 704], I32, kind="ExternalInput").ap()
    dn_sc = nc.dram_tensor("dn_sc", [44, D], F16, kind="ExternalInput").ap()
    c_idbf = nc.dram_tensor("c_idbf", [128, 128], BF16, kind="ExternalInput").ap()
    c_idf = nc.dram_tensor("c_idf", [128, 128], F32, kind="ExternalInput").ap()
    c_mask = nc.dram_tensor("c_mask", [128, 128], F32, kind="ExternalInput").ap()
    c_u128 = nc.dram_tensor("c_u128", [128, 128], F32, kind="ExternalInput").ap()
    c_iota0 = nc.dram_tensor("c_iota0", [1, 128], F32, kind="ExternalInput").ap()
    c_iota1 = nc.dram_tensor("c_iota1", [1, 128], F32, kind="ExternalInput").ap()
    c_ones = nc.dram_tensor("c_ones", [1, 128], F32, kind="ExternalInput").ap()
    c_e127 = nc.dram_tensor("c_e127", [128, 1], F32, kind="ExternalInput").ap()
    c_eps = nc.dram_tensor("c_eps", [128, 4], F32, kind="ExternalInput").ap()

    out = nc.dram_tensor("out", [T, D], F32, kind="ExternalOutput").ap()

    qTs = nc.dram_tensor("qTs", [D, T], BF16, kind="Internal").ap()
    kTs = nc.dram_tensor("kTs", [D, T], BF16, kind="Internal").ap()
    vSs = nc.dram_tensor("vSs", [T, D], BF16, kind="Internal").ap()
    x1s = nc.dram_tensor("x1s", [T, D], F32, kind="Internal").ap()
    x2s = nc.dram_tensor("x2s", [T, D], F32, kind="ExternalInput" if start == "P3" else "Internal").ap()
    x3s = nc.dram_tensor("x3s", [T, D], F32, kind="Internal" if start != "P4" else "ExternalInput").ap()
    ulnSc = nc.dram_tensor("ulnSc", [T, D], BF16, kind="Internal").ap()
    xln2Ts = nc.dram_tensor("xln2Ts", [D, T], BF16, kind="Internal").ap()
    xln4Ts = nc.dram_tensor("xln4Ts", [D, T], BF16, kind="Internal" if start != "P4" else "ExternalInput").ap()
    wguTs = nc.dram_tensor("wguTs", [D, 2 * I_FFN1], BF16, kind="Internal").ap()
    wdnTs = nc.dram_tensor("wdnTs", [I_FFN1, D], BF16, kind="Internal").ap()

    Tn, V, S, Q, G = p.T, p.V, p.S, p.Q, p.G

    id_bf = es.enter_context(nc.sbuf_tensor("id_bf", [128, 128], BF16))
    id_f = es.enter_context(nc.sbuf_tensor("id_f", [128, 128], F32))
    msk = es.enter_context(nc.sbuf_tensor("msk", [128, 128], F32))
    u128 = es.enter_context(nc.sbuf_tensor("u128", [128, 128], F32))
    iota0 = es.enter_context(nc.sbuf_tensor("iota0", [1, 128], F32))
    iota1 = es.enter_context(nc.sbuf_tensor("iota1", [1, 128], F32))
    ones1 = es.enter_context(nc.sbuf_tensor("ones1", [1, 128], F32))
    e127 = es.enter_context(nc.sbuf_tensor("e127", [128, 1], F32))
    epsc = es.enter_context(nc.sbuf_tensor("epsc", [128, 4], F32))

    for dst, src in [(id_bf, c_idbf), (id_f, c_idf), (msk, c_mask), (u128, c_u128),
                     (iota0, c_iota0), (iota1, c_iota1), (ones1, c_ones), (e127, c_e127), (epsc, c_eps)]:
        Q.add(lambda q, d=dst, s=src: q.dma_start(out=d[:], in_=s[:]))
    p.barrier()

    env = dict(locals())

    if start == "P1":
        p1a_emit(nc, p, es, env)
        p1b_emit(nc, p, es, env)
        if upto == "P1":
            return finish_partial(nc, p, es, out, x1s)

        p2_emit(nc, p, es, env)
        if upto == "P2":
            return finish_partial(nc, p, es, out, x2s)

    if start != "P4":
        p3_emit(nc, p, es, env, p3sub)
        if p3sub is not None:
            return finish_partial(nc, p, es, out, x2s)
        if upto == "P3":
            return finish_partial(nc, p, es, out, x3s)
    with ExitStack() as dqes:
        dq = DequantPump(nc, p, dqes, env)
        dq.drain()
        p.barrier()

    p4_emit(nc, p, es, env)
    p.barrier()
    p.Q.add(lambda q: q.nop(), inc=False)
    p.emit(nc, es)
    es.close()
    return nc


def finish_partial(nc, p, es, out, src):
    fin = p.Q.add(lambda q: q.dma_start(out=out[:], in_=src[:]))
    p.barrier()
    p.Q.add(lambda q: q.nop(), inc=False)
    p.emit(nc, es)
    es.close()
    return nc



def p1a_emit(nc, p, es, env):
    Tn, V, S, Q, G = p.T, p.V, p.S, p.Q, p.G
    x_in, wq, wk, wv = env["x_in"], env["wq"], env["wk"], env["wv"]
    qTs, kTs, vSs = env["qTs"], env["kTs"], env["vSs"]
    id_bf = env["id_bf"]
    CH = 1024
    NCH = T // CH
    with ExitStack() as ph:
        xbuf = ph.enter_context(nc.sbuf_tensor("p1_x", [128, 8 * D], F32))
        xlnT = [ph.enter_context(nc.sbuf_tensor(f"p1_xlnT{i}", [128, 16 * CH], BF16)) for i in range(2)]
        xln_bf = ph.enter_context(nc.sbuf_tensor("p1_xln", [128, D], BF16))
        wqk_sb = [ph.enter_context(nc.sbuf_tensor(f"p1_wqk{i}", [128, 16 * 128], BF16)) for i in range(3)]
        wvb_sb = [ph.enter_context(nc.sbuf_tensor(f"p1_wv{i}", [128, 16 * 512], BF16)) for i in range(2)]
        stg = [ph.enter_context(nc.sbuf_tensor(f"p1_stg{i}", [128, 512], BF16)) for i in range(4)]
        lt = mk_ln_tmp(nc, ph, "p1", env["epsc"][:, 0:1])
        ptr = [ph.enter_context(nc.psum_tensor(f"p1_ptr{i}", [128, 1024], BF16)) for i in range(2)]
        pqk = [ph.enter_context(nc.psum_tensor(f"p1_pqk{i}", [128, 512], F32)) for i in range(4)]

        xbuf_free = None
        xlnT_free = [None, None]
        w_free = [None, None, None]
        wv_free = [None, None]
        stg_free = [None] * 4
        ptr_free = [None, None]
        pqk_free = [None] * 4
        x_ld = [None] * NCH
        nstg = 0
        nptr = 0
        npqk = 0
        nw = 0

        x_ld[0] = Q.add(lambda q: q.dma_start(
            out=xbuf.ap().rearrange("p (a d) -> p a d", a=8),
            in_=x_in[0:CH, :].rearrange("(a p) d -> p a d", p=128)), slot="p1x")
        for c in range(NCH):
            sl = c % 2
            ev_tr_last = None
            ev_ln_last = None
            prev_tr = None
            for a in range(8):
                xt = xbuf[:, D * a:D * (a + 1)]
                V.pending.append(x_ld[c])
                if prev_tr is not None:
                    V.pending.append(prev_tr)
                ev_ln = ln_tile(p, xt, xln_bf, lt)
                ev_ln_last = ev_ln
                for half in range(2):
                    pb = nptr % 2
                    nptr += 1
                    evt = None
                    for k in range(8):
                        dt = 8 * half + k
                        evt = Tn.add(lambda t, pb=pb, k=k, dt=dt: t.transpose(
                            ptr[pb][:, 128 * k:128 * (k + 1)],
                            xln_bf[:, 128 * dt:128 * (dt + 1)], id_bf[:]),
                            waits=([ev_ln, ptr_free[pb]] if k == 0 else []),
                            inc=(k == 7))
                        prev_tr = evt
                    ev_tr_last = S.add(lambda s, pb=pb, a=a, half=half, sl=sl: s.copy(
                        xlnT[sl].ap().rearrange("p (t c) -> p t c", t=16)[:, 8 * half:8 * half + 8,
                                                                         128 * a:128 * (a + 1)],
                        ptr[pb].ap().rearrange("p (t c) -> p t c", t=8)),
                        waits=[evt, xlnT_free[sl]])
                    ptr_free[pb] = ev_tr_last
            xbuf_free = ev_ln_last
            if c + 1 < NCH:
                x_ld[c + 1] = Q.add(lambda q, c=c: q.dma_start(
                    out=xbuf.ap().rearrange("p (a d) -> p a d", a=8),
                    in_=x_in[CH * (c + 1):CH * (c + 2), :].rearrange("(a p) d -> p a d", p=128)),
                    waits=[xbuf_free], slot="p1x")
            ev_xlnT = ev_tr_last

            ev_mm = None
            for wmat, dstT in [(wq, qTs), (wk, kTs)]:
                for et in range(DT):
                    wsl = nw % 3
                    nw += 1
                    ev_w = G.add(lambda q, wmat=wmat, et=et, wsl=wsl: q.dma_start(
                        out=wqk_sb[wsl].ap().rearrange("p (a n) -> p a n", a=16),
                        in_=wmat[:, 128 * et:128 * (et + 1)].rearrange("(a p) n -> p a n", p=128)),
                        waits=[w_free[wsl]], slot=f"wqk{wsl}")
                    for th in range(2):
                        pb = npqk % 4
                        npqk += 1
                        for dt in range(DT):
                            ev_mm = Tn.add(lambda t, pb=pb, wsl=wsl, dt=dt, sl=sl, th=th: t.matmul(
                                pqk[pb][:], wqk_sb[wsl][:, 128 * dt:128 * (dt + 1)],
                                xlnT[sl][:, CH * dt + 512 * th: CH * dt + 512 * (th + 1)],
                                start=(dt == 0), stop=(dt == DT - 1)),
                                waits=([ev_w, ev_xlnT, pqk_free[pb]] if dt == 0 else []),
                                inc=(dt == DT - 1))
                        sg = nstg % 4
                        nstg += 1
                        ev_ev = V.add(lambda v, sg=sg, pb=pb: v.tensor_copy(stg[sg][:], pqk[pb][:]),
                                      waits=[ev_mm, stg_free[sg]])
                        pqk_free[pb] = ev_ev
                        ev_st = Q.add(lambda q, sg=sg, et=et, c=c, th=th, dstT=dstT: q.dma_start(
                            out=dstT[128 * et:128 * (et + 1), CH * c + 512 * th: CH * c + 512 * (th + 1)],
                            in_=stg[sg][:]), waits=[ev_ev], slot=f"stg{sg}")
                        stg_free[sg] = ev_st
                    w_free[wsl] = ev_mm
            for nch in range(4):
                wsl = nch % 2
                ev_w = G.add(lambda q, nch=nch, wsl=wsl: q.dma_start(
                    out=wvb_sb[wsl].ap().rearrange("p (a n) -> p a n", a=16),
                    in_=wv[:, 512 * nch:512 * (nch + 1)].rearrange("(a p) n -> p a n", p=128)),
                    waits=[wv_free[wsl]], slot=f"wvb{wsl}")
                for a in range(8):
                    pb = npqk % 4
                    npqk += 1
                    for dt in range(DT):
                        ev_mm = Tn.add(lambda t, pb=pb, wsl=wsl, dt=dt, sl=sl, a=a: t.matmul(
                            pqk[pb][:], xlnT[sl][:, CH * dt + 128 * a: CH * dt + 128 * (a + 1)],
                            wvb_sb[wsl][:, 512 * dt:512 * (dt + 1)],
                            start=(dt == 0), stop=(dt == DT - 1)),
                            waits=([ev_w, ev_xlnT, pqk_free[pb]] if dt == 0 else []),
                            inc=(dt == DT - 1))
                    sg = nstg % 4
                    nstg += 1
                    ev_ev = V.add(lambda v, sg=sg, pb=pb: v.tensor_copy(stg[sg][:], pqk[pb][:]),
                                  waits=[ev_mm, stg_free[sg]])
                    pqk_free[pb] = ev_ev
                    ev_st = Q.add(lambda q, sg=sg, nch=nch, c=c, a=a: q.dma_start(
                        out=vSs[CH * c + 128 * a: CH * c + 128 * (a + 1), 512 * nch:512 * (nch + 1)],
                        in_=stg[sg][:]), waits=[ev_ev], slot=f"stg{sg}")
                    stg_free[sg] = ev_st
                wv_free[wsl] = ev_mm
            xlnT_free[sl] = ev_mm
        p.barrier()



def p1b_emit(nc, p, es, env):
    Tn, V, S, Q, G = p.T, p.V, p.S, p.Q, p.G
    x_in, qTs, kTs, vSs = env["x_in"], env["qTs"], env["kTs"], env["vSs"]
    x1s, xln2Ts = env["x1s"], env["xln2Ts"]
    id_bf, msk = env["id_bf"], env["msk"]
    with ExitStack() as ph:
        qTg = ph.enter_context(nc.sbuf_tensor("p1b_qTg", [128, 16 * 512], BF16))
        kv = [ph.enter_context(nc.sbuf_tensor(f"p1b_kv{i}", [128, 16 * 512], BF16)) for i in range(2)]
        Ssl = [ph.enter_context(nc.sbuf_tensor(f"p1b_S{j}", [128, T], F32)) for j in range(4)]
        Psl = [ph.enter_context(nc.sbuf_tensor(f"p1b_P{j}", [128, T], BF16)) for j in range(4)]
        Osl = [ph.enter_context(nc.sbuf_tensor(f"p1b_O{j}", [128, D], F32)) for j in range(4)]
        nmax = ph.enter_context(nc.sbuf_tensor("p1b_nmax", [128, 4], F32))
        rsum = ph.enter_context(nc.sbuf_tensor("p1b_rsum", [128, 4], F32))
        rinv = ph.enter_context(nc.sbuf_tensor("p1b_rinv", [128, 4], F32))
        pTt = [ph.enter_context(nc.sbuf_tensor(f"p1b_pT{i}", [128, 512], BF16)) for i in range(2)]
        xres = ph.enter_context(nc.sbuf_tensor("p1b_xres", [128, D], F32))
        x1t = ph.enter_context(nc.sbuf_tensor("p1b_x1t", [128, D], F32))
        xln2b = ph.enter_context(nc.sbuf_tensor("p1b_xln2", [128, D], BF16))
        stg2 = ph.enter_context(nc.sbuf_tensor("p1b_stg2", [128, 16 * 128], BF16))
        lt2 = mk_ln_tmp(nc, ph, "p1b", env["epsc"][:, 0:1])
        psS = [ph.enter_context(nc.psum_tensor(f"p1b_psS{i}", [128, 512], F32)) for i in range(2)]
        psO = ph.enter_context(nc.psum_tensor("p1b_psO", [128, 2048], F32))
        psT = [ph.enter_context(nc.psum_tensor(f"p1b_psT{i}", [128, 512], BF16)) for i in range(2)]

        kv_free = [None, None]
        psS_free = [None, None]
        psT_free = [None, None]
        pTt_free = [None, None]
        psO_free = None
        qTg_free = None
        stg2_free = None
        xres_free = None
        x1t_free_dma = None
        x1t_free_ln = None
        pT_last = [None] * 4
        nS = 0
        nT2 = 0
        nV = 0
        ev_mm = None

        for g in range(8):
            ev_q = Q.add(lambda q, g=g: q.dma_start(
                out=qTg.ap().rearrange("p (a n) -> p a n", a=16),
                in_=qTs[:, 512 * g:512 * (g + 1)].rearrange("(a p) n -> p a n", p=128)),
                waits=[qTg_free], slot="qTg")
            ev_S_done = [None] * 4
            for kb in range(g + 1):
                ksl = kb % 2
                ev_k = Q.add(lambda q, kb=kb, ksl=ksl: q.dma_start(
                    out=kv[ksl].ap().rearrange("p (a n) -> p a n", a=16),
                    in_=kTs[:, 512 * kb:512 * (kb + 1)].rearrange("(a p) n -> p a n", p=128)),
                    waits=[kv_free[ksl]], slot=f"kv{ksl}")
                for j in range(4):
                    width = 512 if kb < g else 128 * (j + 1)
                    pb = nS % 2
                    nS += 1
                    for dt in range(DT):
                        ev_mm = Tn.add(lambda t, pb=pb, dt=dt, j=j, ksl=ksl, width=width: t.matmul(
                            psS[pb][:, 0:width], qTg[:, 512 * dt + 128 * j: 512 * dt + 128 * (j + 1)],
                            kv[ksl][:, 512 * dt:512 * dt + width],
                            start=(dt == 0), stop=(dt == DT - 1)),
                            waits=([ev_q, ev_k, psS_free[pb]] if dt == 0 else []),
                            inc=(dt == DT - 1))
                    ev_ev = S.add(lambda s, j=j, kb=kb, pb=pb, width=width: s.activation(
                        Ssl[j][:, 512 * kb:512 * kb + width], psS[pb][:, 0:width],
                        AF.Copy, scale=ATT_SCALE), waits=[ev_mm])
                    psS_free[pb] = ev_ev
                    if kb == g:
                        ev_ev = V.add(lambda v, j=j, kb=kb: v.tensor_tensor(
                            Ssl[j][:, 512 * kb + 128 * j: 512 * kb + 128 * (j + 1)],
                            Ssl[j][:, 512 * kb + 128 * j: 512 * kb + 128 * (j + 1)],
                            msk[:], ALU.add), waits=[ev_ev])
                    ev_S_done[j] = ev_ev
                kv_free[ksl] = ev_mm
            ev_P = [None] * 4
            ev_rinv = [None] * 4
            for j in range(4):
                w = 512 * g + 128 * (j + 1)
                ev1 = V.add(lambda v, j=j, w=w: v.reduce_max(
                    nmax[:, j:j + 1], Ssl[j][:, 0:w], axis=AX.X, negate=True),
                    waits=[ev_S_done[j]])
                ev2 = S.add(lambda s, j=j, w=w: s.activation(
                    Psl[j][:, 0:w], Ssl[j][:, 0:w], AF.Exp, bias=nmax[:, j:j + 1],
                    accum_out=rsum[:, j:j + 1]), waits=[ev1, pT_last[j]])
                ev_P[j] = ev2
                ev_rinv[j] = V.add(lambda v, j=j: v.reciprocal(
                    rinv[:, j:j + 1], rsum[:, j:j + 1]), waits=[ev2])
            ev_O = [None] * 4
            for kb in range(g + 1):
                vsl = nV % 2
                nV += 1
                ev_v = Q.add(lambda q, kb=kb, vsl=vsl: q.dma_start(
                    out=kv[vsl].ap().rearrange("p (a d) -> p a d", a=4),
                    in_=vSs[512 * kb:512 * (kb + 1), :].rearrange("(a p) d -> p a d", p=128)),
                    waits=[kv_free[vsl]], slot=f"kv{vsl}")
                for j in range(4):
                    nkt = 4 if kb < g else j + 1
                    pb = nT2 % 2
                    nT2 += 1
                    evt = None
                    for kt in range(nkt):
                        evt = Tn.add(lambda t, pb=pb, j=j, kb=kb, kt=kt: t.transpose(
                            psT[pb][:, 128 * kt:128 * (kt + 1)],
                            Psl[j][:, 512 * kb + 128 * kt: 512 * kb + 128 * (kt + 1)],
                            id_bf[:]), waits=([ev_P[j], psT_free[pb]] if kt == 0 else []),
                            inc=(kt == nkt - 1))
                    pT_last[j] = evt
                    ev_pc = S.add(lambda s, pb=pb, nkt=nkt: s.copy(
                        pTt[pb][:, 0:128 * nkt], psT[pb][:, 0:128 * nkt]),
                        waits=[evt, pTt_free[pb]])
                    psT_free[pb] = ev_pc
                    for kt in range(nkt):
                        for nchi in range(4):
                            w = []
                            if kt == 0 and nchi == 0:
                                w = [ev_pc, ev_v, psO_free]
                            ev_mm = Tn.add(lambda t, pb=pb, vsl=vsl, kt=kt, nchi=nchi, nkt=nkt: t.matmul(
                                psO[:, 512 * nchi:512 * (nchi + 1)], pTt[pb][:, 128 * kt:128 * (kt + 1)],
                                kv[vsl][:, D * kt + 512 * nchi: D * kt + 512 * (nchi + 1)],
                                start=(kt == 0), stop=(kt == nkt - 1)),
                                waits=w, inc=(kt == nkt - 1 and nchi == 3))
                    pTt_free[pb] = ev_mm
                    if kb == 0:
                        ev_O[j] = V.add(lambda v, j=j: v.tensor_copy(Osl[j][:], psO[:]),
                                        waits=[ev_mm])
                    else:
                        ev_O[j] = V.add(lambda v, j=j: v.tensor_add(Osl[j][:], Osl[j][:], psO[:]),
                                        waits=[ev_mm, ev_O[j]])
                    psO_free = ev_O[j]
                kv_free[vsl] = ev_mm
            prev_tr2 = None
            for j in range(4):
                qi = 4 * g + j
                ev_x = G.add(lambda q, qi=qi: q.dma_start(
                    out=xres[:], in_=x_in[128 * qi:128 * (qi + 1), :]), waits=[xres_free],
                    slot="xres")
                ev_add = V.add(lambda v, j=j: v.scalar_tensor_tensor(
                    x1t[:], Osl[j][:], rinv[:, j:j + 1], xres[:], ALU.mult, ALU.add),
                    waits=[ev_O[j], ev_rinv[j], ev_x, x1t_free_dma, x1t_free_ln])
                xres_free = ev_add
                ev_x1st = G.add(lambda q, qi=qi: q.dma_start(
                    out=x1s[128 * qi:128 * (qi + 1), :], in_=x1t[:]), waits=[ev_add], slot="x1st")
                x1t_free_dma = ev_x1st
                if prev_tr2 is not None:
                    V.pending.append(prev_tr2)
                ev_ln = ln_tile(p, x1t, xln2b, lt2, waits=[ev_add])
                x1t_free_ln = ev_ln
                ev_stg2 = None
                for grp in range(4):
                    pb = nT2 % 2
                    nT2 += 1
                    evt = None
                    for k in range(4):
                        dt = 4 * grp + k
                        evt = Tn.add(lambda t, pb=pb, k=k, dt=dt: t.transpose(
                            psT[pb][:, 128 * k:128 * (k + 1)],
                            xln2b[:, 128 * dt:128 * (dt + 1)], id_bf[:]),
                            waits=([ev_ln, psT_free[pb]] if k == 0 else []), inc=(k == 3))
                        prev_tr2 = evt
                    ev_stg2 = S.add(lambda s, pb=pb, grp=grp: s.copy(
                        stg2[:, 512 * grp:512 * (grp + 1)], psT[pb][:, 0:512]),
                        waits=[evt, stg2_free])
                    psT_free[pb] = ev_stg2
                ev_st2 = G.add(lambda q, qi=qi: q.dma_start(
                    out=xln2Ts[:, 128 * qi:128 * (qi + 1)].rearrange("(a p) n -> p a n", p=128),
                    in_=stg2.ap().rearrange("p (a n) -> p a n", a=16)), waits=[ev_stg2],
                    slot="stg2")
                stg2_free = ev_st2
            qTg_free = ev_mm
        p.barrier()



def p2_emit(nc, p, es, env):
    Tn, V, S, Q, G = p.T, p.V, p.S, p.Q, p.G
    wg8, wu8, wd8 = env["wg8"], env["wu8"], env["wd8"]
    sgate, sup, sdown = env["sgate"], env["sup"], env["sdown"]
    x1s, x2s, xln2Ts = env["x1s"], env["x2s"], env["xln2Ts"]
    id_f = env["id_f"]
    with ExitStack() as ph:
        sg_sb = ph.enter_context(nc.sbuf_tensor("p2_sg", [128, 64], F32))
        su_sb = ph.enter_context(nc.sbuf_tensor("p2_su", [128, 64], F32))
        sd_sb = ph.enter_context(nc.sbuf_tensor("p2_sd", [128, 16], F32))
        ev_sg = Q.add(lambda q: q.dma_start(out=sg_sb[:], in_=sgate.rearrange("(c p) -> p c", p=128),
                                            allow_slow_non_contiguous=True))
        ev_su = Q.add(lambda q: q.dma_start(out=su_sb[:], in_=sup.rearrange("(c p) -> p c", p=128),
                                            allow_slow_non_contiguous=True))
        ev_sd = Q.add(lambda q: q.dma_start(out=sd_sb[:], in_=sdown.rearrange("(c p) -> p c", p=128),
                                            allow_slow_non_contiguous=True))
        xlnTb = ph.enter_context(nc.sbuf_tensor("p2_xlnT", [128, 16 * 512], BF16))
        wg_sb = [ph.enter_context(nc.sbuf_tensor(f"p2_wg{i}", [128, 16 * 256], BF16)) for i in range(2)]
        wu_sb = [ph.enter_context(nc.sbuf_tensor(f"p2_wu{i}", [128, 16 * 256], BF16)) for i in range(2)]
        wd_sb = [ph.enter_context(nc.sbuf_tensor(f"p2_wd{i}", [128, 64 * 128], BF16)) for i in range(2)]
        gate_sb = ph.enter_context(nc.sbuf_tensor("p2_gate", [128, 512], BF16))
        fuse = ph.enter_context(nc.sbuf_tensor("p2_fuse", [128, 64 * 512], BF16))
        yTt = [ph.enter_context(nc.sbuf_tensor(f"p2_yT{i}", [128, 512], F32)) for i in range(2)]
        ynat = ph.enter_context(nc.sbuf_tensor("p2_yn", [128, 4 * 2048], F32))
        x1b = [ph.enter_context(nc.sbuf_tensor(f"p2_x1b{i}", [128, D], F32)) for i in range(2)]
        psG = [ph.enter_context(nc.psum_tensor(f"p2_psG{i}", [128, 512], F32)) for i in range(2)]
        psU = [ph.enter_context(nc.psum_tensor(f"p2_psU{i}", [128, 512], F32)) for i in range(2)]
        psY = [ph.enter_context(nc.psum_tensor(f"p2_psY{i}", [128, 512], F32)) for i in range(2)]
        psN = [ph.enter_context(nc.psum_tensor(f"p2_psN{i}", [128, 512], F32)) for i in range(2)]

        wg_free = [None, None]
        wu_free = [None, None]
        wd_free = [None, None]
        psG_free = [None, None]
        psU_free = [None, None]
        psY_free = [None, None]
        psN_free = [None, None]
        yT_free = [None, None]
        ynat_free = [None]
        gate_free = None
        fuse_free = None
        x2_free = [None, None]
        x1_free = [None, None]
        ev_mm = None

        def load_xln(c, wait_ev):
            return Q.add(lambda q, c=c: q.dma_start(
                out=xlnTb.ap().rearrange("p (a n) -> p a n", a=16),
                in_=xln2Ts[:, 512 * c:512 * (c + 1)].rearrange("(a p) n -> p a n", p=128)),
                waits=[wait_ev], slot="p2x")

        def load_w(kind, it2, sl):
            if kind == "g":
                return G.add(lambda q, it2=it2, sl=sl: q.dma_start(
                    out=wg_sb[sl].ap().rearrange("p (a n) -> p a n", a=16),
                    in_=wg8[:, 256 * it2:256 * (it2 + 1)].rearrange("(a p) n -> p a n", p=128)),
                    waits=[wg_free[sl]], slot=f"wg{sl}")
            if kind == "u":
                return G.add(lambda q, it2=it2, sl=sl: q.dma_start(
                    out=wu_sb[sl].ap().rearrange("p (a n) -> p a n", a=16),
                    in_=wu8[:, 256 * it2:256 * (it2 + 1)].rearrange("(a p) n -> p a n", p=128)),
                    waits=[wu_free[sl]], slot=f"wu{sl}")
            return G.add(lambda q, it2=it2, sl=sl: q.dma_start(
                out=wd_sb[sl].ap().rearrange("p (a n) -> p a n", a=64),
                in_=wd8[:, 128 * it2:128 * (it2 + 1)].rearrange("(a p) n -> p a n", p=128)),
                waits=[wd_free[sl]], slot=f"wd{sl}")

        ev_xln = [None] * 8
        ev_xln[0] = load_xln(0, None)
        for c in range(8):
            sl = c % 2
            evw_g = [None] * 32
            evw_u = [None] * 32
            evw_d = [None] * 16
            evw_g[0] = load_w("g", 0, 0)
            evw_u[0] = load_w("u", 0, 0)
            evw_d[0] = load_w("d", 0, 0)
            evw_d[1] = load_w("d", 1, 1)
            ev_fuse_last = None
            for it in range(64):
                it2 = it // 2
                half = it % 2
                if half == 0 and it2 + 1 < 32:
                    evw_g[it2 + 1] = load_w("g", it2 + 1, (it2 + 1) % 2)
                    evw_u[it2 + 1] = load_w("u", it2 + 1, (it2 + 1) % 2)
                wsl = it2 % 2
                pb = it % 2
                ev_mmg = None
                for dt in range(DT):
                    ev_mmg = Tn.add(lambda t, pb=pb, wsl=wsl, dt=dt, sl=sl, half=half: t.matmul(
                        psG[pb][:], wg_sb[wsl][:, 256 * dt + 128 * half:256 * dt + 128 * (half + 1)],
                        xlnTb[:, 512 * dt:512 * (dt + 1)],
                        start=(dt == 0), stop=(dt == DT - 1)),
                        waits=([evw_g[it2], ev_xln[c], psG_free[pb]] if dt == 0 else []),
                        inc=(dt == DT - 1))
                ev_mmu = None
                for dt in range(DT):
                    ev_mmu = Tn.add(lambda t, pb=pb, wsl=wsl, dt=dt, sl=sl, half=half: t.matmul(
                        psU[pb][:], wu_sb[wsl][:, 256 * dt + 128 * half:256 * dt + 128 * (half + 1)],
                        xlnTb[:, 512 * dt:512 * (dt + 1)],
                        start=(dt == 0), stop=(dt == DT - 1)),
                        waits=([evw_u[it2], psU_free[pb]] if dt == 0 else []),
                        inc=(dt == DT - 1))
                if half == 1:
                    wg_free[wsl] = ev_mmg
                    wu_free[wsl] = ev_mmu
                ev_si = S.add(lambda s, pb=pb, it=it: s.activation(
                    gate_sb[:], psG[pb][:], AF.Silu, scale=sg_sb[:, it:it + 1]),
                    waits=[ev_mmg, ev_sg, gate_free])
                psG_free[pb] = ev_si
                ev_fu = V.add(lambda v, pb=pb, it=it: v.scalar_tensor_tensor(
                    fuse[:, 512 * it:512 * (it + 1)], psU[pb][:], su_sb[:, it:it + 1],
                    gate_sb[:], ALU.mult, ALU.mult),
                    waits=[ev_si, ev_mmu, ev_su, fuse_free])
                psU_free[pb] = ev_fu
                gate_free = ev_fu
                ev_fuse_last = ev_fu
            if c + 1 < 8:
                ev_xln[c + 1] = load_xln(c + 1, ev_mmu)
            ev_yn_done = None
            for ht in range(16):
                wsl = ht % 2
                pb = ht % 2
                for it in range(64):
                    ev_mm = Tn.add(lambda t, pb=pb, wsl=wsl, it=it: t.matmul(
                        psY[pb][:], wd_sb[wsl][:, 128 * it:128 * (it + 1)],
                        fuse[:, 512 * it:512 * (it + 1)],
                        start=(it == 0), stop=(it == 63)),
                        waits=([evw_d[ht], ev_fuse_last, psY_free[pb]] if it == 0 else []),
                        inc=(it == 63))
                wd_free[wsl] = ev_mm
                if ht + 2 < 16:
                    evw_d[ht + 2] = load_w("d", ht + 2, ht % 2)
                ysl = ht % 2
                ev_yT = S.add(lambda s, pb=pb, ht=ht, ysl=ysl: s.activation(
                    yTt[ysl][:], psY[pb][:], AF.Copy, scale=sd_sb[:, ht:ht + 1]),
                    waits=[ev_mm, ev_sd, yT_free[ysl]])
                psY_free[pb] = ev_yT
                pb2 = ht % 2
                evt = None
                for ts in range(4):
                    evt = Tn.add(lambda t, pb2=pb2, ysl=ysl, ts=ts: t.transpose(
                        psN[pb2][:, 128 * ts:128 * (ts + 1)],
                        yTt[ysl][:, 128 * ts:128 * (ts + 1)], id_f[:]),
                        waits=([ev_yT, psN_free[pb2]] if ts == 0 else []), inc=(ts == 3))
                ev_yn = V.add(lambda v, pb2=pb2, ht=ht: v.tensor_copy(
                    ynat.ap().rearrange("p (s h) -> p s h", s=4)[:, :, 128 * ht:128 * (ht + 1)],
                    psN[pb2].ap().rearrange("p (s h) -> p s h", s=4)),
                    waits=[evt, x2_free[0], x2_free[1]])
                psN_free[pb2] = ev_yn
                ev_yn_done = ev_yn
                yT_free[ysl] = evt
            fuse_free = ev_mm
            for ts in range(4):
                ti = 4 * c + ts
                bs = ti % 2
                ev_x1 = Q.add(lambda q, ti=ti, bs=bs: q.dma_start(
                    out=x1b[bs][:], in_=x1s[128 * ti:128 * (ti + 1), :]), waits=[x1_free[bs]],
                    slot=f"x1b{bs}")
                ev_x2 = V.add(lambda v, ts=ts, bs=bs: v.tensor_add(
                    ynat[:, 2048 * ts:2048 * (ts + 1)], ynat[:, 2048 * ts:2048 * (ts + 1)],
                    x1b[bs][:]),
                    waits=[ev_yn_done, ev_x1])
                x1_free[bs] = ev_x2
                ev_st = Q.add(lambda q, ti=ti, ts=ts: q.dma_start(
                    out=x2s[128 * ti:128 * (ti + 1), :], in_=ynat[:, 2048 * ts:2048 * (ts + 1)]),
                    waits=[ev_x2], slot=f"x2st{ti % 2}")
                x2_free[ti % 2] = ev_st
                ynat_free[0] = ev_st
        p.barrier()



def p3_emit(nc, p, es, env, p3sub=None):
    Tn, V, S, Q, G = p.T, p.V, p.S, p.Q, p.G
    x2s, x3s, xln4Ts, ulnSc = env["x2s"], env["x3s"], env["xln4Ts"], env["ulnSc"]
    lre, lim, bre, bim, cre, cim = env["lre"], env["lim"], env["bre"], env["bim"], env["cre"], env["cim"]
    dvec, logdelta = env["dvec"], env["logdelta"]
    id_bf, id_f = env["id_bf"], env["id_f"]
    u128, iota0, iota1, ones1, e127 = env["u128"], env["iota0"], env["iota1"], env["ones1"], env["e127"]

    with ExitStack() as ph:
        prm = ph.enter_context(nc.sbuf_tensor("p3_prm", [1, 32 * 26], F32))

        def pr(i):
            return prm[:, 32 * i:32 * (i + 1)]

        (LRE, LIM, LDT, LAMRE, LRED, LIMD, ABS_, EXRE, C_, S_, ARE, AIM,
         T1, T2, T3, T4, BBR, BBI, A128R, A128I, KRE, KIM, BSR, BSI, RT1, RT2) = range(26)
        prm_i = ph.enter_context(nc.sbuf_tensor("p3_prmi", [1, 32], I32))

        ev_lre = Q.add(lambda q: q.dma_start(out=pr(LRE), in_=lre[:]))
        ev_lim = Q.add(lambda q: q.dma_start(out=pr(LIM), in_=lim[:]))
        ev_ldt = Q.add(lambda q: q.dma_start(out=pr(LDT), in_=logdelta[:, 0:32]))
        e1 = S.add(lambda s: s.activation(pr(T1), pr(LRE), AF.Abs), waits=[ev_lre])
        e2 = S.add(lambda s: s.activation(pr(T2), pr(T1), AF.Exp, scale=-1.0), waits=[e1])
        e3 = S.add(lambda s: s.activation(pr(T3), pr(T2), AF.Ln, bias=env["epsc"][0:1, 1:2]), waits=[e2])
        e4 = V.add(lambda v: v.tensor_scalar_max(pr(T4), pr(LRE), 0.0), waits=[ev_lre])
        e5 = V.add(lambda v: v.tensor_add(pr(T4), pr(T4), pr(T3)), waits=[e3, e4])
        e6 = V.add(lambda v: v.tensor_scalar(pr(LAMRE), pr(T4), 0.0101, -1.0, ALU.add, ALU.mult),
                   waits=[e5, ev_lim])
        e7 = S.add(lambda s: s.activation(pr(T1), pr(LDT), AF.Exp), waits=[ev_ldt, e3])
        e8 = V.add(lambda v: v.tensor_mul(pr(LRED), pr(LAMRE), pr(T1)), waits=[e6, e7])
        e9 = V.add(lambda v: v.tensor_mul(pr(LIMD), pr(LIM), pr(T1)), waits=[e8])
        eA = S.add(lambda s: s.activation(pr(ABS_), pr(LRED), AF.Exp), waits=[e8])

        def rred(dst_i, src_i, off, wl):
            w1 = V.add(lambda v: v.tensor_scalar(pr(RT1), pr(src_i), off, 1.0 / (2 * PI),
                                                 ALU.add, ALU.mult), waits=wl)
            w2 = V.add(lambda v: v.tensor_copy(prm_i[:], pr(RT1)), waits=[w1])
            w3 = V.add(lambda v: v.tensor_copy(pr(RT1), prm_i[:]), waits=[w2])
            w4 = V.add(lambda v: v.scalar_tensor_tensor(pr(RT1), pr(RT1), -2 * PI, pr(src_i),
                                                        ALU.mult, ALU.add), waits=[w3])
            w5 = V.add(lambda v: v.tensor_scalar_add(pr(dst_i), pr(RT1), off), waits=[w4])
            f1 = V.add(lambda v: v.tensor_scalar(pr(RT1), pr(dst_i), PI, None, ALU.is_gt), waits=[w5])
            f2 = V.add(lambda v: v.scalar_tensor_tensor(pr(dst_i), pr(RT1), -2 * PI, pr(dst_i),
                                                        ALU.mult, ALU.add), waits=[f1])
            f3 = V.add(lambda v: v.tensor_scalar(pr(RT1), pr(dst_i), -PI, None, ALU.is_lt), waits=[f2])
            f4 = V.add(lambda v: v.scalar_tensor_tensor(pr(dst_i), pr(RT1), 2 * PI, pr(dst_i),
                                                        ALU.mult, ALU.add), waits=[f3])
            return f4

        eB = rred(T4, LIMD, 0.0, [e9, eA])
        eC = S.add(lambda s: s.activation(pr(S_), pr(T4), AF.Sin), waits=[eB])
        eD = rred(T4, LIMD, PI / 2, [eC])
        eE = S.add(lambda s: s.activation(pr(C_), pr(T4), AF.Sin), waits=[eD])
        eF = V.add(lambda v: v.tensor_mul(pr(ARE), pr(ABS_), pr(C_)), waits=[eE])
        eG = V.add(lambda v: v.tensor_mul(pr(AIM), pr(ABS_), pr(S_)), waits=[eF])
        eH = V.add(lambda v: v.tensor_scalar_add(pr(T1), pr(ARE), -1.0), waits=[eG])
        eI = V.add(lambda v: v.tensor_mul(pr(T2), pr(LAMRE), pr(LAMRE)), waits=[eH])
        eJ = V.add(lambda v: v.scalar_tensor_tensor(pr(T3), pr(LIM), 1.0, pr(LIM), ALU.mult, ALU.mult),
                   waits=[eI])
        eK = V.add(lambda v: v.tensor_add(pr(T2), pr(T2), pr(T3)), waits=[eJ])
        eL = V.add(lambda v: v.reciprocal(pr(T2), pr(T2)), waits=[eK])
        eM = V.add(lambda v: v.tensor_mul(pr(T3), pr(T1), pr(LAMRE)), waits=[eL])
        eN = V.add(lambda v: v.scalar_tensor_tensor(pr(T4), pr(AIM), 1.0, pr(LIM), ALU.mult, ALU.mult),
                   waits=[eM])
        eO = V.add(lambda v: v.tensor_add(pr(T3), pr(T3), pr(T4)), waits=[eN])
        eP = V.add(lambda v: v.tensor_mul(pr(BSR), pr(T3), pr(T2)), waits=[eO])
        eQ = V.add(lambda v: v.tensor_mul(pr(T3), pr(AIM), pr(LAMRE)), waits=[eP])
        eR = V.add(lambda v: v.scalar_tensor_tensor(pr(T4), pr(T1), 1.0, pr(LIM), ALU.mult, ALU.mult),
                   waits=[eQ])
        eS_ = V.add(lambda v: v.tensor_sub(pr(T3), pr(T3), pr(T4)), waits=[eR])
        eT = V.add(lambda v: v.tensor_mul(pr(BSI), pr(T3), pr(T2)), waits=[eS_])
        ev_bs = eT
        eU = S.add(lambda s: s.activation(pr(T1), pr(LRED), AF.Exp, scale=128.0), waits=[eT])
        eV = V.add(lambda v: v.tensor_scalar_mul(pr(T2), pr(LIMD), 128.0), waits=[eU])
        eW = rred(T4, T2, 0.0, [eV])
        eX = S.add(lambda s: s.activation(pr(A128I), pr(T4), AF.Sin), waits=[eW])
        eY = rred(T4, T2, PI / 2, [eX])
        eZ = S.add(lambda s: s.activation(pr(A128R), pr(T4), AF.Sin), waits=[eY])
        ea = V.add(lambda v: v.scalar_tensor_tensor(pr(A128R), pr(T1), 1.0, pr(A128R),
                                                    ALU.mult, ALU.mult), waits=[eZ])
        eb = V.add(lambda v: v.scalar_tensor_tensor(pr(A128I), pr(T1), 1.0, pr(A128I),
                                                    ALU.mult, ALU.mult), waits=[ea])
        ev_a128 = eb

        grids = ph.enter_context(nc.sbuf_tensor("p3_grids", [128, 32 * 10], F32))

        def gr(i):
            return grids[:, 32 * i:32 * (i + 1)]

        GLRE, GLIM, GPR, GPI, GIR, GII, G1R, G1I, GT1, GT2 = range(10)
        tmpm = ph.enter_context(nc.sbuf_tensor("p3_tmpm", [128, 32], F32))
        tmpp = ph.enter_context(nc.sbuf_tensor("p3_tmpp", [128, 32], F32))
        tmpq = ph.enter_context(nc.sbuf_tensor("p3_tmpq", [128, 32], F32))
        tki = ph.enter_context(nc.sbuf_tensor("p3_tki", [128, 32], I32))

        bbT = ph.enter_context(nc.sbuf_tensor("p3_bbT", [128, 16 * 64], BF16))
        ctT = ph.enter_context(nc.sbuf_tensor("p3_ctT", [32, 2 * D], BF16))
        cbuf = ph.enter_context(nc.sbuf_tensor("p3_cbuf", [128, 32], F32))
        dbc = ph.enter_context(nc.sbuf_tensor("p3_dbc", [128, D], F32))
        dline = ph.enter_context(nc.sbuf_tensor("p3_dline", [1, D], F32))
        with ExitStack() as pp:
            psg = pp.enter_context(nc.psum_tensor("p3_psg", [128, 512], F32))
            em1 = Tn.add(lambda t: t.matmul(psg[:, 0:32], iota0[:], pr(LRED), start=True, stop=True),
                         waits=[ev_bs, ev_a128])
            em1b = Tn.add(lambda t: t.matmul(psg[:, 32:64], iota0[:], pr(LIMD), start=True, stop=True),
                          waits=[em1])
            em2 = Tn.add(lambda t: t.matmul(psg[:, 64:96], iota1[:], pr(LRED), start=True, stop=True),
                         waits=[em1b])
            em2b = Tn.add(lambda t: t.matmul(psg[:, 96:128], iota1[:], pr(LIMD), start=True, stop=True),
                          waits=[em2])
            ec1 = V.add(lambda v: v.tensor_copy(gr(GLRE), psg[:, 0:32]), waits=[em2b])
            ec2 = V.add(lambda v: v.tensor_copy(gr(GLIM), psg[:, 32:64]), waits=[ec1])
            ec3 = V.add(lambda v: v.tensor_copy(gr(GT1), psg[:, 64:96]), waits=[ec2])
            ec4 = V.add(lambda v: v.tensor_copy(gr(GT2), psg[:, 96:128]), waits=[ec3])

            def grred(dst, src_ap, off, wl):
                w1 = V.add(lambda v: v.tensor_scalar(tmpq[:], src_ap, off, 1.0 / (2 * PI),
                                                     ALU.add, ALU.mult), waits=wl)
                w2 = V.add(lambda v: v.tensor_copy(tki[:], tmpq[:]), waits=[w1])
                w3 = V.add(lambda v: v.tensor_copy(tmpq[:], tki[:]), waits=[w2])
                w4 = V.add(lambda v: v.scalar_tensor_tensor(tmpq[:], tmpq[:], -2 * PI, src_ap,
                                                            ALU.mult, ALU.add), waits=[w3])
                w5 = V.add(lambda v: v.tensor_scalar_add(dst, tmpq[:], off), waits=[w4])
                f1 = V.add(lambda v: v.tensor_scalar(tmpq[:], dst, PI, None, ALU.is_gt), waits=[w5])
                f2 = V.add(lambda v: v.scalar_tensor_tensor(dst, tmpq[:], -2 * PI, dst,
                                                            ALU.mult, ALU.add), waits=[f1])
                f3 = V.add(lambda v: v.tensor_scalar(tmpq[:], dst, -PI, None, ALU.is_lt), waits=[f2])
                f4 = V.add(lambda v: v.scalar_tensor_tensor(dst, tmpq[:], 2 * PI, dst,
                                                            ALU.mult, ALU.add), waits=[f3])
                return f4

            def build_grid(mag_slot, ph_slot, dr, di, wl):
                wm = S.add(lambda s: s.activation(tmpm[:], gr(mag_slot), AF.Exp), waits=wl)
                wp = grred(tmpp[:], gr(ph_slot), 0.0, [wm])
                ws = S.add(lambda s: s.activation(gr(di), tmpp[:], AF.Sin), waits=[wp])
                wp2 = grred(tmpp[:], gr(ph_slot), PI / 2, [ws])
                wc = S.add(lambda s: s.activation(gr(dr), tmpp[:], AF.Sin), waits=[wp2])
                w1 = V.add(lambda v: v.tensor_mul(gr(dr), tmpm[:], gr(dr)), waits=[wc])
                w2 = V.add(lambda v: v.tensor_mul(gr(di), tmpm[:], gr(di)), waits=[w1])
                return w2

            ev_gp = build_grid(GLRE, GLIM, GPR, GPI, [ec4])
            ev_g1 = build_grid(GT1, GT2, G1R, G1I, [ev_gp])
            wm = S.add(lambda s: s.activation(tmpm[:], gr(GLRE), AF.Exp, scale=-1.0), waits=[ev_g1])
            wp = grred(tmpp[:], gr(GLIM), 0.0, [wm])
            wsn = S.add(lambda s: s.activation(tmpq[:], tmpp[:], AF.Sin), waits=[wp])
            wii = V.add(lambda v: v.scalar_tensor_tensor(gr(GII), tmpq[:], -1.0, tmpm[:],
                                                         ALU.mult, ALU.mult), waits=[wsn])
            wp2 = grred(tmpp[:], gr(GLIM), PI / 2, [wii])
            wcs = S.add(lambda s: s.activation(tmpq[:], tmpp[:], AF.Sin), waits=[wp2])
            wir = V.add(lambda v: v.tensor_mul(gr(GIR), tmpq[:], tmpm[:]), waits=[wcs])
            ev_grids = wir

            brebuf = pp.enter_context(nc.sbuf_tensor("p3_breb", [32, D], F32))
            bimbuf = pp.enter_context(nc.sbuf_tensor("p3_bimb", [32, D], F32))
            bsr_b = pp.enter_context(nc.sbuf_tensor("p3_bsrb", [128, 32], F32))
            bsi_b = pp.enter_context(nc.sbuf_tensor("p3_bsib", [128, 32], F32))
            psb0 = pp.enter_context(nc.psum_tensor("p3_psb0", [128, 128], F32))
            evb1 = Q.add(lambda q: q.dma_start(out=brebuf[:], in_=bre[:]))
            evb2 = Q.add(lambda q: q.dma_start(out=bimbuf[:], in_=bim[:]))
            emb = Tn.add(lambda t: t.matmul(psg[:, 128:160], ones1[:], pr(BSR), start=True, stop=True),
                         waits=[ev_bs, ev_grids])
            emb2 = Tn.add(lambda t: t.matmul(psg[:, 160:192], ones1[:], pr(BSI), start=True, stop=True),
                          waits=[emb])
            evbb1 = V.add(lambda v: v.tensor_copy(bsr_b[:], psg[:, 128:160]), waits=[emb2])
            evbb2 = V.add(lambda v: v.tensor_copy(bsi_b[:], psg[:, 160:192]), waits=[evbb1])
            ev_bbT = None
            for dt in range(DT):
                et1 = Tn.add(lambda t, dt=dt: t.transpose(
                    psb0[:, 0:32], brebuf[:, 128 * dt:128 * (dt + 1)], id_f[0:32, 0:32]),
                    waits=[evb1, evbb2] if dt == 0 else [ev_bbT])
                ev1 = V.add(lambda v: v.tensor_copy(tmpm[:], psb0[:, 0:32]), waits=[et1])
                et2 = Tn.add(lambda t, dt=dt: t.transpose(
                    psb0[:, 32:64], bimbuf[:, 128 * dt:128 * (dt + 1)], id_f[0:32, 0:32]),
                    waits=[ev1, evb2])
                ev2 = V.add(lambda v: v.tensor_copy(tmpp[:], psb0[:, 32:64]), waits=[et2])
                ev3 = V.add(lambda v: v.tensor_mul(tmpq[:], bsr_b[:], tmpm[:]), waits=[ev2])
                ev4 = V.add(lambda v: v.scalar_tensor_tensor(
                    gr(GT1), bsi_b[:], 1.0, tmpp[:], ALU.mult, ALU.mult), waits=[ev3])
                ev5 = V.add(lambda v: v.tensor_sub(tmpq[:], tmpq[:], gr(GT1)), waits=[ev4])
                ev6 = V.add(lambda v, dt=dt: v.tensor_copy(bbT[:, 64 * dt:64 * dt + 32], tmpq[:]),
                            waits=[ev5])
                ev7 = V.add(lambda v: v.tensor_mul(tmpq[:], bsr_b[:], tmpp[:]), waits=[ev6])
                ev8 = V.add(lambda v: v.scalar_tensor_tensor(
                    gr(GT1), bsi_b[:], 1.0, tmpm[:], ALU.mult, ALU.mult), waits=[ev7])
                ev9 = V.add(lambda v: v.tensor_add(tmpq[:], tmpq[:], gr(GT1)), waits=[ev8])
                ev_bbT = V.add(lambda v, dt=dt: v.tensor_copy(bbT[:, 64 * dt + 32:64 * dt + 64], tmpq[:]),
                               waits=[ev9])

            ev_ctT = None
            for half, (csrc, sgn) in enumerate([(cre, 2.0), (cim, -2.0)]):
                for dt in range(DT):
                    evc = Q.add(lambda q, csrc=csrc, dt=dt: q.dma_start(
                        out=cbuf[:], in_=csrc[128 * dt:128 * (dt + 1), :]),
                        waits=[ev_ctT])
                    etc = Tn.add(lambda t: t.transpose(psb0[0:32, :], cbuf[:], id_f[:]),
                                 waits=[evc, ev_ctT, ev_bbT])
                    ev_ctT = V.add(lambda v, half=half, dt=dt, sgn=sgn: v.tensor_scalar_mul(
                        ctT[:, D * half + 128 * dt: D * half + 128 * (dt + 1)],
                        psb0[0:32, :], sgn), waits=[etc])

            evd = Q.add(lambda q: q.dma_start(out=dline[:], in_=dvec[:]))
            ev_dbc = None
            for nch in range(4):
                emd = Tn.add(lambda t, nch=nch: t.matmul(
                    psg[:, 0:512], ones1[:], dline[:, 512 * nch:512 * (nch + 1)],
                    start=True, stop=True), waits=[evd, ev_dbc, ec4, evbb2, ev_bbT])
                ev_dbc = V.add(lambda v, nch=nch: v.tensor_copy(dbc[:, 512 * nch:512 * (nch + 1)],
                                                                psg[:, 0:512]), waits=[emd])
            p.barrier()
        if p3sub == "prep":
            return

        x2b = [ph.enter_context(nc.sbuf_tensor(f"p3_x2b{i}", [128, D], F32)) for i in range(2)]
        ulnb = [ph.enter_context(nc.sbuf_tensor(f"p3_ulnb{i}", [128, D], BF16)) for i in range(2)]
        ulnT = ph.enter_context(nc.sbuf_tensor("p3_ulnT", [128, 16 * 128], BF16))
        buR = ph.enter_context(nc.sbuf_tensor("p3_buR", [128, 32 * 32], F32))
        buI = ph.enter_context(nc.sbuf_tensor("p3_buI", [128, 32 * 32], F32))
        lt3 = mk_ln_tmp(nc, ph, "p3", env["epsc"][:, 0:1])
        with ExitStack() as pp:
            psu = [pp.enter_context(nc.psum_tensor(f"p3_psu{i}", [128, 1024], BF16)) for i in range(2)]
            psbu = pp.enter_context(nc.psum_tensor("p3_psbu", [128, 64], F32))
            x2_ld = [None] * 32
            x2_free = [None, None]
            psu_free = [None, None]
            ulnT_free = None
            ulnb_free = [None, None]
            ev_bu_all = None
            prev_tr = None

            def load_x2(i):
                sl = i % 2
                x2_ld[i] = Q.add(lambda q, i=i, sl=sl: q.dma_start(
                    out=x2b[sl][:], in_=x2s[128 * i:128 * (i + 1), :]), waits=[x2_free[sl]],
                    slot=f"x2b{sl}")

            load_x2(0)
            for i in range(32):
                if i + 1 < 32:
                    load_x2(i + 1)
                sl = i % 2
                V.pending.append(x2_ld[i])
                if ulnb_free[sl] is not None:
                    V.pending.append(ulnb_free[sl])
                ev_ln = ln_tile(p, x2b[sl][:], ulnb[sl], lt3)
                x2_free[sl] = ev_ln
                uln_st = G.add(lambda q, i=i, sl=sl: q.dma_start(
                    out=ulnSc[128 * i:128 * (i + 1), :], in_=ulnb[sl][:]), waits=[ev_ln],
                    slot="ulnst")
                ev_trl = None
                for half in range(2):
                    pb = half
                    evt = None
                    for k in range(8):
                        dt = 8 * half + k
                        evt = Tn.add(lambda t, pb=pb, k=k, dt=dt, sl=sl: t.transpose(
                            psu[pb][:, 128 * k:128 * (k + 1)],
                            ulnb[sl][:, 128 * dt:128 * (dt + 1)], id_bf[:]),
                            waits=([ev_ln, psu_free[pb]] if k == 0 else []), inc=(k == 7))
                        prev_tr = evt
                    ev_trl = S.add(lambda s, pb=pb, half=half: s.copy(
                        ulnT[:, 1024 * half:1024 * (half + 1)], psu[pb][:]),
                        waits=[evt, ulnT_free])
                    psu_free[pb] = ev_trl
                ulnb_free[sl] = prev_tr
                V.pending.append(uln_st)
                ev_mm = None
                for dt in range(DT):
                    ev_mm = Tn.add(lambda t, dt=dt: t.matmul(
                        psbu[:], ulnT[:, 128 * dt:128 * (dt + 1)],
                        bbT[:, 64 * dt:64 * (dt + 1)],
                        start=(dt == 0), stop=(dt == DT - 1)),
                        waits=([ev_trl, ev_bbT, ev_bu_all] if dt == 0 else []),
                        inc=(dt == DT - 1))
                ulnT_free = ev_mm
                if "dq" in env:
                    env["dq"].pump(2)
                evr = V.add(lambda v, i=i: v.tensor_copy(buR[:, 32 * i:32 * (i + 1)], psbu[:, 0:32]),
                            waits=[ev_mm])
                ev_bu_all = V.add(lambda v, i=i: v.tensor_copy(buI[:, 32 * i:32 * (i + 1)],
                                                               psbu[:, 32:64]), waits=[evr])
            p.barrier()
        if p3sub == "a":
            return

        xlR = ph.enter_context(nc.sbuf_tensor("p3_xlR", [128, 32 * 32], F32))
        xlI = ph.enter_context(nc.sbuf_tensor("p3_xlI", [128, 32 * 32], F32))
        cbR = ph.enter_context(nc.sbuf_tensor("p3_cbR", [128, 32], F32))
        cbI = ph.enter_context(nc.sbuf_tensor("p3_cbI", [128, 32], F32))
        srowR = ph.enter_context(nc.sbuf_tensor("p3_srowR", [1, 32 * 32], F32))
        srowI = ph.enter_context(nc.sbuf_tensor("p3_srowI", [1, 32 * 32], F32))
        krowR = ph.enter_context(nc.sbuf_tensor("p3_krowR", [1, 32 * 32], F32))
        krowI = ph.enter_context(nc.sbuf_tensor("p3_krowI", [1, 32 * 32], F32))
        kcR = ph.enter_context(nc.sbuf_tensor("p3_kcR", [1, 32], F32))
        kcI = ph.enter_context(nc.sbuf_tensor("p3_kcI", [1, 32], F32))
        xsTR = ph.enter_context(nc.sbuf_tensor("p3_xsTR", [32, T], BF16))
        xsTI = ph.enter_context(nc.sbuf_tensor("p3_xsTI", [32, T], BF16))
        with ExitStack() as pp:
            pscan = pp.enter_context(nc.psum_tensor("p3_pscan", [128, 64], F32))
            psrow = pp.enter_context(nc.psum_tensor("p3_psrow", [1, 64], F32))
            psb = pp.enter_context(nc.psum_tensor("p3_psb", [128, 128], F32))
            ev_srow = [None] * 32
            ev_xl = [None] * 32
            pscan_free = None
            psrow_free = None
            ev_cum_prev = None
            for ci in range(32):
                bR = buR[:, 32 * ci:32 * (ci + 1)]
                bI = buI[:, 32 * ci:32 * (ci + 1)]
                w1 = V.add(lambda v, bR=bR: v.tensor_mul(cbR[:], gr(GIR), bR), waits=[ev_cum_prev])
                w2 = V.add(lambda v, bI=bI: v.scalar_tensor_tensor(tmpq[:], gr(GII), 1.0, bI,
                                                                   ALU.mult, ALU.mult), waits=[w1])
                w3 = V.add(lambda v: v.tensor_sub(cbR[:], cbR[:], tmpq[:]), waits=[w2])
                w4 = V.add(lambda v, bI=bI: v.tensor_mul(cbI[:], gr(GIR), bI), waits=[w3])
                w5 = V.add(lambda v, bR=bR: v.scalar_tensor_tensor(tmpq[:], gr(GII), 1.0, bR,
                                                                   ALU.mult, ALU.mult), waits=[w4])
                w6 = V.add(lambda v: v.tensor_add(cbI[:], cbI[:], tmpq[:]), waits=[w5])
                m1 = Tn.add(lambda t: t.matmul(pscan[:, 0:32], u128[:], cbR[:], start=True, stop=True),
                            waits=[w6, pscan_free])
                m2 = Tn.add(lambda t: t.matmul(pscan[:, 32:64], u128[:], cbI[:], start=True, stop=True),
                            waits=[m1])
                ev_cum_prev = m2
                w7 = V.add(lambda v, ci=ci: v.tensor_mul(xlR[:, 32 * ci:32 * (ci + 1)],
                                                         gr(GPR), pscan[:, 0:32]), waits=[m2])
                w8 = V.add(lambda v: v.scalar_tensor_tensor(
                    tmpq[:], gr(GPI), 1.0, pscan[:, 32:64], ALU.mult, ALU.mult), waits=[w7])
                w9 = V.add(lambda v, ci=ci: v.tensor_sub(xlR[:, 32 * ci:32 * (ci + 1)],
                                                         xlR[:, 32 * ci:32 * (ci + 1)], tmpq[:]),
                           waits=[w8])
                wA = V.add(lambda v, ci=ci: v.tensor_mul(xlI[:, 32 * ci:32 * (ci + 1)],
                                                         gr(GPI), pscan[:, 0:32]), waits=[w9])
                wB = V.add(lambda v: v.scalar_tensor_tensor(
                    tmpq[:], gr(GPR), 1.0, pscan[:, 32:64], ALU.mult, ALU.mult), waits=[wA])
                wC = V.add(lambda v, ci=ci: v.tensor_add(xlI[:, 32 * ci:32 * (ci + 1)],
                                                         xlI[:, 32 * ci:32 * (ci + 1)], tmpq[:]),
                           waits=[wB])
                pscan_free = wC
                m3 = Tn.add(lambda t, ci=ci: t.matmul(psrow[:, 0:32], e127[:],
                                                      xlR[:, 32 * ci:32 * (ci + 1)],
                                                      start=True, stop=True), waits=[wC, psrow_free])
                m4 = Tn.add(lambda t, ci=ci: t.matmul(psrow[:, 32:64], e127[:],
                                                      xlI[:, 32 * ci:32 * (ci + 1)],
                                                      start=True, stop=True), waits=[m3])
                s1 = V.add(lambda v, ci=ci: v.tensor_copy(srowR[:, 32 * ci:32 * (ci + 1)],
                                                          psrow[:, 0:32]), waits=[m4])
                s2 = V.add(lambda v, ci=ci: v.tensor_copy(srowI[:, 32 * ci:32 * (ci + 1)],
                                                          psrow[:, 32:64]), waits=[s1])
                psrow_free = s2
                ev_srow[ci] = s2
                ev_xl[ci] = wC
                if "dq" in env:
                    env["dq"].pump(1)
            k1 = V.add(lambda v: v.tensor_copy(kcR[:], srowR[:, 0:32]), waits=[ev_srow[0]])
            k2 = V.add(lambda v: v.tensor_copy(kcI[:], srowI[:, 0:32]), waits=[k1])
            k3 = V.add(lambda v: v.tensor_copy(krowR[:, 0:32], kcR[:]), waits=[k2])
            kl = V.add(lambda v: v.tensor_copy(krowI[:, 0:32], kcI[:]), waits=[k3])
            for ci in range(1, 32):
                c1 = V.add(lambda v: v.tensor_mul(tmpq[0:1, :], pr(A128R), kcR[:]),
                           waits=[kl, ev_srow[ci]])
                c2 = V.add(lambda v: v.scalar_tensor_tensor(tmpm[0:1, :], pr(A128I), 1.0, kcI[:],
                                                            ALU.mult, ALU.mult), waits=[c1])
                c3 = V.add(lambda v: v.tensor_sub(tmpq[0:1, :], tmpq[0:1, :], tmpm[0:1, :]),
                           waits=[c2])
                c4 = V.add(lambda v, ci=ci: v.tensor_add(tmpq[0:1, :], tmpq[0:1, :],
                                                         srowR[:, 32 * ci:32 * (ci + 1)]), waits=[c3])
                c5 = V.add(lambda v: v.tensor_mul(tmpm[0:1, :], pr(A128R), kcI[:]), waits=[c4])
                c6 = V.add(lambda v: v.scalar_tensor_tensor(tmpp[0:1, :], pr(A128I), 1.0, kcR[:],
                                                            ALU.mult, ALU.mult), waits=[c5])
                c7 = V.add(lambda v: v.tensor_add(tmpm[0:1, :], tmpm[0:1, :], tmpp[0:1, :]),
                           waits=[c6])
                c8 = V.add(lambda v, ci=ci: v.tensor_add(kcI[:], tmpm[0:1, :],
                                                         srowI[:, 32 * ci:32 * (ci + 1)]), waits=[c7])
                c9 = V.add(lambda v: v.tensor_copy(kcR[:], tmpq[0:1, :]), waits=[c8])
                cA = V.add(lambda v, ci=ci: v.tensor_copy(krowR[:, 32 * ci:32 * (ci + 1)], kcR[:]),
                           waits=[c9])
                kl = V.add(lambda v, ci=ci: v.tensor_copy(krowI[:, 32 * ci:32 * (ci + 1)], kcI[:]),
                           waits=[cA])
            ev_xsT_last = None
            pscan_free = kl
            for ci in range(32):
                if ci > 0:
                    mb1 = Tn.add(lambda t, ci=ci: t.matmul(
                        pscan[:, 0:32], ones1[:], krowR[:, 32 * (ci - 1):32 * ci],
                        start=True, stop=True), waits=[kl, pscan_free])
                    mb2 = Tn.add(lambda t, ci=ci: t.matmul(
                        pscan[:, 32:64], ones1[:], krowI[:, 32 * (ci - 1):32 * ci],
                        start=True, stop=True), waits=[mb1])
                    a1 = V.add(lambda v: v.tensor_mul(tmpq[:], gr(G1R), pscan[:, 0:32]), waits=[mb2])
                    a2 = V.add(lambda v: v.scalar_tensor_tensor(tmpm[:], gr(G1I), 1.0, pscan[:, 32:64],
                                                                ALU.mult, ALU.mult), waits=[a1])
                    a3 = V.add(lambda v: v.tensor_sub(tmpq[:], tmpq[:], tmpm[:]), waits=[a2])
                    a4 = V.add(lambda v, ci=ci: v.tensor_add(xlR[:, 32 * ci:32 * (ci + 1)],
                                                             xlR[:, 32 * ci:32 * (ci + 1)], tmpq[:]),
                               waits=[a3])
                    a5 = V.add(lambda v: v.tensor_mul(tmpq[:], gr(G1I), pscan[:, 0:32]), waits=[a4])
                    a6 = V.add(lambda v: v.scalar_tensor_tensor(tmpm[:], gr(G1R), 1.0, pscan[:, 32:64],
                                                                ALU.mult, ALU.mult), waits=[a5])
                    a7 = V.add(lambda v: v.tensor_add(tmpq[:], tmpq[:], tmpm[:]), waits=[a6])
                    a8 = V.add(lambda v, ci=ci: v.tensor_add(xlI[:, 32 * ci:32 * (ci + 1)],
                                                             xlI[:, 32 * ci:32 * (ci + 1)], tmpq[:]),
                               waits=[a7])
                    pscan_free = a8
                    ev_apply = a8
                else:
                    ev_apply = kl
                t1 = Tn.add(lambda t, ci=ci: t.transpose(
                    psb[0:32, :], xlR[:, 32 * ci:32 * (ci + 1)], id_f[:]),
                    waits=[ev_apply, ev_xl[ci]])
                e1_ = V.add(lambda v, ci=ci: v.tensor_copy(xsTR[:, 128 * ci:128 * (ci + 1)],
                                                           psb[0:32, :]), waits=[t1])
                t2 = Tn.add(lambda t, ci=ci: t.transpose(
                    psb[0:32, :], xlI[:, 32 * ci:32 * (ci + 1)], id_f[:]), waits=[e1_])
                ev_xsT_last = V.add(lambda v, ci=ci: v.tensor_copy(xsTI[:, 128 * ci:128 * (ci + 1)],
                                                                   psb[0:32, :]), waits=[t2])
            p.barrier()
        if p3sub == "b":
            return

        x3t = [ph.enter_context(nc.sbuf_tensor(f"p3_x3t{i}", [128, D], F32)) for i in range(2)]
        ulnb2 = [ph.enter_context(nc.sbuf_tensor(f"p3_ulnb2{i}", [128, D], BF16)) for i in range(2)]
        xln4b = ph.enter_context(nc.sbuf_tensor("p3_xln4", [128, D], BF16))
        stg3 = ph.enter_context(nc.sbuf_tensor("p3_stg3", [128, 16 * 128], BF16))
        lt4 = mk_ln_tmp(nc, ph, "p3c", env["epsc"][:, 0:1])
        with ExitStack() as pp:
            psy = pp.enter_context(nc.psum_tensor("p3_psy", [128, 2048], F32))
            psu = [pp.enter_context(nc.psum_tensor(f"p3c_psu{i}", [128, 1024], BF16)) for i in range(2)]
            psy_free = None
            stg3_free = None
            x3t_free = [None, None]
            psu_free = [None, None]
            x2_free = [None, None]
            uln_free = [None, None]
            prev_tr = None
            for i in range(32):
                sl = i % 2
                ev_x2l = Q.add(lambda q, i=i, sl=sl: q.dma_start(
                    out=x2b[sl][:], in_=x2s[128 * i:128 * (i + 1), :]), waits=[x2_free[sl]],
                    slot=f"x2b{sl}")
                ev_ul = G.add(lambda q, i=i, sl=sl: q.dma_start(
                    out=ulnb2[sl][:], in_=ulnSc[128 * i:128 * (i + 1), :]), waits=[uln_free[sl]],
                    slot=f"uln2{sl}")
                ev_mm = None
                for nch in range(4):
                    Tn.add(lambda t, i=i, nch=nch: t.matmul(
                        psy[:, 512 * nch:512 * (nch + 1)], xsTR[:, 128 * i:128 * (i + 1)],
                        ctT[:, 512 * nch:512 * (nch + 1)], start=True, stop=False),
                        waits=([ev_xsT_last, ev_ctT, psy_free] if nch == 0 else []), inc=False)
                    ev_mm = Tn.add(lambda t, i=i, nch=nch: t.matmul(
                        psy[:, 512 * nch:512 * (nch + 1)], xsTI[:, 128 * i:128 * (i + 1)],
                        ctT[:, D + 512 * nch: D + 512 * (nch + 1)], start=False, stop=True),
                        inc=(nch == 3))
                ev_du = V.add(lambda v, i=i, sl=sl: v.tensor_mul(x3t[sl][:], dbc[:], ulnb2[sl][:]),
                              waits=[ev_mm, ev_dbc, ev_ul, x3t_free[sl]])
                uln_free[sl] = ev_du
                ev_y1 = V.add(lambda v, sl=sl: v.tensor_add(x3t[sl][:], x3t[sl][:], psy[:]),
                              waits=[ev_du])
                psy_free = ev_y1
                ev_y2 = V.add(lambda v, sl=sl: v.tensor_add(x3t[sl][:], x3t[sl][:], x2b[sl][:]),
                              waits=[ev_y1, ev_x2l])
                x2_free[sl] = ev_y2
                ev_st = G.add(lambda q, i=i, sl=sl: q.dma_start(
                    out=x3s[128 * i:128 * (i + 1), :], in_=x3t[sl][:]), waits=[ev_y2],
                    slot=f"x3st{sl}")
                if prev_tr is not None:
                    V.pending.append(prev_tr)
                ev_ln = ln_tile(p, x3t[sl], xln4b, lt4, waits=[ev_y2])
                x3t_free[sl] = ev_ln
                V.pending.append(ev_st)
                ev_stg3 = None
                for half in range(2):
                    pb = half
                    evt = None
                    for k in range(8):
                        dt = 8 * half + k
                        evt = Tn.add(lambda t, pb=pb, k=k, dt=dt: t.transpose(
                            psu[pb][:, 128 * k:128 * (k + 1)],
                            xln4b[:, 128 * dt:128 * (dt + 1)], id_bf[:]),
                            waits=([ev_ln, psu_free[pb]] if k == 0 else []), inc=(k == 7))
                        prev_tr = evt
                    ev_stg3 = S.add(lambda s, pb=pb, half=half: s.copy(
                        stg3[:, 1024 * half:1024 * (half + 1)], psu[pb][:]),
                        waits=[evt, stg3_free])
                    psu_free[pb] = ev_stg3
                ev_st3 = G.add(lambda q, i=i: q.dma_start(
                    out=xln4Ts[:, 128 * i:128 * (i + 1)].rearrange("(a p) n -> p a n", p=128),
                    in_=stg3.ap().rearrange("p (a n) -> p a n", a=16)), waits=[ev_stg3], slot="stg3")
                stg3_free = ev_st3
                if "dq" in env:
                    env["dq"].pump(1)
            p.barrier()




class DequantPump:

    def __init__(self, nc, p, es, env):
        self.nc, self.p, self.env = nc, p, env
        Tn, V, S, Q, G = p.T, p.V, p.S, p.Q, p.G
        id_f = env["id_f"]
        gup_sc, dn_sc = env["gup_sc"], env["dn_sc"]
        self.es = es
        self.sct = es.enter_context(nc.sbuf_tensor("dq_sct", [128, NO1 * 16 + 16 * 44], F32))
        self.psd = [es.enter_context(nc.psum_tensor(f"dq_psd{i}", [128, 1024], BF16)) for i in range(2)]
        self.pk_sb = [es.enter_context(nc.sbuf_tensor(f"dq_pk{i}", [128, 704], I32)) for i in range(2)]
        self.nib = es.enter_context(nc.sbuf_tensor("dq_nib", [128, 2048], I32))
        self.dq1 = es.enter_context(nc.sbuf_tensor("dq_dq1", [128, 2048], I32))
        self.dq2 = es.enter_context(nc.sbuf_tensor("dq_dq2", [128, 2048], I32))
        self.dq3 = es.enter_context(nc.sbuf_tensor("dq_dq3", [128, 2048], I32))
        self.wbf = es.enter_context(nc.sbuf_tensor("dq_wbf", [128, 2048], BF16))
        self.stg4 = [es.enter_context(nc.sbuf_tensor(f"dq_stg4{i}", [128, 1024], BF16)) for i in range(2)]
        self.state = {"pk_free": [None, None], "psd_free": [None, None],
                      "stg4_free": [None, None], "wbf_free": None, "nstg": 0, "npk": 0}
        with ExitStack() as pp:
            scbuf = pp.enter_context(nc.sbuf_tensor("dq_scbuf", [44, 2 * I_FFN1], F32))
            psf = pp.enter_context(nc.psum_tensor("dq_psf", [128, 128], F32))
            evs1 = G.add(lambda q: q.dma_start(out=scbuf[0:16, 0:2 * I_FFN1], in_=gup_sc[:]))
            ev_sct = None
            for ot in range(NO1):
                etc = Tn.add(lambda t, ot=ot: t.transpose(
                    psf[:, 0:16], scbuf[0:16, 128 * ot:128 * (ot + 1)], id_f[0:16, 0:16]),
                    waits=[evs1] if ot == 0 else [ev_sct])
                ev_sct = V.add(lambda v, ot=ot: v.tensor_copy(
                    self.sct[:, 16 * ot:16 * (ot + 1)], psf[:, 0:16]), waits=[etc])
            evs2 = G.add(lambda q: q.dma_start(out=scbuf[0:44, 0:D], in_=dn_sc[:]), waits=[ev_sct])
            DSOFF = NO1 * 16
            for ht in range(16):
                etc = Tn.add(lambda t, ht=ht: t.transpose(
                    psf[:, 0:44], scbuf[0:44, 128 * ht:128 * (ht + 1)], id_f[0:44, 0:44]),
                    waits=[evs2] if ht == 0 else [ev_sct])
                ev_sct = V.add(lambda v, ht=ht: v.tensor_copy(
                    self.sct[:, DSOFF + 44 * ht:DSOFF + 44 * (ht + 1)], psf[:, 0:44]),
                    waits=[etc])
            self.ev_sct = ev_sct
            self.p.barrier()
        self.work = []
        for ot in range(NO1):
            self.work.append((env["gup_pk"], ot, 0, 256, 16 * ot, env["wguTs"], 0))
        DSOFF = NO1 * 16
        for ht in range(16):
            for s3 in range(3):
                w0 = 256 * s3
                nw = 256 if s3 < 2 else 192
                self.work.append((env["dn_pk"], ht, w0, nw, DSOFF + 44 * ht + 16 * s3,
                                  env["wdnTs"], 8 * w0))
        self.cursor = 0
        self.pk_loaded = {}

    def pump(self, n=1):
        for _ in range(n):
            if self.cursor >= len(self.work):
                return
            self._emit(self.work[self.cursor])
            self.cursor += 1

    def drain(self):
        self.pump(len(self.work))

    def _emit(self, item):
        nc, p = self.nc, self.p
        Tn, V, S, Q, G = p.T, p.V, p.S, p.Q, p.G
        id_bf = self.env["id_bf"]
        st = self.state
        src_pk, ot, w0, nwords, scoff, dstT, rowbase = item
        nh = 8 * nwords
        nsc = nh // 128
        key = (id(src_pk), ot)
        if key in self.pk_loaded and self.pk_loaded[key][1] == w0 - 256 or key not in self.pk_loaded:
            pass
        if key not in self.pk_loaded or self.pk_loaded[key][0] is None:
            sl = st["npk"] % 2
            st["npk"] += 1
            tot_w = src_pk.shape[1]
            evp = G.add(lambda q, ot=ot, sl=sl, src_pk=src_pk, tot_w=tot_w: q.dma_start(
                out=self.pk_sb[sl][:, 0:tot_w], in_=src_pk[128 * ot:128 * (ot + 1), :]),
                waits=[st["pk_free"][sl]], slot=f"dqpk{sl}")
            self.pk_loaded = {key: (evp, sl)}
        evp, sl = self.pk_loaded[key]
        evn = None
        for j in range(8):
            evn = V.add(lambda v, j=j, sl=sl, w0=w0, nwords=nwords, nh=nh: v.tensor_scalar(
                self.nib[:, j:nh:8], self.pk_sb[sl][:, w0:w0 + nwords], 4 * j, 15,
                ALU.logical_shift_right, ALU.bitwise_and),
                waits=([evp, self.ev_sct] if j == 0 else []), inc=(j == 7))
        if w0 + nwords >= src_pk.shape[1]:
            st["pk_free"][sl] = evn
        w1 = V.add(lambda v, nh=nh: v.tensor_scalar(self.dq1[:, 0:nh], self.nib[:, 0:nh], 7, None,
                                                    ALU.bitwise_and), waits=[evn])
        w2 = V.add(lambda v, nh=nh: v.tensor_scalar(self.dq2[:, 0:nh], self.nib[:, 0:nh], 3, None,
                                                    ALU.logical_shift_right), waits=[w1])
        w3 = V.add(lambda v, nh=nh: v.tensor_scalar(self.dq3[:, 0:nh], self.dq1[:, 0:nh], 0x400000,
                                                    None, ALU.mult), waits=[w2])
        w4 = V.add(lambda v, nh=nh: v.tensor_scalar(self.nib[:, 0:nh], self.dq1[:, 0:nh], 1, None,
                                                    ALU.is_ge), waits=[w3])
        w5 = V.add(lambda v, nh=nh: v.scalar_tensor_tensor(self.dq3[:, 0:nh], self.nib[:, 0:nh],
                                                           float(0x3EC00000), self.dq3[:, 0:nh],
                                                           ALU.mult, ALU.add), waits=[w4])
        w6 = V.add(lambda v, nh=nh: v.tensor_scalar(self.nib[:, 0:nh], self.dq1[:, 0:nh], 2, None,
                                                    ALU.is_ge), waits=[w5])
        w7 = V.add(lambda v, nh=nh: v.scalar_tensor_tensor(self.dq3[:, 0:nh], self.nib[:, 0:nh],
                                                           float(0x400000), self.dq3[:, 0:nh],
                                                           ALU.mult, ALU.add), waits=[w6])
        w8 = V.add(lambda v, nh=nh: v.scalar_tensor_tensor(self.dq3[:, 0:nh], self.dq2[:, 0:nh],
                                                           float(-2147483648), self.dq3[:, 0:nh],
                                                           ALU.mult, ALU.add), waits=[w7])
        evw = V.add(lambda v, scoff=scoff, nsc=nsc, nh=nh: v.tensor_tensor(
            self.wbf[:, 0:nh].rearrange("p (g n) -> p g n", n=128),
            self.dq3[:, 0:nh].bitcast(F32).rearrange("p (g n) -> p g n", n=128),
            bcast_last(self.sct[:, scoff:scoff + nsc].rearrange("p (g o) -> p g o", o=1), 128),
            ALU.mult), waits=[w8, st["wbf_free"]])
        last_tr = None
        g0 = 0
        while g0 < nsc:
            cnt = min(8, nsc - g0)
            pb = (g0 // 8) % 2
            evt = None
            for k in range(cnt):
                gi = g0 + k
                evt = Tn.add(lambda t, pb=pb, k=k, gi=gi: t.transpose(
                    self.psd[pb][:, 128 * k:128 * (k + 1)], self.wbf[:, 128 * gi:128 * (gi + 1)],
                    id_bf[:]),
                    waits=([evw, st["psd_free"][pb]] if k == 0 else []),
                    inc=(k == cnt - 1))
            last_tr = evt
            sg = st["nstg"] % 2
            st["nstg"] += 1
            evc = S.add(lambda s, pb=pb, sg=sg, cnt=cnt: s.copy(
                self.stg4[sg][:, 0:128 * cnt], self.psd[pb][:, 0:128 * cnt]),
                waits=[evt, st["stg4_free"][sg]])
            st["psd_free"][pb] = evc
            ev_last = Q.add(lambda q, g0=g0, cnt=cnt, ot=ot, sg=sg, dstT=dstT,
                            rowbase=rowbase: q.dma_start(
                out=dstT[rowbase + 128 * g0:rowbase + 128 * (g0 + cnt),
                         128 * ot:128 * (ot + 1)].rearrange("(a p) n -> p a n", p=128),
                in_=self.stg4[sg][:, 0:128 * cnt].rearrange("p (a n) -> p a n", a=cnt)),
                waits=[evc], slot=f"dqst{sg}")
            st["stg4_free"][sg] = ev_last
            g0 += cnt
        st["wbf_free"] = last_tr



def p4_emit(nc, p, es, env):
    Tn, V, S, Q, G = p.T, p.V, p.S, p.Q, p.G
    x3s, xln4Ts, out = env["x3s"], env["xln4Ts"], env["out"]
    gup_pk, gup_sc, dn_pk, dn_sc = env["gup_pk"], env["gup_sc"], env["dn_pk"], env["dn_sc"]
    wguTs, wdnTs = env["wguTs"], env["wdnTs"]
    id_bf, id_f = env["id_bf"], env["id_f"]


    with ExitStack() as ph:
        xlnTb = [ph.enter_context(nc.sbuf_tensor(f"p4_xlnT{i}", [128, 16 * 512], BF16)) for i in range(2)]
        wgu_sb = [ph.enter_context(nc.sbuf_tensor(f"p4_wgu{i}", [128, 16 * 128], BF16)) for i in range(4)]
        wdn_sb = [ph.enter_context(nc.sbuf_tensor(f"p4_wdn{i}", [128, 44 * 128], BF16)) for i in range(2)]
        gate_sb = ph.enter_context(nc.sbuf_tensor("p4_gate", [128, 512], BF16))
        fuse = ph.enter_context(nc.sbuf_tensor("p4_fuse", [128, 44 * 512], BF16))
        yT = ph.enter_context(nc.sbuf_tensor("p4_yT", [128, 16 * 512], F32))
        ynat = ph.enter_context(nc.sbuf_tensor("p4_ynat", [128, 4 * 2048], F32))
        ysq = ph.enter_context(nc.sbuf_tensor("p4_ysq", [128, 2048], F32))
        x3b = [ph.enter_context(nc.sbuf_tensor(f"p4_x3b{i}", [128, D], F32)) for i in range(2)]
        rms = ph.enter_context(nc.sbuf_tensor("p4_rms", [128, 4], F32))
        psG = [ph.enter_context(nc.psum_tensor(f"p4_psG{i}", [128, 512], F32)) for i in range(2)]
        psU = [ph.enter_context(nc.psum_tensor(f"p4_psU{i}", [128, 512], F32)) for i in range(2)]
        psY = [ph.enter_context(nc.psum_tensor(f"p4_psY{i}", [128, 512], F32)) for i in range(2)]
        psN = [ph.enter_context(nc.psum_tensor(f"p4_psN{i}", [128, 512], F32)) for i in range(2)]

        xlnT_free = [None, None]
        wgu_free = [None] * 4
        wdn_free = [None, None]
        psG_free = [None, None]
        psU_free = [None, None]
        psY_free = [None, None]
        psN_free = [None, None]
        gate_free = None
        fuse_free = None
        yT_free = None
        ynat_free = None
        out_free = [None, None]
        x3_free = [None, None]
        ev_mm = None

        def load_xln(c):
            sl = c % 2
            return Q.add(lambda q, c=c, sl=sl: q.dma_start(
                out=xlnTb[sl].ap().rearrange("p (a n) -> p a n", a=16),
                in_=xln4Ts[:, 512 * c:512 * (c + 1)].rearrange("(a p) n -> p a n", p=128)),
                waits=[xlnT_free[sl]], slot=f"p4x{sl}")

        def load_wgu(j, slot, which):
            ot = j if which == 0 else j + NI1
            return Q.add(lambda q, ot=ot, slot=slot: q.dma_start(
                out=wgu_sb[slot].ap().rearrange("p (a n) -> p a n", a=16),
                in_=wguTs[:, 128 * ot:128 * (ot + 1)].rearrange("(a p) n -> p a n", p=128)),
                waits=[wgu_free[slot]], slot=f"wgu{slot}")

        def load_wdn(ht, slot):
            return p.A.add(lambda q, ht=ht, slot=slot: q.dma_start(
                out=wdn_sb[slot].ap().rearrange("p (a n) -> p a n", a=44),
                in_=wdnTs[:, 128 * ht:128 * (ht + 1)].rearrange("(a p) n -> p a n", p=128)),
                waits=[wdn_free[slot]], slot=f"wdn{slot}")

        ev_xln = [None] * 8
        ev_xln[0] = load_xln(0)
        for c in range(8):
            if c + 1 < 8:
                ev_xln[c + 1] = load_xln(c + 1)
            sl = c % 2
            evw_g = [None] * NI1
            evw_u = [None] * NI1
            evw_g[0] = load_wgu(0, 0, 0)
            evw_u[0] = load_wgu(0, 1, 1)
            ev_fuse_last = None
            for j in range(NI1):
                if j + 1 < NI1:
                    evw_g[j + 1] = load_wgu(j + 1, (2 * (j + 1)) % 4, 0)
                    evw_u[j + 1] = load_wgu(j + 1, (2 * (j + 1) + 1) % 4, 1)
                gslot = (2 * j) % 4
                uslot = (2 * j + 1) % 4
                pb = j % 2
                ev_mmg = None
                for dt in range(DT):
                    ev_mmg = Tn.add(lambda t, pb=pb, gslot=gslot, dt=dt, sl=sl: t.matmul(
                        psG[pb][:], wgu_sb[gslot][:, 128 * dt:128 * (dt + 1)],
                        xlnTb[sl][:, 512 * dt:512 * (dt + 1)],
                        start=(dt == 0), stop=(dt == DT - 1)),
                        waits=([evw_g[j], ev_xln[c], psG_free[pb]] if dt == 0 else []),
                        inc=(dt == DT - 1))
                wgu_free[gslot] = ev_mmg
                ev_mmu = None
                for dt in range(DT):
                    ev_mmu = Tn.add(lambda t, pb=pb, uslot=uslot, dt=dt, sl=sl: t.matmul(
                        psU[pb][:], wgu_sb[uslot][:, 128 * dt:128 * (dt + 1)],
                        xlnTb[sl][:, 512 * dt:512 * (dt + 1)],
                        start=(dt == 0), stop=(dt == DT - 1)),
                        waits=([evw_u[j], psU_free[pb]] if dt == 0 else []),
                        inc=(dt == DT - 1))
                wgu_free[uslot] = ev_mmu
                ev_si = S.add(lambda s, pb=pb: s.activation(gate_sb[:], psG[pb][:], AF.Silu),
                              waits=[ev_mmg, gate_free])
                psG_free[pb] = ev_si
                ev_fu = V.add(lambda v, pb=pb, j=j: v.tensor_tensor(
                    fuse[:, 512 * j:512 * (j + 1)], psU[pb][:], gate_sb[:], ALU.mult),
                    waits=[ev_si, ev_mmu, fuse_free])
                psU_free[pb] = ev_fu
                gate_free = ev_fu
                ev_fuse_last = ev_fu
            xlnT_free[sl] = ev_mmu
            evw_d = [None] * 16
            evw_d[0] = load_wdn(0, 0)
            ev_yT_all = None
            for ht in range(16):
                if ht + 1 < 16:
                    evw_d[ht + 1] = load_wdn(ht + 1, (ht + 1) % 2)
                wslot = ht % 2
                pb = ht % 2
                for it in range(NI1):
                    ev_mm = Tn.add(lambda t, pb=pb, wslot=wslot, it=it: t.matmul(
                        psY[pb][:], wdn_sb[wslot][:, 128 * it:128 * (it + 1)],
                        fuse[:, 512 * it:512 * (it + 1)],
                        start=(it == 0), stop=(it == NI1 - 1)),
                        waits=([evw_d[ht], ev_fuse_last, psY_free[pb]] if it == 0 else []),
                        inc=(it == NI1 - 1))
                wdn_free[wslot] = ev_mm
                ev_yT_all = V.add(lambda v, pb=pb, ht=ht: v.tensor_copy(
                    yT[:, 512 * ht:512 * (ht + 1)], psY[pb][:]), waits=[ev_mm, yT_free])
                psY_free[pb] = ev_yT_all
            fuse_free = ev_mm
            last_tr = None
            ev_yn = None
            for ht in range(16):
                pb = ht % 2
                evt = None
                for ts in range(4):
                    evt = Tn.add(lambda t, pb=pb, ht=ht, ts=ts: t.transpose(
                        psN[pb][:, 128 * ts:128 * (ts + 1)],
                        yT[:, 512 * ht + 128 * ts: 512 * ht + 128 * (ts + 1)],
                        id_f[:]), waits=([ev_yT_all, psN_free[pb]] if ts == 0 else []),
                        inc=(ts == 3))
                last_tr = evt
                ev_yn = V.add(lambda v, pb=pb, ht=ht: v.tensor_copy(
                    ynat.ap().rearrange("p (s h) -> p s h", s=4)[:, :, 128 * ht:128 * (ht + 1)],
                    psN[pb].ap().rearrange("p (s h) -> p s h", s=4)),
                    waits=[evt, out_free[0], out_free[1]])
                psN_free[pb] = ev_yn
            yT_free = last_tr
            for ts in range(4):
                ti = 4 * c + ts
                bs = ti % 2
                ev_x3 = G.add(lambda q, ti=ti, bs=bs: q.dma_start(
                    out=x3b[bs][:], in_=x3s[128 * ti:128 * (ti + 1), :]), waits=[x3_free[bs]],
                    slot=f"x3b{bs}")
                ev_sq = V.add(lambda v, ts=ts: v.tensor_mul(
                    ysq[:], ynat[:, 2048 * ts:2048 * (ts + 1)], ynat[:, 2048 * ts:2048 * (ts + 1)]),
                    waits=[ev_yn])
                ev_rs = V.add(lambda v, ts=ts: v.reduce_sum(rms[:, ts:ts + 1], ysq[:], axis=AX.X),
                              waits=[ev_sq])
                ev_l = S.add(lambda s, ts=ts: s.activation(rms[:, ts:ts + 1], rms[:, ts:ts + 1],
                                                           AF.Ln, scale=1.0 / D, bias=env["epsc"][:, 2:3]),
                             waits=[ev_rs])
                ev_e = S.add(lambda s, ts=ts: s.activation(rms[:, ts:ts + 1], rms[:, ts:ts + 1],
                                                           AF.Exp, scale=-0.5), waits=[ev_l])
                ev_o = V.add(lambda v, ts=ts, bs=bs: v.scalar_tensor_tensor(
                    ynat[:, 2048 * ts:2048 * (ts + 1)], ynat[:, 2048 * ts:2048 * (ts + 1)],
                    rms[:, ts:ts + 1], x3b[bs][:], ALU.mult, ALU.add),
                    waits=[ev_e, ev_x3])
                x3_free[bs] = ev_o
                ev_so = G.add(lambda q, ti=ti, ts=ts, bs=bs: q.dma_start(
                    out=out[128 * ti:128 * (ti + 1), :], in_=ynat[:, 2048 * ts:2048 * (ts + 1)]),
                    waits=[ev_o], slot=f"outst{bs}")
                out_free[bs] = ev_so
            ynat_free = ev_so



def make_consts():
    idbf = np.eye(128, dtype=ml_dtypes.bfloat16)
    idf = np.eye(128, dtype=np.float32)
    mask = np.where(np.arange(128)[None, :] > np.arange(128)[:, None],
                    np.float32(NEG_INF), np.float32(0.0)).astype(np.float32)
    u128 = np.triu(np.ones((128, 128), dtype=np.float32))
    iota0 = np.arange(128, dtype=np.float32).reshape(1, 128)
    iota1 = np.arange(1, 129, dtype=np.float32).reshape(1, 128)
    ones = np.ones((1, 128), dtype=np.float32)
    e127 = np.zeros((128, 1), dtype=np.float32)
    e127[127, 0] = 1.0
    eps = np.tile(np.array([[1e-5, 1.0, 1e-6, 0.0]], dtype=np.float32), (128, 1))
    return {"c_idbf": idbf, "c_idf": idf, "c_mask": mask, "c_u128": u128,
            "c_iota0": iota0, "c_iota1": iota1, "c_ones": ones, "c_e127": e127, "c_eps": eps}


def make_in_map(inputs, b):
    m = {
        "x_in": np.ascontiguousarray(inputs["L0_attn_x"][b]),
        "wq": inputs["L0_attn_Wq"], "wk": inputs["L0_attn_Wk"], "wv": inputs["L0_attn_Wv"],
        "wg8": inputs["L0_ffn_w_gate"], "wu8": inputs["L0_ffn_w_up"], "wd8": inputs["L0_ffn_w_down"],
        "sgate": inputs["L0_ffn_s_gate"], "sup": inputs["L0_ffn_s_up"], "sdown": inputs["L0_ffn_s_down"],
        "lre": np.asarray(inputs["L1_attn_Lambda_unconstrained_re"]).reshape(1, 32),
        "lim": np.asarray(inputs["L1_attn_Lambda_im"]).reshape(1, 32),
        "bre": inputs["L1_attn_B_re"], "bim": inputs["L1_attn_B_im"],
        "cre": inputs["L1_attn_C_re"], "cim": inputs["L1_attn_C_im"],
        "dvec": np.asarray(inputs["L1_attn_D"]).reshape(1, D),
        "logdelta": np.asarray(inputs["L1_attn_log_Delta"]).reshape(1, 64),
        "gup_pk": inputs["L1_ffn_gate_up_packed"], "gup_sc": inputs["L1_ffn_gate_up_scales"],
        "dn_pk": inputs["L1_ffn_down_packed"], "dn_sc": inputs["L1_ffn_down_scales"],
    }
    m = {k: np.asarray(v) for k, v in m.items()}
    m.update(make_consts())
    return m


def kernel(**inputs):
    nc = build_nc("P4")
    in_maps = [make_in_map(inputs, b) for b in range(8)]
    res = run_bass_kernel_spmd(nc, in_maps, core_ids=list(range(8)))
    return np.stack([res.results[b]["out"] for b in range(8)], axis=0)


# version 09
# speedup vs baseline: 1.0583x; 1.0482x over previous
import sys
import numpy as np
from contextlib import ExitStack

try:
    import concourse.bass as bass
except ImportError:
    sys.path.insert(0, "/opt/trn_rl_repo")
    import concourse.bass as bass
import ml_dtypes
import concourse.mybir as mybir
from concourse.bass_utils import run_bass_kernel_spmd
from concourse.ap import AP as APcls


def bcast_last(ap3, count):
    return APcls(tensor=ap3.tensor, offset=ap3.offset,
                 ap=[list(q) for q in ap3.ap[:-1]] + [[0, count]])

F32 = mybir.dt.float32
BF16 = mybir.dt.bfloat16
I8 = mybir.dt.int8
I32 = mybir.dt.int32
F16 = mybir.dt.float16
AF = mybir.ActivationFunctionType
ALU = mybir.AluOpType
AX = mybir.AxisListType

T = 4096
D = 2048
DT = 16
NEG_INF = -1e30
ATT_SCALE = 1.0 / float(np.sqrt(2048.0))
I_FFN0 = 8192
I_FFN1 = 5632
NI1 = I_FFN1 // 128
NO1 = 2 * I_FFN1 // 128
PI = float(np.pi)



class E:

    NRING = 8

    def __init__(self, plan, key, semname, step):
        self.plan = plan
        self.key = key
        self.semname = semname
        self.step = step
        self.is_dma = step == 16
        if self.is_dma:
            self.ringnames = [f"{semname}{i}" for i in range(self.NRING)]
            self.ringvals = [0] * self.NRING
            self.ndma = 0
            self.slotmap = {}
        self.n = 0
        self.marks = {}
        self.pending = []

    def add(self, fn, waits=(), inc=True, slot=None):
        todo = []
        for ev in list(self.pending) + list(waits):
            if ev is None:
                continue
            s, v = ev
            if v <= 0 or self.marks.get(s, 0) >= v:
                continue
            self.marks[s] = v
            todo.append((s, v))
        self.pending = []
        plan = self.plan
        myev = None
        if inc and self.is_dma:
            if slot is not None:
                if slot not in self.slotmap:
                    self.slotmap[slot] = [f"{self.semname}s{len(self.slotmap)}", 0]
                ent = self.slotmap[slot]
                semname = ent[0]
                ent[1] += 16
                myev = (semname, ent[1])
            else:
                rslot = self.ndma % self.NRING
                self.ndma += 1
                semname = self.ringnames[rslot]
                prev = self.ringvals[rslot]
                if prev > 0 and self.marks.get(semname, 0) < prev:
                    todo.append((semname, prev))
                    self.marks[semname] = prev
                self.ringvals[rslot] = prev + 16
                myev = (semname, prev + 16)

            def cl(q):
                for s, v in todo:
                    q.wait_ge(plan.sems[s], v)
                fn(q).then_inc(plan.sems[semname], 16)
        elif inc:
            self.n += self.step
            myev = (self.semname, self.n)
            semname = self.semname
            step = self.step

            def cl(q):
                for s, v in todo:
                    q.wait_ge(plan.sems[s], v)
                fn(q).then_inc(plan.sems[semname], step)
        else:
            def cl(q):
                for s, v in todo:
                    q.wait_ge(plan.sems[s], v)
                fn(q)
        plan.streams[self.key].append(cl)
        return myev

    def now(self):
        if self.is_dma:
            return None
        return (self.semname, self.n)

    def all_events(self):
        if self.is_dma:
            evs = [(self.ringnames[i], self.ringvals[i]) for i in range(self.NRING)
                   if self.ringvals[i] > 0]
            evs += [(ent[0], ent[1]) for ent in self.slotmap.values() if ent[1] > 0]
            return evs
        return [(self.semname, self.n)] if self.n > 0 else []


class Plan:
    def __init__(self):
        self.streams = {k: [] for k in ("sync", "gpsimd", "tensor", "vector", "scalar")}
        self.sems = {}
        self.T = E(self, "tensor", "sT", 1)
        self.V = E(self, "vector", "sV", 1)
        self.S = E(self, "scalar", "sS", 1)
        self.Q = E(self, "sync", "sQ", 16)
        self.G = E(self, "gpsimd", "sG", 16)
        self.A = E(self, "scalar", "sA", 16)
        self.all = [self.T, self.V, self.S, self.Q, self.G, self.A]

    def barrier(self):
        evs = []
        for e in self.all:
            evs.extend(e.all_events())
        for e in self.all:
            e.pending.extend(evs)

    def emit(self, nc, es):
        for name in ("sT", "sV", "sS"):
            self.sems[name] = es.enter_context(nc.semaphore(name))
        for base in ("sQ", "sG", "sA"):
            for i in range(E.NRING):
                name = f"{base}{i}"
                self.sems[name] = es.enter_context(nc.semaphore(name))
        for e in (self.Q, self.G, self.A):
            for ent in e.slotmap.values():
                self.sems[ent[0]] = es.enter_context(nc.semaphore(ent[0]))
        block = es.enter_context(nc.Block())

        @block.sync
        def _(q):
            for cl in self.streams["sync"]:
                cl(q)

        @block.gpsimd
        def _(q):
            for cl in self.streams["gpsimd"]:
                cl(q)

        @block.tensor
        def _(q):
            for cl in self.streams["tensor"]:
                cl(q)

        @block.vector
        def _(q):
            for cl in self.streams["vector"]:
                cl(q)

        @block.scalar
        def _(q):
            for cl in self.streams["scalar"]:
                cl(q)


def ln_tile(p, xt, out_bf, tmp, waits=()):
    V, S = p.V, p.S
    ev = None
    for k in range(4):
        ev = V.add(lambda v, k=k: v.bn_stats(tmp["bn"][:, 6 * k:6 * k + 6],
                                             xt[:, 512 * k:512 * (k + 1)]),
                   waits=(waits if k == 0 else ()), inc=(k == 3))
    ev = V.add(lambda v: v.bn_aggr(tmp["mv"][:], tmp["bn"][:]), waits=[ev])
    e1 = S.add(lambda s: s.activation(tmp["r1"][:], tmp["mv"][:, 1:2], AF.Ln, bias=tmp["eps"]),
               waits=[ev])
    e2 = S.add(lambda s: s.activation(tmp["r2"][:], tmp["r1"][:], AF.Exp, scale=-0.5),
               waits=[e1])
    e3 = V.add(lambda v: v.scalar_tensor_tensor(tmp["nmr"][:], tmp["mv"][:, 0:1], -1.0,
                                                tmp["r2"][:], ALU.mult, ALU.mult),
               waits=[e2])
    e4 = V.add(lambda v: v.tensor_scalar(out_bf[:], xt[:], tmp["r2"][:, 0:1],
                                         tmp["nmr"][:, 0:1], ALU.mult, ALU.add),
               waits=[e3])
    return e4


def mk_ln_tmp(nc, es, tag, eps_ap=None):
    return {
        "bn": es.enter_context(nc.sbuf_tensor(f"{tag}_bn", [128, 24], F32)),
        "mv": es.enter_context(nc.sbuf_tensor(f"{tag}_mv", [128, 2], F32)),
        "r1": es.enter_context(nc.sbuf_tensor(f"{tag}_r1", [128, 1], F32)),
        "r2": es.enter_context(nc.sbuf_tensor(f"{tag}_r2", [128, 1], F32)),
        "nmr": es.enter_context(nc.sbuf_tensor(f"{tag}_nmr", [128, 1], F32)),
        "eps": eps_ap,
    }



def build_nc(upto="P4", start="P1", p3sub=None):
    nc = bass.Bass("TRN2", target_bir_lowering=False, debug=False)
    p = Plan()
    es = ExitStack()

    x_in = nc.dram_tensor("x_in", [T, D], F32, kind="ExternalInput").ap()
    wq = nc.dram_tensor("wq", [D, D], F32, kind="ExternalInput").ap()
    wk = nc.dram_tensor("wk", [D, D], F32, kind="ExternalInput").ap()
    wv = nc.dram_tensor("wv", [D, D], F32, kind="ExternalInput").ap()
    wg8 = nc.dram_tensor("wg8", [D, I_FFN0], I8, kind="ExternalInput").ap()
    wu8 = nc.dram_tensor("wu8", [D, I_FFN0], I8, kind="ExternalInput").ap()
    wd8 = nc.dram_tensor("wd8", [I_FFN0, D], I8, kind="ExternalInput").ap()
    sgate = nc.dram_tensor("sgate", [I_FFN0], F32, kind="ExternalInput").ap()
    sup = nc.dram_tensor("sup", [I_FFN0], F32, kind="ExternalInput").ap()
    sdown = nc.dram_tensor("sdown", [D], F32, kind="ExternalInput").ap()
    lre = nc.dram_tensor("lre", [1, 32], F32, kind="ExternalInput").ap()
    lim = nc.dram_tensor("lim", [1, 32], F32, kind="ExternalInput").ap()
    bre = nc.dram_tensor("bre", [32, D], F32, kind="ExternalInput").ap()
    bim = nc.dram_tensor("bim", [32, D], F32, kind="ExternalInput").ap()
    cre = nc.dram_tensor("cre", [D, 32], F32, kind="ExternalInput").ap()
    cim = nc.dram_tensor("cim", [D, 32], F32, kind="ExternalInput").ap()
    dvec = nc.dram_tensor("dvec", [1, D], F32, kind="ExternalInput").ap()
    logdelta = nc.dram_tensor("logdelta", [1, 64], F32, kind="ExternalInput").ap()
    gup_pk = nc.dram_tensor("gup_pk", [2 * I_FFN1, 256], I32, kind="ExternalInput").ap()
    gup_sc = nc.dram_tensor("gup_sc", [16, 2 * I_FFN1], F16, kind="ExternalInput").ap()
    dn_pk = nc.dram_tensor("dn_pk", [D, 704], I32, kind="ExternalInput").ap()
    dn_sc = nc.dram_tensor("dn_sc", [44, D], F16, kind="ExternalInput").ap()
    c_idbf = nc.dram_tensor("c_idbf", [128, 128], BF16, kind="ExternalInput").ap()
    c_idf = nc.dram_tensor("c_idf", [128, 128], F32, kind="ExternalInput").ap()
    c_mask = nc.dram_tensor("c_mask", [128, 128], F32, kind="ExternalInput").ap()
    c_u128 = nc.dram_tensor("c_u128", [128, 128], F32, kind="ExternalInput").ap()
    c_iota0 = nc.dram_tensor("c_iota0", [1, 128], F32, kind="ExternalInput").ap()
    c_iota1 = nc.dram_tensor("c_iota1", [1, 128], F32, kind="ExternalInput").ap()
    c_ones = nc.dram_tensor("c_ones", [1, 128], F32, kind="ExternalInput").ap()
    c_e127 = nc.dram_tensor("c_e127", [128, 1], F32, kind="ExternalInput").ap()
    c_eps = nc.dram_tensor("c_eps", [128, 4], F32, kind="ExternalInput").ap()

    out = nc.dram_tensor("out", [T, D], F32, kind="ExternalOutput").ap()

    qTs = nc.dram_tensor("qTs", [D, T], BF16, kind="Internal").ap()
    kTs = nc.dram_tensor("kTs", [D, T], BF16, kind="Internal").ap()
    vSs = nc.dram_tensor("vSs", [T, D], BF16, kind="Internal").ap()
    x1s = nc.dram_tensor("x1s", [T, D], F32, kind="Internal").ap()
    x2s = nc.dram_tensor("x2s", [T, D], F32, kind="ExternalInput" if start == "P3" else "Internal").ap()
    x3s = nc.dram_tensor("x3s", [T, D], F32, kind="Internal" if start != "P4" else "ExternalInput").ap()
    ulnSc = nc.dram_tensor("ulnSc", [T, D], BF16, kind="Internal").ap()
    xln2Ts = nc.dram_tensor("xln2Ts", [D, T], BF16, kind="Internal").ap()
    xln4Ts = nc.dram_tensor("xln4Ts", [D, T], BF16, kind="Internal" if start != "P4" else "ExternalInput").ap()
    wguTs = nc.dram_tensor("wguTs", [D, 2 * I_FFN1], BF16, kind="Internal").ap()
    wdnTs = nc.dram_tensor("wdnTs", [I_FFN1, D], BF16, kind="Internal").ap()

    Tn, V, S, Q, G = p.T, p.V, p.S, p.Q, p.G

    id_bf = es.enter_context(nc.sbuf_tensor("id_bf", [128, 128], BF16))
    id_f = es.enter_context(nc.sbuf_tensor("id_f", [128, 128], F32))
    msk = es.enter_context(nc.sbuf_tensor("msk", [128, 128], F32))
    u128 = es.enter_context(nc.sbuf_tensor("u128", [128, 128], F32))
    iota0 = es.enter_context(nc.sbuf_tensor("iota0", [1, 128], F32))
    iota1 = es.enter_context(nc.sbuf_tensor("iota1", [1, 128], F32))
    ones1 = es.enter_context(nc.sbuf_tensor("ones1", [1, 128], F32))
    e127 = es.enter_context(nc.sbuf_tensor("e127", [128, 1], F32))
    epsc = es.enter_context(nc.sbuf_tensor("epsc", [128, 4], F32))

    for dst, src in [(id_bf, c_idbf), (id_f, c_idf), (msk, c_mask), (u128, c_u128),
                     (iota0, c_iota0), (iota1, c_iota1), (ones1, c_ones), (e127, c_e127), (epsc, c_eps)]:
        Q.add(lambda q, d=dst, s=src: q.dma_start(out=d[:], in_=s[:]))
    p.barrier()

    env = dict(locals())

    if start == "P1":
        p1a_emit(nc, p, es, env)
        p1b_emit(nc, p, es, env)
        if upto == "P1":
            return finish_partial(nc, p, es, out, x1s)

        p2_emit(nc, p, es, env)
        if upto == "P2":
            return finish_partial(nc, p, es, out, x2s)

    if start != "P4":
        p3_emit(nc, p, es, env, p3sub)
        if p3sub is not None:
            return finish_partial(nc, p, es, out, x2s)
        if upto == "P3":
            return finish_partial(nc, p, es, out, x3s)
    with ExitStack() as dqes:
        dq = DequantPump(nc, p, dqes, env)
        dq.drain()
        p.barrier()

    p4_emit(nc, p, es, env)
    p.barrier()
    p.Q.add(lambda q: q.nop(), inc=False)
    p.emit(nc, es)
    es.close()
    return nc


def finish_partial(nc, p, es, out, src):
    fin = p.Q.add(lambda q: q.dma_start(out=out[:], in_=src[:]))
    p.barrier()
    p.Q.add(lambda q: q.nop(), inc=False)
    p.emit(nc, es)
    es.close()
    return nc



def p1a_emit(nc, p, es, env):
    Tn, V, S, Q, G = p.T, p.V, p.S, p.Q, p.G
    x_in, wq, wk, wv = env["x_in"], env["wq"], env["wk"], env["wv"]
    qTs, kTs, vSs = env["qTs"], env["kTs"], env["vSs"]
    id_bf = env["id_bf"]
    CH = 1024
    NCH = T // CH
    with ExitStack() as ph:
        xbuf = ph.enter_context(nc.sbuf_tensor("p1_x", [128, 8 * D], F32))
        xlnT = [ph.enter_context(nc.sbuf_tensor(f"p1_xlnT{i}", [128, 16 * CH], BF16)) for i in range(2)]
        xln_bf = ph.enter_context(nc.sbuf_tensor("p1_xln", [128, D], BF16))
        wqk_sb = [ph.enter_context(nc.sbuf_tensor(f"p1_wqk{i}", [128, 16 * 128], BF16)) for i in range(3)]
        wvb_sb = [ph.enter_context(nc.sbuf_tensor(f"p1_wv{i}", [128, 16 * 512], BF16)) for i in range(2)]
        stg = [ph.enter_context(nc.sbuf_tensor(f"p1_stg{i}", [128, 512], BF16)) for i in range(4)]
        lt = mk_ln_tmp(nc, ph, "p1", env["epsc"][:, 0:1])
        ptr = [ph.enter_context(nc.psum_tensor(f"p1_ptr{i}", [128, 1024], BF16)) for i in range(2)]
        pqk = [ph.enter_context(nc.psum_tensor(f"p1_pqk{i}", [128, 512], F32)) for i in range(4)]

        xbuf_free = None
        xlnT_free = [None, None]
        w_free = [None, None, None]
        wv_free = [None, None]
        stg_free = [None] * 4
        ptr_free = [None, None]
        pqk_free = [None] * 4
        x_ld = [None] * NCH
        nstg = 0
        nptr = 0
        npqk = 0
        nw = 0

        x_ld[0] = Q.add(lambda q: q.dma_start(
            out=xbuf.ap().rearrange("p (a d) -> p a d", a=8),
            in_=x_in[0:CH, :].rearrange("(a p) d -> p a d", p=128)), slot="p1x")
        for c in range(NCH):
            sl = c % 2
            ev_tr_last = None
            ev_ln_last = None
            prev_tr = None
            for a in range(8):
                xt = xbuf[:, D * a:D * (a + 1)]
                V.pending.append(x_ld[c])
                if prev_tr is not None:
                    V.pending.append(prev_tr)
                ev_ln = ln_tile(p, xt, xln_bf, lt)
                ev_ln_last = ev_ln
                for half in range(2):
                    pb = nptr % 2
                    nptr += 1
                    evt = None
                    for k in range(8):
                        dt = 8 * half + k
                        evt = Tn.add(lambda t, pb=pb, k=k, dt=dt: t.transpose(
                            ptr[pb][:, 128 * k:128 * (k + 1)],
                            xln_bf[:, 128 * dt:128 * (dt + 1)], id_bf[:]),
                            waits=([ev_ln, ptr_free[pb]] if k == 0 else []),
                            inc=(k == 7))
                        prev_tr = evt
                    ev_tr_last = S.add(lambda s, pb=pb, a=a, half=half, sl=sl: s.copy(
                        xlnT[sl].ap().rearrange("p (t c) -> p t c", t=16)[:, 8 * half:8 * half + 8,
                                                                         128 * a:128 * (a + 1)],
                        ptr[pb].ap().rearrange("p (t c) -> p t c", t=8)),
                        waits=[evt, xlnT_free[sl]])
                    ptr_free[pb] = ev_tr_last
            xbuf_free = ev_ln_last
            if c + 1 < NCH:
                x_ld[c + 1] = Q.add(lambda q, c=c: q.dma_start(
                    out=xbuf.ap().rearrange("p (a d) -> p a d", a=8),
                    in_=x_in[CH * (c + 1):CH * (c + 2), :].rearrange("(a p) d -> p a d", p=128)),
                    waits=[xbuf_free], slot="p1x")
            ev_xlnT = ev_tr_last

            ev_mm = None
            for wmat, dstT in [(wq, qTs), (wk, kTs)]:
                for et in range(DT):
                    wsl = nw % 3
                    nw += 1
                    ev_w = G.add(lambda q, wmat=wmat, et=et, wsl=wsl: q.dma_start(
                        out=wqk_sb[wsl].ap().rearrange("p (a n) -> p a n", a=16),
                        in_=wmat[:, 128 * et:128 * (et + 1)].rearrange("(a p) n -> p a n", p=128)),
                        waits=[w_free[wsl]], slot=f"wqk{wsl}")
                    for th in range(2):
                        pb = npqk % 4
                        npqk += 1
                        for dt in range(DT):
                            ev_mm = Tn.add(lambda t, pb=pb, wsl=wsl, dt=dt, sl=sl, th=th: t.matmul(
                                pqk[pb][:], wqk_sb[wsl][:, 128 * dt:128 * (dt + 1)],
                                xlnT[sl][:, CH * dt + 512 * th: CH * dt + 512 * (th + 1)],
                                start=(dt == 0), stop=(dt == DT - 1)),
                                waits=([ev_w, ev_xlnT, pqk_free[pb]] if dt == 0 else []),
                                inc=(dt == DT - 1))
                        sg = nstg % 4
                        nstg += 1
                        ev_ev = V.add(lambda v, sg=sg, pb=pb: v.tensor_copy(stg[sg][:], pqk[pb][:]),
                                      waits=[ev_mm, stg_free[sg]])
                        pqk_free[pb] = ev_ev
                        ev_st = Q.add(lambda q, sg=sg, et=et, c=c, th=th, dstT=dstT: q.dma_start(
                            out=dstT[128 * et:128 * (et + 1), CH * c + 512 * th: CH * c + 512 * (th + 1)],
                            in_=stg[sg][:]), waits=[ev_ev], slot=f"stg{sg}")
                        stg_free[sg] = ev_st
                    w_free[wsl] = ev_mm
            for nch in range(4):
                wsl = nch % 2
                ev_w = G.add(lambda q, nch=nch, wsl=wsl: q.dma_start(
                    out=wvb_sb[wsl].ap().rearrange("p (a n) -> p a n", a=16),
                    in_=wv[:, 512 * nch:512 * (nch + 1)].rearrange("(a p) n -> p a n", p=128)),
                    waits=[wv_free[wsl]], slot=f"wvb{wsl}")
                for a in range(8):
                    pb = npqk % 4
                    npqk += 1
                    for dt in range(DT):
                        ev_mm = Tn.add(lambda t, pb=pb, wsl=wsl, dt=dt, sl=sl, a=a: t.matmul(
                            pqk[pb][:], xlnT[sl][:, CH * dt + 128 * a: CH * dt + 128 * (a + 1)],
                            wvb_sb[wsl][:, 512 * dt:512 * (dt + 1)],
                            start=(dt == 0), stop=(dt == DT - 1)),
                            waits=([ev_w, ev_xlnT, pqk_free[pb]] if dt == 0 else []),
                            inc=(dt == DT - 1))
                    sg = nstg % 4
                    nstg += 1
                    ev_ev = V.add(lambda v, sg=sg, pb=pb: v.tensor_copy(stg[sg][:], pqk[pb][:]),
                                  waits=[ev_mm, stg_free[sg]])
                    pqk_free[pb] = ev_ev
                    ev_st = Q.add(lambda q, sg=sg, nch=nch, c=c, a=a: q.dma_start(
                        out=vSs[CH * c + 128 * a: CH * c + 128 * (a + 1), 512 * nch:512 * (nch + 1)],
                        in_=stg[sg][:]), waits=[ev_ev], slot=f"stg{sg}")
                    stg_free[sg] = ev_st
                wv_free[wsl] = ev_mm
            xlnT_free[sl] = ev_mm
        p.barrier()



def p1b_emit(nc, p, es, env):
    Tn, V, S, Q, G = p.T, p.V, p.S, p.Q, p.G
    x_in, qTs, kTs, vSs = env["x_in"], env["qTs"], env["kTs"], env["vSs"]
    x1s, xln2Ts = env["x1s"], env["xln2Ts"]
    id_bf, msk = env["id_bf"], env["msk"]
    with ExitStack() as ph:
        qTg = ph.enter_context(nc.sbuf_tensor("p1b_qTg", [128, 16 * 512], BF16))
        kv = [ph.enter_context(nc.sbuf_tensor(f"p1b_kv{i}", [128, 16 * 512], BF16)) for i in range(2)]
        Ssl = [ph.enter_context(nc.sbuf_tensor(f"p1b_S{j}", [128, T], F32)) for j in range(4)]
        Psl = [ph.enter_context(nc.sbuf_tensor(f"p1b_P{j}", [128, T], BF16)) for j in range(4)]
        Osl = [ph.enter_context(nc.sbuf_tensor(f"p1b_O{j}", [128, D], F32)) for j in range(4)]
        nmax = ph.enter_context(nc.sbuf_tensor("p1b_nmax", [128, 4], F32))
        rsum = ph.enter_context(nc.sbuf_tensor("p1b_rsum", [128, 4], F32))
        rinv = ph.enter_context(nc.sbuf_tensor("p1b_rinv", [128, 4], F32))
        pTt = [ph.enter_context(nc.sbuf_tensor(f"p1b_pT{i}", [128, 512], BF16)) for i in range(2)]
        xres = ph.enter_context(nc.sbuf_tensor("p1b_xres", [128, D], F32))
        x1t = ph.enter_context(nc.sbuf_tensor("p1b_x1t", [128, D], F32))
        xln2b = ph.enter_context(nc.sbuf_tensor("p1b_xln2", [128, D], BF16))
        stg2 = ph.enter_context(nc.sbuf_tensor("p1b_stg2", [128, 16 * 128], BF16))
        lt2 = mk_ln_tmp(nc, ph, "p1b", env["epsc"][:, 0:1])
        psS = [ph.enter_context(nc.psum_tensor(f"p1b_psS{i}", [128, 512], F32)) for i in range(2)]
        psO = ph.enter_context(nc.psum_tensor("p1b_psO", [128, 2048], F32))
        psT = [ph.enter_context(nc.psum_tensor(f"p1b_psT{i}", [128, 512], BF16)) for i in range(2)]

        kv_free = [None, None]
        psS_free = [None, None]
        psT_free = [None, None]
        pTt_free = [None, None]
        psO_free = None
        qTg_free = None
        stg2_free = None
        xres_free = None
        x1t_free_dma = None
        x1t_free_ln = None
        pT_last = [None] * 4
        nS = 0
        nT2 = 0
        nV = 0
        ev_mm = None

        for g in range(8):
            ev_q = Q.add(lambda q, g=g: q.dma_start(
                out=qTg.ap().rearrange("p (a n) -> p a n", a=16),
                in_=qTs[:, 512 * g:512 * (g + 1)].rearrange("(a p) n -> p a n", p=128)),
                waits=[qTg_free], slot="qTg")
            ev_S_done = [None] * 4
            for kb in range(g + 1):
                ksl = kb % 2
                ev_k = Q.add(lambda q, kb=kb, ksl=ksl: q.dma_start(
                    out=kv[ksl].ap().rearrange("p (a n) -> p a n", a=16),
                    in_=kTs[:, 512 * kb:512 * (kb + 1)].rearrange("(a p) n -> p a n", p=128)),
                    waits=[kv_free[ksl]], slot=f"kv{ksl}")
                for j in range(4):
                    width = 512 if kb < g else 128 * (j + 1)
                    pb = nS % 2
                    nS += 1
                    for dt in range(DT):
                        ev_mm = Tn.add(lambda t, pb=pb, dt=dt, j=j, ksl=ksl, width=width: t.matmul(
                            psS[pb][:, 0:width], qTg[:, 512 * dt + 128 * j: 512 * dt + 128 * (j + 1)],
                            kv[ksl][:, 512 * dt:512 * dt + width],
                            start=(dt == 0), stop=(dt == DT - 1)),
                            waits=([ev_q, ev_k, psS_free[pb]] if dt == 0 else []),
                            inc=(dt == DT - 1))
                    ev_ev = S.add(lambda s, j=j, kb=kb, pb=pb, width=width: s.activation(
                        Ssl[j][:, 512 * kb:512 * kb + width], psS[pb][:, 0:width],
                        AF.Copy, scale=ATT_SCALE), waits=[ev_mm])
                    psS_free[pb] = ev_ev
                    if kb == g:
                        ev_ev = V.add(lambda v, j=j, kb=kb: v.tensor_tensor(
                            Ssl[j][:, 512 * kb + 128 * j: 512 * kb + 128 * (j + 1)],
                            Ssl[j][:, 512 * kb + 128 * j: 512 * kb + 128 * (j + 1)],
                            msk[:], ALU.add), waits=[ev_ev])
                    ev_S_done[j] = ev_ev
                kv_free[ksl] = ev_mm
            ev_P = [None] * 4
            ev_rinv = [None] * 4
            for j in range(4):
                w = 512 * g + 128 * (j + 1)
                ev1 = V.add(lambda v, j=j, w=w: v.reduce_max(
                    nmax[:, j:j + 1], Ssl[j][:, 0:w], axis=AX.X, negate=True),
                    waits=[ev_S_done[j]])
                ev2 = S.add(lambda s, j=j, w=w: s.activation(
                    Psl[j][:, 0:w], Ssl[j][:, 0:w], AF.Exp, bias=nmax[:, j:j + 1],
                    accum_out=rsum[:, j:j + 1]), waits=[ev1, pT_last[j]])
                ev_P[j] = ev2
                ev_rinv[j] = V.add(lambda v, j=j: v.reciprocal(
                    rinv[:, j:j + 1], rsum[:, j:j + 1]), waits=[ev2])
            ev_O = [None] * 4
            for kb in range(g + 1):
                vsl = nV % 2
                nV += 1
                ev_v = Q.add(lambda q, kb=kb, vsl=vsl: q.dma_start(
                    out=kv[vsl].ap().rearrange("p (a d) -> p a d", a=4),
                    in_=vSs[512 * kb:512 * (kb + 1), :].rearrange("(a p) d -> p a d", p=128)),
                    waits=[kv_free[vsl]], slot=f"kv{vsl}")
                for j in range(4):
                    nkt = 4 if kb < g else j + 1
                    pb = nT2 % 2
                    nT2 += 1
                    evt = None
                    for kt in range(nkt):
                        evt = Tn.add(lambda t, pb=pb, j=j, kb=kb, kt=kt: t.transpose(
                            psT[pb][:, 128 * kt:128 * (kt + 1)],
                            Psl[j][:, 512 * kb + 128 * kt: 512 * kb + 128 * (kt + 1)],
                            id_bf[:]), waits=([ev_P[j], psT_free[pb]] if kt == 0 else []),
                            inc=(kt == nkt - 1))
                    pT_last[j] = evt
                    ev_pc = S.add(lambda s, pb=pb, nkt=nkt: s.copy(
                        pTt[pb][:, 0:128 * nkt], psT[pb][:, 0:128 * nkt]),
                        waits=[evt, pTt_free[pb]])
                    psT_free[pb] = ev_pc
                    for kt in range(nkt):
                        for nchi in range(4):
                            w = []
                            if kt == 0 and nchi == 0:
                                w = [ev_pc, ev_v, psO_free]
                            ev_mm = Tn.add(lambda t, pb=pb, vsl=vsl, kt=kt, nchi=nchi, nkt=nkt: t.matmul(
                                psO[:, 512 * nchi:512 * (nchi + 1)], pTt[pb][:, 128 * kt:128 * (kt + 1)],
                                kv[vsl][:, D * kt + 512 * nchi: D * kt + 512 * (nchi + 1)],
                                start=(kt == 0), stop=(kt == nkt - 1)),
                                waits=w, inc=(kt == nkt - 1 and nchi == 3))
                    pTt_free[pb] = ev_mm
                    if kb == 0:
                        ev_O[j] = V.add(lambda v, j=j: v.tensor_copy(Osl[j][:], psO[:]),
                                        waits=[ev_mm])
                    else:
                        ev_O[j] = V.add(lambda v, j=j: v.tensor_add(Osl[j][:], Osl[j][:], psO[:]),
                                        waits=[ev_mm, ev_O[j]])
                    psO_free = ev_O[j]
                kv_free[vsl] = ev_mm
            prev_tr2 = None
            for j in range(4):
                qi = 4 * g + j
                ev_x = G.add(lambda q, qi=qi: q.dma_start(
                    out=xres[:], in_=x_in[128 * qi:128 * (qi + 1), :]), waits=[xres_free],
                    slot="xres")
                ev_add = V.add(lambda v, j=j: v.scalar_tensor_tensor(
                    x1t[:], Osl[j][:], rinv[:, j:j + 1], xres[:], ALU.mult, ALU.add),
                    waits=[ev_O[j], ev_rinv[j], ev_x, x1t_free_dma, x1t_free_ln])
                xres_free = ev_add
                ev_x1st = G.add(lambda q, qi=qi: q.dma_start(
                    out=x1s[128 * qi:128 * (qi + 1), :], in_=x1t[:]), waits=[ev_add], slot="x1st")
                x1t_free_dma = ev_x1st
                if prev_tr2 is not None:
                    V.pending.append(prev_tr2)
                ev_ln = ln_tile(p, x1t, xln2b, lt2, waits=[ev_add])
                x1t_free_ln = ev_ln
                ev_stg2 = None
                for grp in range(4):
                    pb = nT2 % 2
                    nT2 += 1
                    evt = None
                    for k in range(4):
                        dt = 4 * grp + k
                        evt = Tn.add(lambda t, pb=pb, k=k, dt=dt: t.transpose(
                            psT[pb][:, 128 * k:128 * (k + 1)],
                            xln2b[:, 128 * dt:128 * (dt + 1)], id_bf[:]),
                            waits=([ev_ln, psT_free[pb]] if k == 0 else []), inc=(k == 3))
                        prev_tr2 = evt
                    ev_stg2 = S.add(lambda s, pb=pb, grp=grp: s.copy(
                        stg2[:, 512 * grp:512 * (grp + 1)], psT[pb][:, 0:512]),
                        waits=[evt, stg2_free])
                    psT_free[pb] = ev_stg2
                ev_st2 = G.add(lambda q, qi=qi: q.dma_start(
                    out=xln2Ts[:, 128 * qi:128 * (qi + 1)].rearrange("(a p) n -> p a n", p=128),
                    in_=stg2.ap().rearrange("p (a n) -> p a n", a=16)), waits=[ev_stg2],
                    slot="stg2")
                stg2_free = ev_st2
            qTg_free = ev_mm
        p.barrier()



def p2_emit(nc, p, es, env):
    Tn, V, S, Q, G = p.T, p.V, p.S, p.Q, p.G
    wg8, wu8, wd8 = env["wg8"], env["wu8"], env["wd8"]
    sgate, sup, sdown = env["sgate"], env["sup"], env["sdown"]
    x1s, x2s, xln2Ts = env["x1s"], env["x2s"], env["xln2Ts"]
    id_f = env["id_f"]
    with ExitStack() as ph:
        sg_sb = ph.enter_context(nc.sbuf_tensor("p2_sg", [128, 64], F32))
        su_sb = ph.enter_context(nc.sbuf_tensor("p2_su", [128, 64], F32))
        sd_sb = ph.enter_context(nc.sbuf_tensor("p2_sd", [128, 16], F32))
        ev_sg = Q.add(lambda q: q.dma_start(out=sg_sb[:], in_=sgate.rearrange("(c p) -> p c", p=128),
                                            allow_slow_non_contiguous=True))
        ev_su = Q.add(lambda q: q.dma_start(out=su_sb[:], in_=sup.rearrange("(c p) -> p c", p=128),
                                            allow_slow_non_contiguous=True))
        ev_sd = Q.add(lambda q: q.dma_start(out=sd_sb[:], in_=sdown.rearrange("(c p) -> p c", p=128),
                                            allow_slow_non_contiguous=True))
        xlnTb = ph.enter_context(nc.sbuf_tensor("p2_xlnT", [128, 16 * 512], BF16))
        wg_sb = [ph.enter_context(nc.sbuf_tensor(f"p2_wg{i}", [128, 16 * 256], BF16)) for i in range(2)]
        wu_sb = [ph.enter_context(nc.sbuf_tensor(f"p2_wu{i}", [128, 16 * 256], BF16)) for i in range(2)]
        wd_sb = [ph.enter_context(nc.sbuf_tensor(f"p2_wd{i}", [128, 64 * 128], BF16)) for i in range(2)]
        gate_sb = ph.enter_context(nc.sbuf_tensor("p2_gate", [128, 512], BF16))
        fuse = ph.enter_context(nc.sbuf_tensor("p2_fuse", [128, 64 * 512], BF16))
        yTt = [ph.enter_context(nc.sbuf_tensor(f"p2_yT{i}", [128, 512], F32)) for i in range(2)]
        ynat = ph.enter_context(nc.sbuf_tensor("p2_yn", [128, 4 * 2048], F32))
        x1b = [ph.enter_context(nc.sbuf_tensor(f"p2_x1b{i}", [128, D], F32)) for i in range(2)]
        psG = [ph.enter_context(nc.psum_tensor(f"p2_psG{i}", [128, 512], F32)) for i in range(2)]
        psU = [ph.enter_context(nc.psum_tensor(f"p2_psU{i}", [128, 512], F32)) for i in range(2)]
        psY = [ph.enter_context(nc.psum_tensor(f"p2_psY{i}", [128, 512], F32)) for i in range(2)]
        psN = [ph.enter_context(nc.psum_tensor(f"p2_psN{i}", [128, 512], F32)) for i in range(2)]

        wg_free = [None, None]
        wu_free = [None, None]
        wd_free = [None, None]
        psG_free = [None, None]
        psU_free = [None, None]
        psY_free = [None, None]
        psN_free = [None, None]
        yT_free = [None, None]
        ynat_free = [None]
        gate_free = None
        fuse_free = None
        x2_free = [None, None]
        x1_free = [None, None]
        ev_mm = None

        def load_xln(c, wait_ev):
            return Q.add(lambda q, c=c: q.dma_start(
                out=xlnTb.ap().rearrange("p (a n) -> p a n", a=16),
                in_=xln2Ts[:, 512 * c:512 * (c + 1)].rearrange("(a p) n -> p a n", p=128)),
                waits=[wait_ev], slot="p2x")

        def load_w(kind, it2, sl):
            if kind == "g":
                return G.add(lambda q, it2=it2, sl=sl: q.dma_start(
                    out=wg_sb[sl].ap().rearrange("p (a n) -> p a n", a=16),
                    in_=wg8[:, 256 * it2:256 * (it2 + 1)].rearrange("(a p) n -> p a n", p=128)),
                    waits=[wg_free[sl]], slot=f"wg{sl}")
            if kind == "u":
                return G.add(lambda q, it2=it2, sl=sl: q.dma_start(
                    out=wu_sb[sl].ap().rearrange("p (a n) -> p a n", a=16),
                    in_=wu8[:, 256 * it2:256 * (it2 + 1)].rearrange("(a p) n -> p a n", p=128)),
                    waits=[wu_free[sl]], slot=f"wu{sl}")
            return G.add(lambda q, it2=it2, sl=sl: q.dma_start(
                out=wd_sb[sl].ap().rearrange("p (a n) -> p a n", a=64),
                in_=wd8[:, 128 * it2:128 * (it2 + 1)].rearrange("(a p) n -> p a n", p=128)),
                waits=[wd_free[sl]], slot=f"wd{sl}")

        ev_xln = [None] * 8
        ev_xln[0] = load_xln(0, None)
        for c in range(8):
            sl = c % 2
            evw_g = [None] * 32
            evw_u = [None] * 32
            evw_d = [None] * 16
            evw_g[0] = load_w("g", 0, 0)
            evw_u[0] = load_w("u", 0, 0)
            evw_d[0] = load_w("d", 0, 0)
            evw_d[1] = load_w("d", 1, 1)
            ev_fuse_last = None
            for it in range(64):
                it2 = it // 2
                half = it % 2
                if half == 0 and it2 + 1 < 32:
                    evw_g[it2 + 1] = load_w("g", it2 + 1, (it2 + 1) % 2)
                    evw_u[it2 + 1] = load_w("u", it2 + 1, (it2 + 1) % 2)
                wsl = it2 % 2
                pb = it % 2
                ev_mmg = None
                for dt in range(DT):
                    ev_mmg = Tn.add(lambda t, pb=pb, wsl=wsl, dt=dt, sl=sl, half=half: t.matmul(
                        psG[pb][:], wg_sb[wsl][:, 256 * dt + 128 * half:256 * dt + 128 * (half + 1)],
                        xlnTb[:, 512 * dt:512 * (dt + 1)],
                        start=(dt == 0), stop=(dt == DT - 1)),
                        waits=([evw_g[it2], ev_xln[c], psG_free[pb]] if dt == 0 else []),
                        inc=(dt == DT - 1))
                ev_mmu = None
                for dt in range(DT):
                    ev_mmu = Tn.add(lambda t, pb=pb, wsl=wsl, dt=dt, sl=sl, half=half: t.matmul(
                        psU[pb][:], wu_sb[wsl][:, 256 * dt + 128 * half:256 * dt + 128 * (half + 1)],
                        xlnTb[:, 512 * dt:512 * (dt + 1)],
                        start=(dt == 0), stop=(dt == DT - 1)),
                        waits=([evw_u[it2], psU_free[pb]] if dt == 0 else []),
                        inc=(dt == DT - 1))
                if half == 1:
                    wg_free[wsl] = ev_mmg
                    wu_free[wsl] = ev_mmu
                ev_si = S.add(lambda s, pb=pb, it=it: s.activation(
                    gate_sb[:], psG[pb][:], AF.Silu, scale=sg_sb[:, it:it + 1]),
                    waits=[ev_mmg, ev_sg, gate_free])
                psG_free[pb] = ev_si
                ev_fu = V.add(lambda v, pb=pb, it=it: v.scalar_tensor_tensor(
                    fuse[:, 512 * it:512 * (it + 1)], psU[pb][:], su_sb[:, it:it + 1],
                    gate_sb[:], ALU.mult, ALU.mult),
                    waits=[ev_si, ev_mmu, ev_su, fuse_free])
                psU_free[pb] = ev_fu
                gate_free = ev_fu
                ev_fuse_last = ev_fu
            if c + 1 < 8:
                ev_xln[c + 1] = load_xln(c + 1, ev_mmu)
            ev_yn_done = None
            for ht in range(16):
                wsl = ht % 2
                pb = ht % 2
                for it in range(64):
                    ev_mm = Tn.add(lambda t, pb=pb, wsl=wsl, it=it: t.matmul(
                        psY[pb][:], wd_sb[wsl][:, 128 * it:128 * (it + 1)],
                        fuse[:, 512 * it:512 * (it + 1)],
                        start=(it == 0), stop=(it == 63)),
                        waits=([evw_d[ht], ev_fuse_last, psY_free[pb]] if it == 0 else []),
                        inc=(it == 63))
                wd_free[wsl] = ev_mm
                if ht + 2 < 16:
                    evw_d[ht + 2] = load_w("d", ht + 2, ht % 2)
                ysl = ht % 2
                ev_yT = S.add(lambda s, pb=pb, ht=ht, ysl=ysl: s.activation(
                    yTt[ysl][:], psY[pb][:], AF.Copy, scale=sd_sb[:, ht:ht + 1]),
                    waits=[ev_mm, ev_sd, yT_free[ysl]])
                psY_free[pb] = ev_yT
                pb2 = ht % 2
                evt = None
                for ts in range(4):
                    evt = Tn.add(lambda t, pb2=pb2, ysl=ysl, ts=ts: t.transpose(
                        psN[pb2][:, 128 * ts:128 * (ts + 1)],
                        yTt[ysl][:, 128 * ts:128 * (ts + 1)], id_f[:]),
                        waits=([ev_yT, psN_free[pb2]] if ts == 0 else []), inc=(ts == 3))
                ev_yn = V.add(lambda v, pb2=pb2, ht=ht: v.tensor_copy(
                    ynat.ap().rearrange("p (s h) -> p s h", s=4)[:, :, 128 * ht:128 * (ht + 1)],
                    psN[pb2].ap().rearrange("p (s h) -> p s h", s=4)),
                    waits=[evt, x2_free[0], x2_free[1]])
                psN_free[pb2] = ev_yn
                ev_yn_done = ev_yn
                yT_free[ysl] = evt
            fuse_free = ev_mm
            for ts in range(4):
                ti = 4 * c + ts
                bs = ti % 2
                ev_x1 = Q.add(lambda q, ti=ti, bs=bs: q.dma_start(
                    out=x1b[bs][:], in_=x1s[128 * ti:128 * (ti + 1), :]), waits=[x1_free[bs]],
                    slot=f"x1b{bs}")
                ev_x2 = V.add(lambda v, ts=ts, bs=bs: v.tensor_add(
                    ynat[:, 2048 * ts:2048 * (ts + 1)], ynat[:, 2048 * ts:2048 * (ts + 1)],
                    x1b[bs][:]),
                    waits=[ev_yn_done, ev_x1])
                x1_free[bs] = ev_x2
                ev_st = Q.add(lambda q, ti=ti, ts=ts: q.dma_start(
                    out=x2s[128 * ti:128 * (ti + 1), :], in_=ynat[:, 2048 * ts:2048 * (ts + 1)]),
                    waits=[ev_x2], slot=f"x2st{ti % 2}")
                x2_free[ti % 2] = ev_st
                ynat_free[0] = ev_st
        p.barrier()



def p3_emit(nc, p, es, env, p3sub=None):
    Tn, V, S, Q, G = p.T, p.V, p.S, p.Q, p.G
    x2s, x3s, xln4Ts, ulnSc = env["x2s"], env["x3s"], env["xln4Ts"], env["ulnSc"]
    lre, lim, bre, bim, cre, cim = env["lre"], env["lim"], env["bre"], env["bim"], env["cre"], env["cim"]
    dvec, logdelta = env["dvec"], env["logdelta"]
    id_bf, id_f = env["id_bf"], env["id_f"]
    u128, iota0, iota1, ones1, e127 = env["u128"], env["iota0"], env["iota1"], env["ones1"], env["e127"]

    with ExitStack() as ph:
        prm = ph.enter_context(nc.sbuf_tensor("p3_prm", [1, 32 * 26], F32))

        def pr(i):
            return prm[:, 32 * i:32 * (i + 1)]

        (LRE, LIM, LDT, LAMRE, LRED, LIMD, ABS_, EXRE, C_, S_, ARE, AIM,
         T1, T2, T3, T4, BBR, BBI, A128R, A128I, KRE, KIM, BSR, BSI, RT1, RT2) = range(26)
        prm_i = ph.enter_context(nc.sbuf_tensor("p3_prmi", [1, 32], I32))

        ev_lre = Q.add(lambda q: q.dma_start(out=pr(LRE), in_=lre[:]))
        ev_lim = Q.add(lambda q: q.dma_start(out=pr(LIM), in_=lim[:]))
        ev_ldt = Q.add(lambda q: q.dma_start(out=pr(LDT), in_=logdelta[:, 0:32]))
        e1 = S.add(lambda s: s.activation(pr(T1), pr(LRE), AF.Abs), waits=[ev_lre])
        e2 = S.add(lambda s: s.activation(pr(T2), pr(T1), AF.Exp, scale=-1.0), waits=[e1])
        e3 = S.add(lambda s: s.activation(pr(T3), pr(T2), AF.Ln, bias=env["epsc"][0:1, 1:2]), waits=[e2])
        e4 = V.add(lambda v: v.tensor_scalar_max(pr(T4), pr(LRE), 0.0), waits=[ev_lre])
        e5 = V.add(lambda v: v.tensor_add(pr(T4), pr(T4), pr(T3)), waits=[e3, e4])
        e6 = V.add(lambda v: v.tensor_scalar(pr(LAMRE), pr(T4), 0.0101, -1.0, ALU.add, ALU.mult),
                   waits=[e5, ev_lim])
        e7 = S.add(lambda s: s.activation(pr(T1), pr(LDT), AF.Exp), waits=[ev_ldt, e3])
        e8 = V.add(lambda v: v.tensor_mul(pr(LRED), pr(LAMRE), pr(T1)), waits=[e6, e7])
        e9 = V.add(lambda v: v.tensor_mul(pr(LIMD), pr(LIM), pr(T1)), waits=[e8])
        eA = S.add(lambda s: s.activation(pr(ABS_), pr(LRED), AF.Exp), waits=[e8])

        def rred(dst_i, src_i, off, wl):
            w1 = V.add(lambda v: v.tensor_scalar(pr(RT1), pr(src_i), off, 1.0 / (2 * PI),
                                                 ALU.add, ALU.mult), waits=wl)
            w2 = V.add(lambda v: v.tensor_copy(prm_i[:], pr(RT1)), waits=[w1])
            w3 = V.add(lambda v: v.tensor_copy(pr(RT1), prm_i[:]), waits=[w2])
            w4 = V.add(lambda v: v.scalar_tensor_tensor(pr(RT1), pr(RT1), -2 * PI, pr(src_i),
                                                        ALU.mult, ALU.add), waits=[w3])
            w5 = V.add(lambda v: v.tensor_scalar_add(pr(dst_i), pr(RT1), off), waits=[w4])
            f1 = V.add(lambda v: v.tensor_scalar(pr(RT1), pr(dst_i), PI, None, ALU.is_gt), waits=[w5])
            f2 = V.add(lambda v: v.scalar_tensor_tensor(pr(dst_i), pr(RT1), -2 * PI, pr(dst_i),
                                                        ALU.mult, ALU.add), waits=[f1])
            f3 = V.add(lambda v: v.tensor_scalar(pr(RT1), pr(dst_i), -PI, None, ALU.is_lt), waits=[f2])
            f4 = V.add(lambda v: v.scalar_tensor_tensor(pr(dst_i), pr(RT1), 2 * PI, pr(dst_i),
                                                        ALU.mult, ALU.add), waits=[f3])
            return f4

        eB = rred(T4, LIMD, 0.0, [e9, eA])
        eC = S.add(lambda s: s.activation(pr(S_), pr(T4), AF.Sin), waits=[eB])
        eD = rred(T4, LIMD, PI / 2, [eC])
        eE = S.add(lambda s: s.activation(pr(C_), pr(T4), AF.Sin), waits=[eD])
        eF = V.add(lambda v: v.tensor_mul(pr(ARE), pr(ABS_), pr(C_)), waits=[eE])
        eG = V.add(lambda v: v.tensor_mul(pr(AIM), pr(ABS_), pr(S_)), waits=[eF])
        eH = V.add(lambda v: v.tensor_scalar_add(pr(T1), pr(ARE), -1.0), waits=[eG])
        eI = V.add(lambda v: v.tensor_mul(pr(T2), pr(LAMRE), pr(LAMRE)), waits=[eH])
        eJ = V.add(lambda v: v.scalar_tensor_tensor(pr(T3), pr(LIM), 1.0, pr(LIM), ALU.mult, ALU.mult),
                   waits=[eI])
        eK = V.add(lambda v: v.tensor_add(pr(T2), pr(T2), pr(T3)), waits=[eJ])
        eL = V.add(lambda v: v.reciprocal(pr(T2), pr(T2)), waits=[eK])
        eM = V.add(lambda v: v.tensor_mul(pr(T3), pr(T1), pr(LAMRE)), waits=[eL])
        eN = V.add(lambda v: v.scalar_tensor_tensor(pr(T4), pr(AIM), 1.0, pr(LIM), ALU.mult, ALU.mult),
                   waits=[eM])
        eO = V.add(lambda v: v.tensor_add(pr(T3), pr(T3), pr(T4)), waits=[eN])
        eP = V.add(lambda v: v.tensor_mul(pr(BSR), pr(T3), pr(T2)), waits=[eO])
        eQ = V.add(lambda v: v.tensor_mul(pr(T3), pr(AIM), pr(LAMRE)), waits=[eP])
        eR = V.add(lambda v: v.scalar_tensor_tensor(pr(T4), pr(T1), 1.0, pr(LIM), ALU.mult, ALU.mult),
                   waits=[eQ])
        eS_ = V.add(lambda v: v.tensor_sub(pr(T3), pr(T3), pr(T4)), waits=[eR])
        eT = V.add(lambda v: v.tensor_mul(pr(BSI), pr(T3), pr(T2)), waits=[eS_])
        ev_bs = eT
        eU = S.add(lambda s: s.activation(pr(T1), pr(LRED), AF.Exp, scale=128.0), waits=[eT])
        eV = V.add(lambda v: v.tensor_scalar_mul(pr(T2), pr(LIMD), 128.0), waits=[eU])
        eW = rred(T4, T2, 0.0, [eV])
        eX = S.add(lambda s: s.activation(pr(A128I), pr(T4), AF.Sin), waits=[eW])
        eY = rred(T4, T2, PI / 2, [eX])
        eZ = S.add(lambda s: s.activation(pr(A128R), pr(T4), AF.Sin), waits=[eY])
        ea = V.add(lambda v: v.scalar_tensor_tensor(pr(A128R), pr(T1), 1.0, pr(A128R),
                                                    ALU.mult, ALU.mult), waits=[eZ])
        eb = V.add(lambda v: v.scalar_tensor_tensor(pr(A128I), pr(T1), 1.0, pr(A128I),
                                                    ALU.mult, ALU.mult), waits=[ea])
        ev_a128 = eb

        grids = ph.enter_context(nc.sbuf_tensor("p3_grids", [128, 32 * 10], F32))

        def gr(i):
            return grids[:, 32 * i:32 * (i + 1)]

        GLRE, GLIM, GPR, GPI, GIR, GII, G1R, G1I, GT1, GT2 = range(10)
        tmpm = ph.enter_context(nc.sbuf_tensor("p3_tmpm", [128, 32], F32))
        tmpp = ph.enter_context(nc.sbuf_tensor("p3_tmpp", [128, 32], F32))
        tmpq = ph.enter_context(nc.sbuf_tensor("p3_tmpq", [128, 32], F32))
        tki = ph.enter_context(nc.sbuf_tensor("p3_tki", [128, 32], I32))

        bbT = ph.enter_context(nc.sbuf_tensor("p3_bbT", [128, 16 * 64], BF16))
        ctT = ph.enter_context(nc.sbuf_tensor("p3_ctT", [32, 2 * D], BF16))
        cbuf = ph.enter_context(nc.sbuf_tensor("p3_cbuf", [128, 32], F32))
        dbc = ph.enter_context(nc.sbuf_tensor("p3_dbc", [128, D], F32))
        dline = ph.enter_context(nc.sbuf_tensor("p3_dline", [1, D], F32))
        with ExitStack() as pp:
            psg = pp.enter_context(nc.psum_tensor("p3_psg", [128, 512], F32))
            em1 = Tn.add(lambda t: t.matmul(psg[:, 0:32], iota0[:], pr(LRED), start=True, stop=True),
                         waits=[ev_bs, ev_a128])
            em1b = Tn.add(lambda t: t.matmul(psg[:, 32:64], iota0[:], pr(LIMD), start=True, stop=True),
                          waits=[em1])
            em2 = Tn.add(lambda t: t.matmul(psg[:, 64:96], iota1[:], pr(LRED), start=True, stop=True),
                         waits=[em1b])
            em2b = Tn.add(lambda t: t.matmul(psg[:, 96:128], iota1[:], pr(LIMD), start=True, stop=True),
                          waits=[em2])
            ec1 = V.add(lambda v: v.tensor_copy(gr(GLRE), psg[:, 0:32]), waits=[em2b])
            ec2 = V.add(lambda v: v.tensor_copy(gr(GLIM), psg[:, 32:64]), waits=[ec1])
            ec3 = V.add(lambda v: v.tensor_copy(gr(GT1), psg[:, 64:96]), waits=[ec2])
            ec4 = V.add(lambda v: v.tensor_copy(gr(GT2), psg[:, 96:128]), waits=[ec3])

            def grred(dst, src_ap, off, wl):
                w1 = V.add(lambda v: v.tensor_scalar(tmpq[:], src_ap, off, 1.0 / (2 * PI),
                                                     ALU.add, ALU.mult), waits=wl)
                w2 = V.add(lambda v: v.tensor_copy(tki[:], tmpq[:]), waits=[w1])
                w3 = V.add(lambda v: v.tensor_copy(tmpq[:], tki[:]), waits=[w2])
                w4 = V.add(lambda v: v.scalar_tensor_tensor(tmpq[:], tmpq[:], -2 * PI, src_ap,
                                                            ALU.mult, ALU.add), waits=[w3])
                w5 = V.add(lambda v: v.tensor_scalar_add(dst, tmpq[:], off), waits=[w4])
                f1 = V.add(lambda v: v.tensor_scalar(tmpq[:], dst, PI, None, ALU.is_gt), waits=[w5])
                f2 = V.add(lambda v: v.scalar_tensor_tensor(dst, tmpq[:], -2 * PI, dst,
                                                            ALU.mult, ALU.add), waits=[f1])
                f3 = V.add(lambda v: v.tensor_scalar(tmpq[:], dst, -PI, None, ALU.is_lt), waits=[f2])
                f4 = V.add(lambda v: v.scalar_tensor_tensor(dst, tmpq[:], 2 * PI, dst,
                                                            ALU.mult, ALU.add), waits=[f3])
                return f4

            def build_grid(mag_slot, ph_slot, dr, di, wl):
                wm = S.add(lambda s: s.activation(tmpm[:], gr(mag_slot), AF.Exp), waits=wl)
                wp = grred(tmpp[:], gr(ph_slot), 0.0, [wm])
                ws = S.add(lambda s: s.activation(gr(di), tmpp[:], AF.Sin), waits=[wp])
                wp2 = grred(tmpp[:], gr(ph_slot), PI / 2, [ws])
                wc = S.add(lambda s: s.activation(gr(dr), tmpp[:], AF.Sin), waits=[wp2])
                w1 = V.add(lambda v: v.tensor_mul(gr(dr), tmpm[:], gr(dr)), waits=[wc])
                w2 = V.add(lambda v: v.tensor_mul(gr(di), tmpm[:], gr(di)), waits=[w1])
                return w2

            ev_gp = build_grid(GLRE, GLIM, GPR, GPI, [ec4])
            ev_g1 = build_grid(GT1, GT2, G1R, G1I, [ev_gp])
            wm = S.add(lambda s: s.activation(tmpm[:], gr(GLRE), AF.Exp, scale=-1.0), waits=[ev_g1])
            wp = grred(tmpp[:], gr(GLIM), 0.0, [wm])
            wsn = S.add(lambda s: s.activation(tmpq[:], tmpp[:], AF.Sin), waits=[wp])
            wii = V.add(lambda v: v.scalar_tensor_tensor(gr(GII), tmpq[:], -1.0, tmpm[:],
                                                         ALU.mult, ALU.mult), waits=[wsn])
            wp2 = grred(tmpp[:], gr(GLIM), PI / 2, [wii])
            wcs = S.add(lambda s: s.activation(tmpq[:], tmpp[:], AF.Sin), waits=[wp2])
            wir = V.add(lambda v: v.tensor_mul(gr(GIR), tmpq[:], tmpm[:]), waits=[wcs])
            ev_grids = wir

            brebuf = pp.enter_context(nc.sbuf_tensor("p3_breb", [32, D], F32))
            bimbuf = pp.enter_context(nc.sbuf_tensor("p3_bimb", [32, D], F32))
            bsr_b = pp.enter_context(nc.sbuf_tensor("p3_bsrb", [128, 32], F32))
            bsi_b = pp.enter_context(nc.sbuf_tensor("p3_bsib", [128, 32], F32))
            psb0 = pp.enter_context(nc.psum_tensor("p3_psb0", [128, 128], F32))
            evb1 = Q.add(lambda q: q.dma_start(out=brebuf[:], in_=bre[:]))
            evb2 = Q.add(lambda q: q.dma_start(out=bimbuf[:], in_=bim[:]))
            emb = Tn.add(lambda t: t.matmul(psg[:, 128:160], ones1[:], pr(BSR), start=True, stop=True),
                         waits=[ev_bs, ev_grids])
            emb2 = Tn.add(lambda t: t.matmul(psg[:, 160:192], ones1[:], pr(BSI), start=True, stop=True),
                          waits=[emb])
            evbb1 = V.add(lambda v: v.tensor_copy(bsr_b[:], psg[:, 128:160]), waits=[emb2])
            evbb2 = V.add(lambda v: v.tensor_copy(bsi_b[:], psg[:, 160:192]), waits=[evbb1])
            ev_bbT = None
            for dt in range(DT):
                et1 = Tn.add(lambda t, dt=dt: t.transpose(
                    psb0[:, 0:32], brebuf[:, 128 * dt:128 * (dt + 1)], id_f[0:32, 0:32]),
                    waits=[evb1, evbb2] if dt == 0 else [ev_bbT])
                ev1 = V.add(lambda v: v.tensor_copy(tmpm[:], psb0[:, 0:32]), waits=[et1])
                et2 = Tn.add(lambda t, dt=dt: t.transpose(
                    psb0[:, 32:64], bimbuf[:, 128 * dt:128 * (dt + 1)], id_f[0:32, 0:32]),
                    waits=[ev1, evb2])
                ev2 = V.add(lambda v: v.tensor_copy(tmpp[:], psb0[:, 32:64]), waits=[et2])
                ev3 = V.add(lambda v: v.tensor_mul(tmpq[:], bsr_b[:], tmpm[:]), waits=[ev2])
                ev4 = V.add(lambda v: v.scalar_tensor_tensor(
                    gr(GT1), bsi_b[:], 1.0, tmpp[:], ALU.mult, ALU.mult), waits=[ev3])
                ev5 = V.add(lambda v: v.tensor_sub(tmpq[:], tmpq[:], gr(GT1)), waits=[ev4])
                ev6 = V.add(lambda v, dt=dt: v.tensor_copy(bbT[:, 64 * dt:64 * dt + 32], tmpq[:]),
                            waits=[ev5])
                ev7 = V.add(lambda v: v.tensor_mul(tmpq[:], bsr_b[:], tmpp[:]), waits=[ev6])
                ev8 = V.add(lambda v: v.scalar_tensor_tensor(
                    gr(GT1), bsi_b[:], 1.0, tmpm[:], ALU.mult, ALU.mult), waits=[ev7])
                ev9 = V.add(lambda v: v.tensor_add(tmpq[:], tmpq[:], gr(GT1)), waits=[ev8])
                ev_bbT = V.add(lambda v, dt=dt: v.tensor_copy(bbT[:, 64 * dt + 32:64 * dt + 64], tmpq[:]),
                               waits=[ev9])

            ev_ctT = None
            for half, (csrc, sgn) in enumerate([(cre, 2.0), (cim, -2.0)]):
                for dt in range(DT):
                    evc = Q.add(lambda q, csrc=csrc, dt=dt: q.dma_start(
                        out=cbuf[:], in_=csrc[128 * dt:128 * (dt + 1), :]),
                        waits=[ev_ctT])
                    etc = Tn.add(lambda t: t.transpose(psb0[0:32, :], cbuf[:], id_f[:]),
                                 waits=[evc, ev_ctT, ev_bbT])
                    ev_ctT = V.add(lambda v, half=half, dt=dt, sgn=sgn: v.tensor_scalar_mul(
                        ctT[:, D * half + 128 * dt: D * half + 128 * (dt + 1)],
                        psb0[0:32, :], sgn), waits=[etc])

            evd = Q.add(lambda q: q.dma_start(out=dline[:], in_=dvec[:]))
            ev_dbc = None
            for nch in range(4):
                emd = Tn.add(lambda t, nch=nch: t.matmul(
                    psg[:, 0:512], ones1[:], dline[:, 512 * nch:512 * (nch + 1)],
                    start=True, stop=True), waits=[evd, ev_dbc, ec4, evbb2, ev_bbT])
                ev_dbc = V.add(lambda v, nch=nch: v.tensor_copy(dbc[:, 512 * nch:512 * (nch + 1)],
                                                                psg[:, 0:512]), waits=[emd])
            p.barrier()
        if p3sub == "prep":
            return

        x2b = [ph.enter_context(nc.sbuf_tensor(f"p3_x2b{i}", [128, D], F32)) for i in range(2)]
        ulnb = [ph.enter_context(nc.sbuf_tensor(f"p3_ulnb{i}", [128, D], BF16)) for i in range(2)]
        ulnT = ph.enter_context(nc.sbuf_tensor("p3_ulnT", [128, 16 * 128], BF16))
        buR = ph.enter_context(nc.sbuf_tensor("p3_buR", [128, 32 * 32], F32))
        buI = ph.enter_context(nc.sbuf_tensor("p3_buI", [128, 32 * 32], F32))
        lt3 = mk_ln_tmp(nc, ph, "p3", env["epsc"][:, 0:1])
        with ExitStack() as pp:
            psu = [pp.enter_context(nc.psum_tensor(f"p3_psu{i}", [128, 1024], BF16)) for i in range(2)]
            psbu = pp.enter_context(nc.psum_tensor("p3_psbu", [128, 64], F32))
            x2_ld = [None] * 32
            x2_free = [None, None]
            psu_free = [None, None]
            ulnT_free = None
            ulnb_free = [None, None]
            ev_bu_all = None
            prev_tr = None

            def load_x2(i):
                sl = i % 2
                x2_ld[i] = Q.add(lambda q, i=i, sl=sl: q.dma_start(
                    out=x2b[sl][:], in_=x2s[128 * i:128 * (i + 1), :]), waits=[x2_free[sl]],
                    slot=f"x2b{sl}")

            load_x2(0)
            for i in range(32):
                if i + 1 < 32:
                    load_x2(i + 1)
                sl = i % 2
                V.pending.append(x2_ld[i])
                if ulnb_free[sl] is not None:
                    V.pending.append(ulnb_free[sl])
                ev_ln = ln_tile(p, x2b[sl][:], ulnb[sl], lt3)
                x2_free[sl] = ev_ln
                uln_st = G.add(lambda q, i=i, sl=sl: q.dma_start(
                    out=ulnSc[128 * i:128 * (i + 1), :], in_=ulnb[sl][:]), waits=[ev_ln],
                    slot="ulnst")
                ev_trl = None
                for half in range(2):
                    pb = half
                    evt = None
                    for k in range(8):
                        dt = 8 * half + k
                        evt = Tn.add(lambda t, pb=pb, k=k, dt=dt, sl=sl: t.transpose(
                            psu[pb][:, 128 * k:128 * (k + 1)],
                            ulnb[sl][:, 128 * dt:128 * (dt + 1)], id_bf[:]),
                            waits=([ev_ln, psu_free[pb]] if k == 0 else []), inc=(k == 7))
                        prev_tr = evt
                    ev_trl = S.add(lambda s, pb=pb, half=half: s.copy(
                        ulnT[:, 1024 * half:1024 * (half + 1)], psu[pb][:]),
                        waits=[evt, ulnT_free])
                    psu_free[pb] = ev_trl
                ulnb_free[sl] = prev_tr
                V.pending.append(uln_st)
                ev_mm = None
                for dt in range(DT):
                    ev_mm = Tn.add(lambda t, dt=dt: t.matmul(
                        psbu[:], ulnT[:, 128 * dt:128 * (dt + 1)],
                        bbT[:, 64 * dt:64 * (dt + 1)],
                        start=(dt == 0), stop=(dt == DT - 1)),
                        waits=([ev_trl, ev_bbT, ev_bu_all] if dt == 0 else []),
                        inc=(dt == DT - 1))
                ulnT_free = ev_mm
                if "dq" in env:
                    env["dq"].pump(2)
                evr = V.add(lambda v, i=i: v.tensor_copy(buR[:, 32 * i:32 * (i + 1)], psbu[:, 0:32]),
                            waits=[ev_mm])
                ev_bu_all = V.add(lambda v, i=i: v.tensor_copy(buI[:, 32 * i:32 * (i + 1)],
                                                               psbu[:, 32:64]), waits=[evr])
            p.barrier()
        if p3sub == "a":
            return

        xlR = ph.enter_context(nc.sbuf_tensor("p3_xlR", [128, 32 * 32], F32))
        xlI = ph.enter_context(nc.sbuf_tensor("p3_xlI", [128, 32 * 32], F32))
        cbR = ph.enter_context(nc.sbuf_tensor("p3_cbR", [128, 32], F32))
        cbI = ph.enter_context(nc.sbuf_tensor("p3_cbI", [128, 32], F32))
        srowR = ph.enter_context(nc.sbuf_tensor("p3_srowR", [1, 32 * 32], F32))
        srowI = ph.enter_context(nc.sbuf_tensor("p3_srowI", [1, 32 * 32], F32))
        krowR = ph.enter_context(nc.sbuf_tensor("p3_krowR", [1, 32 * 32], F32))
        krowI = ph.enter_context(nc.sbuf_tensor("p3_krowI", [1, 32 * 32], F32))
        kcR = ph.enter_context(nc.sbuf_tensor("p3_kcR", [1, 32], F32))
        kcI = ph.enter_context(nc.sbuf_tensor("p3_kcI", [1, 32], F32))
        xsTR = ph.enter_context(nc.sbuf_tensor("p3_xsTR", [32, T], BF16))
        xsTI = ph.enter_context(nc.sbuf_tensor("p3_xsTI", [32, T], BF16))
        with ExitStack() as pp:
            pscan = pp.enter_context(nc.psum_tensor("p3_pscan", [128, 64], F32))
            psrow = pp.enter_context(nc.psum_tensor("p3_psrow", [1, 64], F32))
            psb = pp.enter_context(nc.psum_tensor("p3_psb", [128, 128], F32))
            ev_srow = [None] * 32
            ev_xl = [None] * 32
            pscan_free = None
            psrow_free = None
            ev_cum_prev = None
            for ci in range(32):
                bR = buR[:, 32 * ci:32 * (ci + 1)]
                bI = buI[:, 32 * ci:32 * (ci + 1)]
                w1 = V.add(lambda v, bR=bR: v.tensor_mul(cbR[:], gr(GIR), bR), waits=[ev_cum_prev])
                w2 = V.add(lambda v, bI=bI: v.scalar_tensor_tensor(tmpq[:], gr(GII), 1.0, bI,
                                                                   ALU.mult, ALU.mult), waits=[w1])
                w3 = V.add(lambda v: v.tensor_sub(cbR[:], cbR[:], tmpq[:]), waits=[w2])
                w4 = V.add(lambda v, bI=bI: v.tensor_mul(cbI[:], gr(GIR), bI), waits=[w3])
                w5 = V.add(lambda v, bR=bR: v.scalar_tensor_tensor(tmpq[:], gr(GII), 1.0, bR,
                                                                   ALU.mult, ALU.mult), waits=[w4])
                w6 = V.add(lambda v: v.tensor_add(cbI[:], cbI[:], tmpq[:]), waits=[w5])
                m1 = Tn.add(lambda t: t.matmul(pscan[:, 0:32], u128[:], cbR[:], start=True, stop=True),
                            waits=[w6, pscan_free])
                m2 = Tn.add(lambda t: t.matmul(pscan[:, 32:64], u128[:], cbI[:], start=True, stop=True),
                            waits=[m1])
                ev_cum_prev = m2
                w7 = V.add(lambda v, ci=ci: v.tensor_mul(xlR[:, 32 * ci:32 * (ci + 1)],
                                                         gr(GPR), pscan[:, 0:32]), waits=[m2])
                w8 = V.add(lambda v: v.scalar_tensor_tensor(
                    tmpq[:], gr(GPI), 1.0, pscan[:, 32:64], ALU.mult, ALU.mult), waits=[w7])
                w9 = V.add(lambda v, ci=ci: v.tensor_sub(xlR[:, 32 * ci:32 * (ci + 1)],
                                                         xlR[:, 32 * ci:32 * (ci + 1)], tmpq[:]),
                           waits=[w8])
                wA = V.add(lambda v, ci=ci: v.tensor_mul(xlI[:, 32 * ci:32 * (ci + 1)],
                                                         gr(GPI), pscan[:, 0:32]), waits=[w9])
                wB = V.add(lambda v: v.scalar_tensor_tensor(
                    tmpq[:], gr(GPR), 1.0, pscan[:, 32:64], ALU.mult, ALU.mult), waits=[wA])
                wC = V.add(lambda v, ci=ci: v.tensor_add(xlI[:, 32 * ci:32 * (ci + 1)],
                                                         xlI[:, 32 * ci:32 * (ci + 1)], tmpq[:]),
                           waits=[wB])
                pscan_free = wC
                m3 = Tn.add(lambda t, ci=ci: t.matmul(psrow[:, 0:32], e127[:],
                                                      xlR[:, 32 * ci:32 * (ci + 1)],
                                                      start=True, stop=True), waits=[wC, psrow_free])
                m4 = Tn.add(lambda t, ci=ci: t.matmul(psrow[:, 32:64], e127[:],
                                                      xlI[:, 32 * ci:32 * (ci + 1)],
                                                      start=True, stop=True), waits=[m3])
                s1 = V.add(lambda v, ci=ci: v.tensor_copy(srowR[:, 32 * ci:32 * (ci + 1)],
                                                          psrow[:, 0:32]), waits=[m4])
                s2 = V.add(lambda v, ci=ci: v.tensor_copy(srowI[:, 32 * ci:32 * (ci + 1)],
                                                          psrow[:, 32:64]), waits=[s1])
                psrow_free = s2
                ev_srow[ci] = s2
                ev_xl[ci] = wC
                if "dq" in env:
                    env["dq"].pump(1)
            k1 = V.add(lambda v: v.tensor_copy(kcR[:], srowR[:, 0:32]), waits=[ev_srow[0]])
            k2 = V.add(lambda v: v.tensor_copy(kcI[:], srowI[:, 0:32]), waits=[k1])
            k3 = V.add(lambda v: v.tensor_copy(krowR[:, 0:32], kcR[:]), waits=[k2])
            kl = V.add(lambda v: v.tensor_copy(krowI[:, 0:32], kcI[:]), waits=[k3])
            for ci in range(1, 32):
                c1 = V.add(lambda v: v.tensor_mul(tmpq[0:1, :], pr(A128R), kcR[:]),
                           waits=[kl, ev_srow[ci]])
                c2 = V.add(lambda v: v.scalar_tensor_tensor(tmpm[0:1, :], pr(A128I), 1.0, kcI[:],
                                                            ALU.mult, ALU.mult), waits=[c1])
                c3 = V.add(lambda v: v.tensor_sub(tmpq[0:1, :], tmpq[0:1, :], tmpm[0:1, :]),
                           waits=[c2])
                c4 = V.add(lambda v, ci=ci: v.tensor_add(tmpq[0:1, :], tmpq[0:1, :],
                                                         srowR[:, 32 * ci:32 * (ci + 1)]), waits=[c3])
                c5 = V.add(lambda v: v.tensor_mul(tmpm[0:1, :], pr(A128R), kcI[:]), waits=[c4])
                c6 = V.add(lambda v: v.scalar_tensor_tensor(tmpp[0:1, :], pr(A128I), 1.0, kcR[:],
                                                            ALU.mult, ALU.mult), waits=[c5])
                c7 = V.add(lambda v: v.tensor_add(tmpm[0:1, :], tmpm[0:1, :], tmpp[0:1, :]),
                           waits=[c6])
                c8 = V.add(lambda v, ci=ci: v.tensor_add(kcI[:], tmpm[0:1, :],
                                                         srowI[:, 32 * ci:32 * (ci + 1)]), waits=[c7])
                c9 = V.add(lambda v: v.tensor_copy(kcR[:], tmpq[0:1, :]), waits=[c8])
                cA = V.add(lambda v, ci=ci: v.tensor_copy(krowR[:, 32 * ci:32 * (ci + 1)], kcR[:]),
                           waits=[c9])
                kl = V.add(lambda v, ci=ci: v.tensor_copy(krowI[:, 32 * ci:32 * (ci + 1)], kcI[:]),
                           waits=[cA])
            ev_xsT_last = None
            pscan_free = kl
            for ci in range(32):
                if ci > 0:
                    mb1 = Tn.add(lambda t, ci=ci: t.matmul(
                        pscan[:, 0:32], ones1[:], krowR[:, 32 * (ci - 1):32 * ci],
                        start=True, stop=True), waits=[kl, pscan_free])
                    mb2 = Tn.add(lambda t, ci=ci: t.matmul(
                        pscan[:, 32:64], ones1[:], krowI[:, 32 * (ci - 1):32 * ci],
                        start=True, stop=True), waits=[mb1])
                    a1 = V.add(lambda v: v.tensor_mul(tmpq[:], gr(G1R), pscan[:, 0:32]), waits=[mb2])
                    a2 = V.add(lambda v: v.scalar_tensor_tensor(tmpm[:], gr(G1I), 1.0, pscan[:, 32:64],
                                                                ALU.mult, ALU.mult), waits=[a1])
                    a3 = V.add(lambda v: v.tensor_sub(tmpq[:], tmpq[:], tmpm[:]), waits=[a2])
                    a4 = V.add(lambda v, ci=ci: v.tensor_add(xlR[:, 32 * ci:32 * (ci + 1)],
                                                             xlR[:, 32 * ci:32 * (ci + 1)], tmpq[:]),
                               waits=[a3])
                    a5 = V.add(lambda v: v.tensor_mul(tmpq[:], gr(G1I), pscan[:, 0:32]), waits=[a4])
                    a6 = V.add(lambda v: v.scalar_tensor_tensor(tmpm[:], gr(G1R), 1.0, pscan[:, 32:64],
                                                                ALU.mult, ALU.mult), waits=[a5])
                    a7 = V.add(lambda v: v.tensor_add(tmpq[:], tmpq[:], tmpm[:]), waits=[a6])
                    a8 = V.add(lambda v, ci=ci: v.tensor_add(xlI[:, 32 * ci:32 * (ci + 1)],
                                                             xlI[:, 32 * ci:32 * (ci + 1)], tmpq[:]),
                               waits=[a7])
                    pscan_free = a8
                    ev_apply = a8
                else:
                    ev_apply = kl
                t1 = Tn.add(lambda t, ci=ci: t.transpose(
                    psb[0:32, :], xlR[:, 32 * ci:32 * (ci + 1)], id_f[:]),
                    waits=[ev_apply, ev_xl[ci]])
                e1_ = V.add(lambda v, ci=ci: v.tensor_copy(xsTR[:, 128 * ci:128 * (ci + 1)],
                                                           psb[0:32, :]), waits=[t1])
                t2 = Tn.add(lambda t, ci=ci: t.transpose(
                    psb[0:32, :], xlI[:, 32 * ci:32 * (ci + 1)], id_f[:]), waits=[e1_])
                ev_xsT_last = V.add(lambda v, ci=ci: v.tensor_copy(xsTI[:, 128 * ci:128 * (ci + 1)],
                                                                   psb[0:32, :]), waits=[t2])
            p.barrier()
        if p3sub == "b":
            return

        x3t = [ph.enter_context(nc.sbuf_tensor(f"p3_x3t{i}", [128, D], F32)) for i in range(2)]
        ulnb2 = [ph.enter_context(nc.sbuf_tensor(f"p3_ulnb2{i}", [128, D], BF16)) for i in range(2)]
        xln4b = ph.enter_context(nc.sbuf_tensor("p3_xln4", [128, D], BF16))
        stg3 = ph.enter_context(nc.sbuf_tensor("p3_stg3", [128, 16 * 128], BF16))
        lt4 = mk_ln_tmp(nc, ph, "p3c", env["epsc"][:, 0:1])
        with ExitStack() as pp:
            psy = pp.enter_context(nc.psum_tensor("p3_psy", [128, 2048], F32))
            psu = [pp.enter_context(nc.psum_tensor(f"p3c_psu{i}", [128, 1024], BF16)) for i in range(2)]
            psy_free = None
            stg3_free = None
            x3t_free = [None, None]
            psu_free = [None, None]
            x2_free = [None, None]
            uln_free = [None, None]
            prev_tr = None
            for i in range(32):
                sl = i % 2
                ev_x2l = Q.add(lambda q, i=i, sl=sl: q.dma_start(
                    out=x2b[sl][:], in_=x2s[128 * i:128 * (i + 1), :]), waits=[x2_free[sl]],
                    slot=f"x2b{sl}")
                ev_ul = G.add(lambda q, i=i, sl=sl: q.dma_start(
                    out=ulnb2[sl][:], in_=ulnSc[128 * i:128 * (i + 1), :]), waits=[uln_free[sl]],
                    slot=f"uln2{sl}")
                ev_mm = None
                for nch in range(4):
                    Tn.add(lambda t, i=i, nch=nch: t.matmul(
                        psy[:, 512 * nch:512 * (nch + 1)], xsTR[:, 128 * i:128 * (i + 1)],
                        ctT[:, 512 * nch:512 * (nch + 1)], start=True, stop=False),
                        waits=([ev_xsT_last, ev_ctT, psy_free] if nch == 0 else []), inc=False)
                    ev_mm = Tn.add(lambda t, i=i, nch=nch: t.matmul(
                        psy[:, 512 * nch:512 * (nch + 1)], xsTI[:, 128 * i:128 * (i + 1)],
                        ctT[:, D + 512 * nch: D + 512 * (nch + 1)], start=False, stop=True),
                        inc=(nch == 3))
                ev_du = V.add(lambda v, i=i, sl=sl: v.tensor_mul(x3t[sl][:], dbc[:], ulnb2[sl][:]),
                              waits=[ev_mm, ev_dbc, ev_ul, x3t_free[sl]])
                uln_free[sl] = ev_du
                ev_y1 = V.add(lambda v, sl=sl: v.tensor_add(x3t[sl][:], x3t[sl][:], psy[:]),
                              waits=[ev_du])
                psy_free = ev_y1
                ev_y2 = V.add(lambda v, sl=sl: v.tensor_add(x3t[sl][:], x3t[sl][:], x2b[sl][:]),
                              waits=[ev_y1, ev_x2l])
                x2_free[sl] = ev_y2
                ev_st = G.add(lambda q, i=i, sl=sl: q.dma_start(
                    out=x3s[128 * i:128 * (i + 1), :], in_=x3t[sl][:]), waits=[ev_y2],
                    slot=f"x3st{sl}")
                if prev_tr is not None:
                    V.pending.append(prev_tr)
                ev_ln = ln_tile(p, x3t[sl], xln4b, lt4, waits=[ev_y2])
                x3t_free[sl] = ev_ln
                V.pending.append(ev_st)
                ev_stg3 = None
                for half in range(2):
                    pb = half
                    evt = None
                    for k in range(8):
                        dt = 8 * half + k
                        evt = Tn.add(lambda t, pb=pb, k=k, dt=dt: t.transpose(
                            psu[pb][:, 128 * k:128 * (k + 1)],
                            xln4b[:, 128 * dt:128 * (dt + 1)], id_bf[:]),
                            waits=([ev_ln, psu_free[pb]] if k == 0 else []), inc=(k == 7))
                        prev_tr = evt
                    ev_stg3 = S.add(lambda s, pb=pb, half=half: s.copy(
                        stg3[:, 1024 * half:1024 * (half + 1)], psu[pb][:]),
                        waits=[evt, stg3_free])
                    psu_free[pb] = ev_stg3
                ev_st3 = G.add(lambda q, i=i: q.dma_start(
                    out=xln4Ts[:, 128 * i:128 * (i + 1)].rearrange("(a p) n -> p a n", p=128),
                    in_=stg3.ap().rearrange("p (a n) -> p a n", a=16)), waits=[ev_stg3], slot="stg3")
                stg3_free = ev_st3
                if "dq" in env:
                    env["dq"].pump(1)
            p.barrier()




class DequantPump:

    def __init__(self, nc, p, es, env):
        self.nc, self.p, self.env = nc, p, env
        Tn, V, S, Q, G = p.T, p.V, p.S, p.Q, p.G
        id_f = env["id_f"]
        gup_sc, dn_sc = env["gup_sc"], env["dn_sc"]
        self.es = es
        self.sct = es.enter_context(nc.sbuf_tensor("dq_sct", [128, NO1 * 16 + 16 * 44], F32))
        self.psd = [es.enter_context(nc.psum_tensor(f"dq_psd{i}", [128, 1024], BF16)) for i in range(2)]
        self.pk_sb = [es.enter_context(nc.sbuf_tensor(f"dq_pk{i}", [128, 704], I32)) for i in range(2)]
        self.nib = es.enter_context(nc.sbuf_tensor("dq_nib", [128, I_FFN1], I32))
        self.dq1 = es.enter_context(nc.sbuf_tensor("dq_dq1", [128, I_FFN1], I32))
        self.dq2 = es.enter_context(nc.sbuf_tensor("dq_dq2", [128, I_FFN1], I32))
        self.dq3 = es.enter_context(nc.sbuf_tensor("dq_dq3", [128, I_FFN1], I32))
        self.wbf = es.enter_context(nc.sbuf_tensor("dq_wbf", [128, I_FFN1], BF16))
        self.stg4 = [es.enter_context(nc.sbuf_tensor(f"dq_stg4{i}", [128, 1024], BF16)) for i in range(2)]
        self.state = {"pk_free": [None, None], "psd_free": [None, None],
                      "stg4_free": [None, None], "wbf_free": None, "nstg": 0, "npk": 0}
        with ExitStack() as pp:
            scbuf = pp.enter_context(nc.sbuf_tensor("dq_scbuf", [44, 2 * I_FFN1], F32))
            psf = pp.enter_context(nc.psum_tensor("dq_psf", [128, 128], F32))
            evs1 = G.add(lambda q: q.dma_start(out=scbuf[0:16, 0:2 * I_FFN1], in_=gup_sc[:]))
            ev_sct = None
            for ot in range(NO1):
                etc = Tn.add(lambda t, ot=ot: t.transpose(
                    psf[:, 0:16], scbuf[0:16, 128 * ot:128 * (ot + 1)], id_f[0:16, 0:16]),
                    waits=[evs1] if ot == 0 else [ev_sct])
                ev_sct = V.add(lambda v, ot=ot: v.tensor_copy(
                    self.sct[:, 16 * ot:16 * (ot + 1)], psf[:, 0:16]), waits=[etc])
            evs2 = G.add(lambda q: q.dma_start(out=scbuf[0:44, 0:D], in_=dn_sc[:]), waits=[ev_sct])
            DSOFF = NO1 * 16
            for ht in range(16):
                etc = Tn.add(lambda t, ht=ht: t.transpose(
                    psf[:, 0:44], scbuf[0:44, 128 * ht:128 * (ht + 1)], id_f[0:44, 0:44]),
                    waits=[evs2] if ht == 0 else [ev_sct])
                ev_sct = V.add(lambda v, ht=ht: v.tensor_copy(
                    self.sct[:, DSOFF + 44 * ht:DSOFF + 44 * (ht + 1)], psf[:, 0:44]),
                    waits=[etc])
            self.ev_sct = ev_sct
            self.p.barrier()
        self.work = []
        for ot in range(NO1):
            self.work.append((env["gup_pk"], ot, 0, 256, 16 * ot, env["wguTs"], 0))
        DSOFF = NO1 * 16
        for ht in range(16):
            self.work.append((env["dn_pk"], ht, 0, 704, DSOFF + 44 * ht, env["wdnTs"], 0))
        self.cursor = 0

    def pump(self, n=1):
        for _ in range(n):
            if self.cursor >= len(self.work):
                return
            self._emit(self.work[self.cursor])
            self.cursor += 1

    def drain(self):
        self.pump(len(self.work))

    def _emit(self, item):
        nc, p = self.nc, self.p
        Tn, V, S, Q, G = p.T, p.V, p.S, p.Q, p.G
        id_bf = self.env["id_bf"]
        st = self.state
        src_pk, ot, w0, nwords, scoff, dstT, rowbase = item
        nh = 8 * nwords
        nsc = nh // 128
        sl = st["npk"] % 2
        st["npk"] += 1
        evp = G.add(lambda q, ot=ot, sl=sl, src_pk=src_pk, nwords=nwords: q.dma_start(
            out=self.pk_sb[sl][:, 0:nwords], in_=src_pk[128 * ot:128 * (ot + 1), :]),
            waits=[st["pk_free"][sl]], slot=f"dqpk{sl}")
        evn = None
        for j in range(8):
            evn = V.add(lambda v, j=j, sl=sl, nwords=nwords, nh=nh: v.tensor_scalar(
                self.nib[:, j:nh:8], self.pk_sb[sl][:, 0:nwords], 4 * j, 15,
                ALU.logical_shift_right, ALU.bitwise_and),
                waits=([evp, self.ev_sct] if j == 0 else []), inc=(j == 7))
        st["pk_free"][sl] = evn
        w1 = V.add(lambda v, nh=nh: v.tensor_scalar(self.dq1[:, 0:nh], self.nib[:, 0:nh], 7, None,
                                                    ALU.bitwise_and), waits=[evn])
        w2 = V.add(lambda v, nh=nh: v.tensor_scalar(self.dq2[:, 0:nh], self.nib[:, 0:nh], 3, None,
                                                    ALU.logical_shift_right), waits=[w1])
        w3 = V.add(lambda v, nh=nh: v.tensor_scalar(self.dq3[:, 0:nh], self.dq1[:, 0:nh], 0x400000,
                                                    None, ALU.mult), waits=[w2])
        w4 = V.add(lambda v, nh=nh: v.tensor_scalar(self.nib[:, 0:nh], self.dq1[:, 0:nh], 1, None,
                                                    ALU.is_ge), waits=[w3])
        w5 = V.add(lambda v, nh=nh: v.scalar_tensor_tensor(self.dq3[:, 0:nh], self.nib[:, 0:nh],
                                                           float(0x3EC00000), self.dq3[:, 0:nh],
                                                           ALU.mult, ALU.add), waits=[w4])
        w6 = V.add(lambda v, nh=nh: v.tensor_scalar(self.nib[:, 0:nh], self.dq1[:, 0:nh], 2, None,
                                                    ALU.is_ge), waits=[w5])
        w7 = V.add(lambda v, nh=nh: v.scalar_tensor_tensor(self.dq3[:, 0:nh], self.nib[:, 0:nh],
                                                           float(0x400000), self.dq3[:, 0:nh],
                                                           ALU.mult, ALU.add), waits=[w6])
        w8 = V.add(lambda v, nh=nh: v.scalar_tensor_tensor(self.dq3[:, 0:nh], self.dq2[:, 0:nh],
                                                           float(-2147483648), self.dq3[:, 0:nh],
                                                           ALU.mult, ALU.add), waits=[w7])
        evw = V.add(lambda v, scoff=scoff, nsc=nsc, nh=nh: v.tensor_tensor(
            self.wbf[:, 0:nh].rearrange("p (g n) -> p g n", n=128),
            self.dq3[:, 0:nh].bitcast(F32).rearrange("p (g n) -> p g n", n=128),
            bcast_last(self.sct[:, scoff:scoff + nsc].rearrange("p (g o) -> p g o", o=1), 128),
            ALU.mult), waits=[w8, st["wbf_free"]])
        last_tr = None
        g0 = 0
        while g0 < nsc:
            cnt = min(8, nsc - g0)
            pb = (g0 // 8) % 2
            evt = None
            for k in range(cnt):
                gi = g0 + k
                evt = Tn.add(lambda t, pb=pb, k=k, gi=gi: t.transpose(
                    self.psd[pb][:, 128 * k:128 * (k + 1)], self.wbf[:, 128 * gi:128 * (gi + 1)],
                    id_bf[:]),
                    waits=([evw, st["psd_free"][pb]] if k == 0 else []),
                    inc=(k == cnt - 1))
            last_tr = evt
            sg = st["nstg"] % 2
            st["nstg"] += 1
            evc = S.add(lambda s, pb=pb, sg=sg, cnt=cnt: s.copy(
                self.stg4[sg][:, 0:128 * cnt], self.psd[pb][:, 0:128 * cnt]),
                waits=[evt, st["stg4_free"][sg]])
            st["psd_free"][pb] = evc
            ev_last = Q.add(lambda q, g0=g0, cnt=cnt, ot=ot, sg=sg, dstT=dstT,
                            rowbase=rowbase: q.dma_start(
                out=dstT[rowbase + 128 * g0:rowbase + 128 * (g0 + cnt),
                         128 * ot:128 * (ot + 1)].rearrange("(a p) n -> p a n", p=128),
                in_=self.stg4[sg][:, 0:128 * cnt].rearrange("p (a n) -> p a n", a=cnt)),
                waits=[evc], slot=f"dqst{sg}")
            st["stg4_free"][sg] = ev_last
            g0 += cnt
        st["wbf_free"] = last_tr



def p4_emit(nc, p, es, env):
    Tn, V, S, Q, G = p.T, p.V, p.S, p.Q, p.G
    x3s, xln4Ts, out = env["x3s"], env["xln4Ts"], env["out"]
    gup_pk, gup_sc, dn_pk, dn_sc = env["gup_pk"], env["gup_sc"], env["dn_pk"], env["dn_sc"]
    wguTs, wdnTs = env["wguTs"], env["wdnTs"]
    id_bf, id_f = env["id_bf"], env["id_f"]


    with ExitStack() as ph:
        xlnTb = [ph.enter_context(nc.sbuf_tensor(f"p4_xlnT{i}", [128, 16 * 512], BF16)) for i in range(2)]
        wgu_sb = [ph.enter_context(nc.sbuf_tensor(f"p4_wgu{i}", [128, 16 * 128], BF16)) for i in range(4)]
        wdn_sb = [ph.enter_context(nc.sbuf_tensor(f"p4_wdn{i}", [128, 44 * 128], BF16)) for i in range(2)]
        gate_sb = ph.enter_context(nc.sbuf_tensor("p4_gate", [128, 512], BF16))
        fuse = ph.enter_context(nc.sbuf_tensor("p4_fuse", [128, 44 * 512], BF16))
        yT = ph.enter_context(nc.sbuf_tensor("p4_yT", [128, 16 * 512], F32))
        ynat = ph.enter_context(nc.sbuf_tensor("p4_ynat", [128, 4 * 2048], F32))
        ysq = ph.enter_context(nc.sbuf_tensor("p4_ysq", [128, 2048], F32))
        x3b = [ph.enter_context(nc.sbuf_tensor(f"p4_x3b{i}", [128, D], F32)) for i in range(2)]
        rms = ph.enter_context(nc.sbuf_tensor("p4_rms", [128, 4], F32))
        psG = [ph.enter_context(nc.psum_tensor(f"p4_psG{i}", [128, 512], F32)) for i in range(2)]
        psU = [ph.enter_context(nc.psum_tensor(f"p4_psU{i}", [128, 512], F32)) for i in range(2)]
        psY = [ph.enter_context(nc.psum_tensor(f"p4_psY{i}", [128, 512], F32)) for i in range(2)]
        psN = [ph.enter_context(nc.psum_tensor(f"p4_psN{i}", [128, 512], F32)) for i in range(2)]

        xlnT_free = [None, None]
        wgu_free = [None] * 4
        wdn_free = [None, None]
        psG_free = [None, None]
        psU_free = [None, None]
        psY_free = [None, None]
        psN_free = [None, None]
        gate_free = None
        fuse_free = None
        yT_free = None
        ynat_free = None
        out_free = [None, None]
        x3_free = [None, None]
        ev_mm = None

        def load_xln(c):
            sl = c % 2
            return Q.add(lambda q, c=c, sl=sl: q.dma_start(
                out=xlnTb[sl].ap().rearrange("p (a n) -> p a n", a=16),
                in_=xln4Ts[:, 512 * c:512 * (c + 1)].rearrange("(a p) n -> p a n", p=128)),
                waits=[xlnT_free[sl]], slot=f"p4x{sl}")

        def load_wgu(j, slot, which):
            ot = j if which == 0 else j + NI1
            return Q.add(lambda q, ot=ot, slot=slot: q.dma_start(
                out=wgu_sb[slot].ap().rearrange("p (a n) -> p a n", a=16),
                in_=wguTs[:, 128 * ot:128 * (ot + 1)].rearrange("(a p) n -> p a n", p=128)),
                waits=[wgu_free[slot]], slot=f"wgu{slot}")

        def load_wdn(ht, slot):
            return p.A.add(lambda q, ht=ht, slot=slot: q.dma_start(
                out=wdn_sb[slot].ap().rearrange("p (a n) -> p a n", a=44),
                in_=wdnTs[:, 128 * ht:128 * (ht + 1)].rearrange("(a p) n -> p a n", p=128)),
                waits=[wdn_free[slot]], slot=f"wdn{slot}")

        ev_xln = [None] * 8
        ev_xln[0] = load_xln(0)
        for c in range(8):
            if c + 1 < 8:
                ev_xln[c + 1] = load_xln(c + 1)
            sl = c % 2
            evw_g = [None] * NI1
            evw_u = [None] * NI1
            evw_g[0] = load_wgu(0, 0, 0)
            evw_u[0] = load_wgu(0, 1, 1)
            ev_fuse_last = None
            for j in range(NI1):
                if j + 1 < NI1:
                    evw_g[j + 1] = load_wgu(j + 1, (2 * (j + 1)) % 4, 0)
                    evw_u[j + 1] = load_wgu(j + 1, (2 * (j + 1) + 1) % 4, 1)
                gslot = (2 * j) % 4
                uslot = (2 * j + 1) % 4
                pb = j % 2
                ev_mmg = None
                for dt in range(DT):
                    ev_mmg = Tn.add(lambda t, pb=pb, gslot=gslot, dt=dt, sl=sl: t.matmul(
                        psG[pb][:], wgu_sb[gslot][:, 128 * dt:128 * (dt + 1)],
                        xlnTb[sl][:, 512 * dt:512 * (dt + 1)],
                        start=(dt == 0), stop=(dt == DT - 1)),
                        waits=([evw_g[j], ev_xln[c], psG_free[pb]] if dt == 0 else []),
                        inc=(dt == DT - 1))
                wgu_free[gslot] = ev_mmg
                ev_mmu = None
                for dt in range(DT):
                    ev_mmu = Tn.add(lambda t, pb=pb, uslot=uslot, dt=dt, sl=sl: t.matmul(
                        psU[pb][:], wgu_sb[uslot][:, 128 * dt:128 * (dt + 1)],
                        xlnTb[sl][:, 512 * dt:512 * (dt + 1)],
                        start=(dt == 0), stop=(dt == DT - 1)),
                        waits=([evw_u[j], psU_free[pb]] if dt == 0 else []),
                        inc=(dt == DT - 1))
                wgu_free[uslot] = ev_mmu
                ev_si = S.add(lambda s, pb=pb: s.activation(gate_sb[:], psG[pb][:], AF.Silu),
                              waits=[ev_mmg, gate_free])
                psG_free[pb] = ev_si
                ev_fu = V.add(lambda v, pb=pb, j=j: v.tensor_tensor(
                    fuse[:, 512 * j:512 * (j + 1)], psU[pb][:], gate_sb[:], ALU.mult),
                    waits=[ev_si, ev_mmu, fuse_free])
                psU_free[pb] = ev_fu
                gate_free = ev_fu
                ev_fuse_last = ev_fu
            xlnT_free[sl] = ev_mmu
            evw_d = [None] * 16
            evw_d[0] = load_wdn(0, 0)
            ev_yT_all = None
            for ht in range(16):
                if ht + 1 < 16:
                    evw_d[ht + 1] = load_wdn(ht + 1, (ht + 1) % 2)
                wslot = ht % 2
                pb = ht % 2
                for it in range(NI1):
                    ev_mm = Tn.add(lambda t, pb=pb, wslot=wslot, it=it: t.matmul(
                        psY[pb][:], wdn_sb[wslot][:, 128 * it:128 * (it + 1)],
                        fuse[:, 512 * it:512 * (it + 1)],
                        start=(it == 0), stop=(it == NI1 - 1)),
                        waits=([evw_d[ht], ev_fuse_last, psY_free[pb]] if it == 0 else []),
                        inc=(it == NI1 - 1))
                wdn_free[wslot] = ev_mm
                ev_yT_all = V.add(lambda v, pb=pb, ht=ht: v.tensor_copy(
                    yT[:, 512 * ht:512 * (ht + 1)], psY[pb][:]), waits=[ev_mm, yT_free])
                psY_free[pb] = ev_yT_all
            fuse_free = ev_mm
            last_tr = None
            ev_yn = None
            for ht in range(16):
                pb = ht % 2
                evt = None
                for ts in range(4):
                    evt = Tn.add(lambda t, pb=pb, ht=ht, ts=ts: t.transpose(
                        psN[pb][:, 128 * ts:128 * (ts + 1)],
                        yT[:, 512 * ht + 128 * ts: 512 * ht + 128 * (ts + 1)],
                        id_f[:]), waits=([ev_yT_all, psN_free[pb]] if ts == 0 else []),
                        inc=(ts == 3))
                last_tr = evt
                ev_yn = V.add(lambda v, pb=pb, ht=ht: v.tensor_copy(
                    ynat.ap().rearrange("p (s h) -> p s h", s=4)[:, :, 128 * ht:128 * (ht + 1)],
                    psN[pb].ap().rearrange("p (s h) -> p s h", s=4)),
                    waits=[evt, out_free[0], out_free[1]])
                psN_free[pb] = ev_yn
            yT_free = last_tr
            for ts in range(4):
                ti = 4 * c + ts
                bs = ti % 2
                ev_x3 = G.add(lambda q, ti=ti, bs=bs: q.dma_start(
                    out=x3b[bs][:], in_=x3s[128 * ti:128 * (ti + 1), :]), waits=[x3_free[bs]],
                    slot=f"x3b{bs}")
                ev_sq = V.add(lambda v, ts=ts: v.tensor_mul(
                    ysq[:], ynat[:, 2048 * ts:2048 * (ts + 1)], ynat[:, 2048 * ts:2048 * (ts + 1)]),
                    waits=[ev_yn])
                ev_rs = V.add(lambda v, ts=ts: v.reduce_sum(rms[:, ts:ts + 1], ysq[:], axis=AX.X),
                              waits=[ev_sq])
                ev_l = S.add(lambda s, ts=ts: s.activation(rms[:, ts:ts + 1], rms[:, ts:ts + 1],
                                                           AF.Ln, scale=1.0 / D, bias=env["epsc"][:, 2:3]),
                             waits=[ev_rs])
                ev_e = S.add(lambda s, ts=ts: s.activation(rms[:, ts:ts + 1], rms[:, ts:ts + 1],
                                                           AF.Exp, scale=-0.5), waits=[ev_l])
                ev_o = V.add(lambda v, ts=ts, bs=bs: v.scalar_tensor_tensor(
                    ynat[:, 2048 * ts:2048 * (ts + 1)], ynat[:, 2048 * ts:2048 * (ts + 1)],
                    rms[:, ts:ts + 1], x3b[bs][:], ALU.mult, ALU.add),
                    waits=[ev_e, ev_x3])
                x3_free[bs] = ev_o
                ev_so = G.add(lambda q, ti=ti, ts=ts, bs=bs: q.dma_start(
                    out=out[128 * ti:128 * (ti + 1), :], in_=ynat[:, 2048 * ts:2048 * (ts + 1)]),
                    waits=[ev_o], slot=f"outst{bs}")
                out_free[bs] = ev_so
            ynat_free = ev_so



def make_consts():
    idbf = np.eye(128, dtype=ml_dtypes.bfloat16)
    idf = np.eye(128, dtype=np.float32)
    mask = np.where(np.arange(128)[None, :] > np.arange(128)[:, None],
                    np.float32(NEG_INF), np.float32(0.0)).astype(np.float32)
    u128 = np.triu(np.ones((128, 128), dtype=np.float32))
    iota0 = np.arange(128, dtype=np.float32).reshape(1, 128)
    iota1 = np.arange(1, 129, dtype=np.float32).reshape(1, 128)
    ones = np.ones((1, 128), dtype=np.float32)
    e127 = np.zeros((128, 1), dtype=np.float32)
    e127[127, 0] = 1.0
    eps = np.tile(np.array([[1e-5, 1.0, 1e-6, 0.0]], dtype=np.float32), (128, 1))
    return {"c_idbf": idbf, "c_idf": idf, "c_mask": mask, "c_u128": u128,
            "c_iota0": iota0, "c_iota1": iota1, "c_ones": ones, "c_e127": e127, "c_eps": eps}


def make_in_map(inputs, b):
    m = {
        "x_in": np.ascontiguousarray(inputs["L0_attn_x"][b]),
        "wq": inputs["L0_attn_Wq"], "wk": inputs["L0_attn_Wk"], "wv": inputs["L0_attn_Wv"],
        "wg8": inputs["L0_ffn_w_gate"], "wu8": inputs["L0_ffn_w_up"], "wd8": inputs["L0_ffn_w_down"],
        "sgate": inputs["L0_ffn_s_gate"], "sup": inputs["L0_ffn_s_up"], "sdown": inputs["L0_ffn_s_down"],
        "lre": np.asarray(inputs["L1_attn_Lambda_unconstrained_re"]).reshape(1, 32),
        "lim": np.asarray(inputs["L1_attn_Lambda_im"]).reshape(1, 32),
        "bre": inputs["L1_attn_B_re"], "bim": inputs["L1_attn_B_im"],
        "cre": inputs["L1_attn_C_re"], "cim": inputs["L1_attn_C_im"],
        "dvec": np.asarray(inputs["L1_attn_D"]).reshape(1, D),
        "logdelta": np.asarray(inputs["L1_attn_log_Delta"]).reshape(1, 64),
        "gup_pk": inputs["L1_ffn_gate_up_packed"], "gup_sc": inputs["L1_ffn_gate_up_scales"],
        "dn_pk": inputs["L1_ffn_down_packed"], "dn_sc": inputs["L1_ffn_down_scales"],
    }
    m = {k: np.asarray(v) for k, v in m.items()}
    m.update(make_consts())
    return m


def kernel(**inputs):
    nc = build_nc("P4")
    in_maps = [make_in_map(inputs, b) for b in range(8)]
    res = run_bass_kernel_spmd(nc, in_maps, core_ids=list(range(8)))
    return np.stack([res.results[b]["out"] for b in range(8)], axis=0)
